# version 16
# baseline (speedup 1.0000x reference)
import numpy as np
import ml_dtypes
from contextlib import ExitStack

import concourse.bass as bass
import concourse.mybir as mybir
from concourse.bass_utils import run_bass_kernel_spmd

F32 = mybir.dt.float32
BF16 = mybir.dt.bfloat16
ALU = mybir.AluOpType
AF = mybir.ActivationFunctionType
AX = mybir.AxisListType

NCORES = 8
BPC = 2
T = 2048
D = 1024
NT = BPC * T
NSUB = NT // 128



class Buf:
    __slots__ = ("h", "name", "last_w", "readers")

    def __init__(self, h, name):
        self.h = h
        self.name = name
        self.last_w = None
        self.readers = []

    def __getitem__(self, sl):
        return self.h[sl]

    def ap(self):
        return self.h.ap() if hasattr(self.h, "ap") else self.h


class Op:
    __slots__ = ("engine", "fn", "deps", "idx", "wanted", "is_dma", "rank",
                 "tag", "slot", "seq")

    def __init__(self, engine, fn, idx, is_dma, tag):
        self.engine = engine
        self.fn = fn
        self.deps = []
        self.idx = idx
        self.wanted = False
        self.is_dma = is_dma
        self.rank = None
        self.tag = tag
        self.slot = None
        self.seq = None


ENGINES = ("SYNC", "TE", "DVE", "ACT", "POOL")


class Prog:
    def __init__(self, name):
        self.name = name
        self.nc = bass.Bass("TRN2", target_bir_lowering=False)
        self.es = ExitStack()
        self.stacks = [self.es]
        self.ops = {e: [] for e in ENGINES}
        self._uid = 0
        self.NSEM = 16
        self._dma_rr = {"SYNC": 0, "POOL": 0}
        self._dma_seq = {("SYNC", i): 0 for i in range(self.NSEM)}
        self._dma_seq.update({("POOL", i): 0 for i in range(self.NSEM)})
        self._dram_cache = {}
        self._last_dma = {}

    from contextlib import contextmanager as _ctxm

    @_ctxm
    def scope(self):
        st = ExitStack()
        self.stacks.append(st)
        try:
            yield
        finally:
            self.stacks.pop()
            st.close()

    def barrier(self, tag="bar"):
        lasts = []
        for e in ENGINES:
            if self.ops[e]:
                o = self.ops[e][-1]
                if not o.is_dma:
                    lasts.append(o)
        lasts.extend(self._last_dma.values())
        for e in ENGINES:
            if not self.ops[e]:
                continue
            def fn(eng):
                return eng.nop()
            o = Op(e, fn, len(self.ops[e]), False, tag + e)
            o.deps = [d for d in lasts if d.engine != e or d.is_dma]
            self.ops[e].append(o)

    def _nm(self, base):
        self._uid += 1
        return f"{base}_{self._uid}"

    def sbuf(self, name, shape, dtype):
        h = self.stacks[-1].enter_context(
            self.nc.sbuf_tensor(self._nm(name), shape, dtype))
        return Buf(h, name)

    def psum(self, name, shape, dtype):
        h = self.stacks[-1].enter_context(
            self.nc.psum_tensor(self._nm(name), shape, dtype))
        return Buf(h, name)

    def dram_in(self, name, shape, dtype):
        if name in self._dram_cache:
            return self._dram_cache[name]
        h = self.nc.dram_tensor(name, list(shape), dtype, kind="ExternalInput")
        b = Buf(h, name)
        self._dram_cache[name] = b
        return b

    def dram_out(self, name, shape, dtype):
        h = self.nc.dram_tensor(name, list(shape), dtype, kind="ExternalOutput")
        return Buf(h, name)

    def dram_scratch(self, name, shape, dtype):
        h = self.nc.dram_tensor(self._nm(name), list(shape), dtype, kind="Internal")
        return Buf(h, name)

    def split_views(self, buf, n):
        return [Buf(buf.h, f"{buf.name}_v{i}") for i in range(n)]

    def op(self, engine, fn, r=(), w=(), tag="", dma=False):
        o = Op(engine, fn, len(self.ops[engine]), dma, tag)
        if dma:
            o.slot = self._dma_rr[engine] % self.NSEM
            self._dma_rr[engine] += 1
            self._dma_seq[(engine, o.slot)] += 1
            o.seq = self._dma_seq[(engine, o.slot)]
            self._last_dma[(engine, o.slot)] = o
        deps = []
        for b in r:
            if b.last_w is not None:
                deps.append(b.last_w)
        for b in w:
            if b.last_w is not None:
                deps.append(b.last_w)
            deps.extend(b.readers)
        o.deps = deps
        self.ops[engine].append(o)
        for b in r:
            b.readers.append(o)
        for b in w:
            b.last_w = o
            b.readers = []
        return o

    def dma(self, out_ap, in_ap, r=(), w=(), engine="SYNC", tag="dma"):
        def fn(eng):
            return eng.dma_start(out=out_ap, in_=in_ap)
        return self.op(engine, fn, r=r, w=w, tag=tag, dma=True)

    def build(self):
        for e in ENGINES:
            for o in self.ops[e]:
                kept = []
                for d in o.deps:
                    if d.engine == "TE" and o.engine == "TE" and not d.is_dma:
                        continue
                    if (d.engine == o.engine and d.is_dma and o.is_dma
                            and d.slot == o.slot):
                        continue
                    kept.append(d)
                    d.wanted = True
                o.deps = kept
        for e in ENGINES:
            rk = 0
            for o in self.ops[e]:
                if o.wanted and not o.is_dma:
                    rk += 1
                    o.rank = rk
        sems = {}
        dsems = {}
        with ExitStack() as ses:
            for e in ENGINES:
                if any((not o.is_dma) for o in self.ops[e]):
                    sems[e] = ses.enter_context(self.nc.semaphore(f"sem_{e}"))
            for q in ("SYNC", "POOL"):
                if any(o.is_dma for o in self.ops[q]):
                    for i in range(self.NSEM):
                        dsems[(q, i)] = ses.enter_context(
                            self.nc.semaphore(f"dsem_{q}_{i}"))
            block = ses.enter_context(self.nc.Block())

            def section(eng_obj, e):
                waited = {}
                for o in self.ops[e]:
                    need = {}
                    for d in o.deps:
                        if d.is_dma:
                            key = (d.engine, d.slot)
                            sem, v = dsems[key], 16 * d.seq
                        else:
                            key = d.engine
                            sem, v = sems[key], d.rank
                        if need.get(key, (None, 0))[1] < v:
                            need[key] = (sem, v)
                    if o.is_dma and o.seq > 1:
                        key = (o.engine, o.slot)
                        v = 16 * (o.seq - 1)
                        if need.get(key, (None, 0))[1] < v:
                            need[key] = (dsems[key], v)
                    for key, (sem, v) in need.items():
                        if waited.get(key, 0) < v:
                            eng_obj.wait_ge(sem, v)
                            waited[key] = v
                    instr = o.fn(eng_obj)
                    if o.is_dma:
                        instr.then_inc(dsems[(o.engine, o.slot)], 16)
                    elif o.wanted:
                        if instr is None:
                            raise RuntimeError(f"op {o.tag} on {e} returned no instr")
                        instr.then_inc(sems[o.engine], 1)

            if self.ops["SYNC"]:
                block.sync(lambda eng: section(eng, "SYNC"))
            if self.ops["TE"]:
                block.tensor(lambda eng: section(eng, "TE"))
            if self.ops["DVE"]:
                block.vector(lambda eng: section(eng, "DVE"))
            if self.ops["ACT"]:
                block.scalar(lambda eng: section(eng, "ACT"))
            if self.ops["POOL"]:
                block.gpsimd(lambda eng: section(eng, "POOL"))
        self.es.close()
        return self.nc


    def barrier_outputs(self, out_dma_ops):
        def fn(eng):
            return eng.nop()
        o = self.op("SYNC", fn, tag="final_barrier")
        o.deps = list(out_dma_ops)
        for d in out_dma_ops:
            d.wanted = True

    def mm(self, psum_ap, pairs, r, w, tag="mm", extra_mms=None):
        def fn(eng):
            n = len(pairs)
            instr = None
            for i, (lh, rh) in enumerate(pairs):
                instr = eng.matmul(psum_ap, lhsT=lh, rhs=rh,
                                   start=(i == 0), stop=(i == n - 1))
            return instr
        return self.op("TE", fn, r=r, w=w, tag=tag)

    def transpose_to(self, psum_buf, psum_off, src_ap, ident_ap, r, w, n=1,
                     tag="tr"):
        raise NotImplementedError


def ts_op(eng, out, in0, s1, s2, op0, op1=ALU.bypass):
    return eng.tensor_scalar(out=out, in0=in0, scalar1=s1, scalar2=s2,
                             op0=op0, op1=op1)



def host_constants():
    c = {}
    ident = np.eye(128, dtype=np.float32)
    c["c_ident_bf"] = ident.astype(ml_dtypes.bfloat16)
    c["c_ident_f32"] = ident
    c["c_ut_f32"] = np.triu(np.ones((128, 128), dtype=np.float32))
    c["c_ones_row"] = np.ones((1, 128), dtype=np.float32)
    c["c_ones_col"] = np.ones((128, 1), dtype=np.float32)
    Dh = 64
    inv = 1.0 / 10000.0 ** (np.arange(0, Dh, 2, dtype=np.float32) / Dh)
    t = np.arange(T, dtype=np.float32)
    fr = t[:, None] * inv[None, :]
    emb = np.concatenate([fr, fr], axis=-1)
    c["c_rope_cos"] = np.cos(emb).astype(np.float32)
    c["c_rope_sin"] = np.sin(emb).astype(np.float32)
    return c



def w_chunked_bf16(p, name, rows, cols, stage_bufs, dst, st_idx):
    src = p.dram_in(name, (rows, cols), F32)
    nch = rows // 128
    rview = src.ap().rearrange("(n p) c -> n p c", p=128)
    for ci in range(nch):
        stg = stage_bufs[st_idx[0] % len(stage_bufs)]
        st_idx[0] += 1
        p.dma(stg[:, 0:cols], rview[ci], r=[src], w=[stg], tag=f"ld_{name}{ci}")
        cpy = (lambda stg, ci: lambda eng: eng.tensor_copy(
            dst[:, ci * cols:(ci + 1) * cols], stg[:, 0:cols]))(stg, ci)
        p.op("DVE", cpy, r=[stg], w=[dst], tag=f"cast_{name}{ci}")
    return dst


def bcast_rows(p, pm_buf, dst, src_ap, ones_row, ncols, r, tag="bcast"):
    nspl = (ncols + 511) // 512
    for si in range(nspl):
        c0, c1 = si * 512, min(ncols, (si + 1) * 512)
        w = c1 - c0
        p.mm(pm_buf[:, 0:w], [(ones_row[0:1, :], src_ap[0:1, c0:c1])],
             r=list(r), w=[pm_buf], tag=tag)
        def ev(eng, c0=c0, c1=c1, w=w):
            return eng.tensor_copy(dst[:, c0:c1], pm_buf[:, 0:w])
        p.op("DVE", ev, r=[pm_buf], w=[dst], tag=tag + "_ev")


def emit_launch1(p, x_in, x_out):
    nc = p.nc
    xv = x_in.ap().rearrange("b (n p) d -> (b n) p d", p=128)
    ov = x_out.ap().rearrange("b (n p) d -> (b n) p d", p=128)

    ident = p.dram_in("c_ident_bf", (128, 128), BF16)
    ident32_h = p.dram_in("c_ident_f32", (128, 128), F32)
    ut_h = p.dram_in("c_ut_f32", (128, 128), F32)
    ones_row_h = p.dram_in("c_ones_row", (1, 128), F32)
    ones_col_h = p.dram_in("c_ones_col", (128, 1), F32)
    cos_h = p.dram_in("c_rope_cos", (T, 64), F32)
    sin_h = p.dram_in("c_rope_sin", (T, 64), F32)

    c_ident = p.sbuf("ident", [128, 128], BF16)
    c_ident32 = p.sbuf("ident32", [128, 128], F32)
    c_ut = p.sbuf("ut", [128, 128], F32)
    c_ut_bf = p.sbuf("utbf", [128, 128], BF16)
    c_ones_row = p.sbuf("ones_row", [1, 128], F32)
    c_ones_col = p.sbuf("ones_col", [128, 1], F32)
    NROPE = T // 128
    cos_t = p.sbuf("cos", [128, NROPE * 64], F32)
    sin_t = p.sbuf("sin", [128, NROPE * 64], F32)
    p.dma(c_ident[:], ident.ap(), r=[ident], w=[c_ident])
    p.dma(c_ident32[:], ident32_h.ap(), r=[ident32_h], w=[c_ident32])
    p.dma(c_ut[:], ut_h.ap(), r=[ut_h], w=[c_ut])
    p.op("DVE", lambda eng: eng.tensor_copy(c_ut_bf[:], c_ut[:]), r=[c_ut], w=[c_ut_bf])
    p.dma(c_ones_row[:], ones_row_h.ap(), r=[ones_row_h], w=[c_ones_row])
    p.dma(c_ones_col[:], ones_col_h.ap(), r=[ones_col_h], w=[c_ones_col])
    p.dma(cos_t.ap().rearrange("p (n d) -> p n d", n=NROPE),
          cos_h.ap().rearrange("(n p) d -> p n d", p=128), r=[cos_h], w=[cos_t])
    p.dma(sin_t.ap().rearrange("p (n d) -> p n d", n=NROPE),
          sin_h.ap().rearrange("(n p) d -> p n d", p=128), r=[sin_h], w=[sin_t])

    stg = [p.sbuf(f"wstg{i}", [128, 1024], F32) for i in range(2)]
    sti = [0]
    wq = p.sbuf("wq", [128, 8 * 1024], BF16)
    wk = p.sbuf("wk", [128, 8 * 1024], BF16)
    wv = p.sbuf("wv", [128, 8 * 1024], BF16)
    wo = p.sbuf("wo", [128, 8 * 1024], BF16)
    wb = p.sbuf("wb", [128, 8 * 16], F32)
    w_chunked_bf16(p, "WqT", 1024, 1024, stg, wq, sti)
    w_chunked_bf16(p, "WkT", 1024, 1024, stg, wk, sti)
    w_chunked_bf16(p, "WvT", 1024, 1024, stg, wv, sti)
    w_chunked_bf16(p, "WoT", 1024, 1024, stg, wo, sti)
    w_chunked_bf16(p, "WbT", 1024, 16, stg, wb, sti)

    bb_h = p.dram_in("attn_bb", (1, 16), F32)
    bo_h = p.dram_in("attn_bo", (1, 1024), F32)
    sc_h = p.dram_in("attn_scale", (1, 1024), F32)
    bb_r = p.sbuf("bb_r", [1, 16], F32)
    bo_r = p.sbuf("bo_r", [1, 1024], F32)
    sc_r = p.sbuf("sc_r", [1, 1024], F32)
    p.dma(bb_r[:], bb_h.ap(), r=[bb_h], w=[bb_r])
    p.dma(bo_r[:], bo_h.ap(), r=[bo_h], w=[bo_r])
    p.dma(sc_r[:], sc_h.ap(), r=[sc_h], w=[sc_r])

    pB1 = p.psum("pB1", [128, 512], F32)
    bb_b = p.sbuf("bb_b", [128, 16], F32)
    bo_b = p.sbuf("bo_b", [128, 1024], F32)
    sc_b = p.sbuf("sc_b", [128, 1024], F32)
    bcast_rows(p, pB1, bb_b, bb_r.ap(), c_ones_row, 16, r=[bb_r, c_ones_row])
    bcast_rows(p, pB1, bo_b, bo_r.ap(), c_ones_row, 1024, r=[bo_r, c_ones_row])
    bcast_rows(p, pB1, sc_b, sc_r.ap(), c_ones_row, 1024, r=[sc_r, c_ones_row])

    xb = [p.sbuf(f"xb{i}", [128, 1024], F32) for i in range(2)]
    xa_f = p.sbuf("xa_f", [128, 1024], F32)
    xa_fm = p.sbuf("xa_fm", [128, 1024], BF16)
    xa_fm32 = p.sbuf("xa_fm32", [128, 1024], F32)
    scr_bf = p.sbuf("scr_bf", [128, 1024], BF16)
    colA = p.sbuf("colA", [128, 8], F32)
    colB = p.sbuf("colB", [128, 8], F32)
    rot = p.sbuf("rot", [128, 1024], F32)
    phq = p.sbuf("phq", [128, 1024], F32)
    phk = p.sbuf("phk", [128, 1024], F32)
    qt_bf = p.sbuf("qt_bf", [128, 1024], BF16)
    kt_bf = p.sbuf("kt_bf", [128, 1024], BF16)
    qt_fm = p.sbuf("qt_fm", [128, 1024], BF16)
    kt_fm = p.sbuf("kt_fm", [128, 1024], BF16)
    v_aug = p.sbuf("v_aug", [128, 16 * 65], BF16)
    att = p.sbuf("att", [128, 1024], BF16)
    att_fm = p.sbuf("att_fm", [128, 1024], BF16)
    beta = p.sbuf("beta", [128, 16], F32)
    lnb = p.sbuf("lnb", [128, 16], F32)
    pi_t = p.sbuf("pi_t", [128, 16], F32)
    rpi_t = p.sbuf("rpi_t", [128, 16], F32)
    piC = p.sbuf("piC", [1, 16], F32)
    piC_b = p.sbuf("piC_b", [64, 16], F32)
    amask = [p.sbuf(f"amask{i}", [128, 128], BF16) for i in range(2)]
    s2aug = [p.sbuf(f"s2aug{i}", [128, 130], BF16) for i in range(8)]
    s_aug = [p.sbuf(f"saug{i}", [128, 65], F32) for i in range(8)]
    s_scr = p.sbuf("s_scr", [64, 65], F32)
    hbuf = p.sbuf("hbuf", [128, 1024], F32)
    obuf = p.sbuf("obuf", [128, 1024], F32)

    pP = p.psum("pP", [128, 1024], F32)
    pT = p.psum("pT", [128, 512], BF16)
    pA = p.psum("pA", [128, 128], F32)
    pO = p.psum("pO", [128, 130], F32)
    pS = p.psum("pS", [64, 65], F32)
    pM = p.psum("pM", [128, 16], F32)
    pM2 = pB1
    pM3 = pB1

    p.op("DVE", lambda eng: eng.memset(v_aug[:], 1.0), w=[v_aug])
    for i in range(8):
        p.op("DVE", (lambda i: lambda eng: eng.memset(s2aug[i][:], 0.0))(i),
             w=[s2aug[i]])
        p.op("DVE", (lambda i: lambda eng: eng.memset(s_aug[i][:], 0.0))(i),
             w=[s_aug[i]])

    out_dmas = []

    def ln_stats(src_buf, src_ap, rstd_b, negm_b, eps, tag):
        rstd_col = rstd_b[:]
        negm_col = negm_b[:]
        """LayerNorm stats of [128, 1024] fp32 -> rstd & -mean*rstd columns."""
        p.op("DVE", lambda eng: eng.tensor_reduce(
            out=colA[:, 0:1], in_=src_ap, axis=AX.X, op=ALU.add, negate=True),
            r=[src_buf], w=[colA], tag=tag + "_sum")
        p.op("ACT", lambda eng: eng.activation(
            out=scr_bf[:], in_=src_ap, func=AF.Square, accum_out=colA[:, 1:2]),
            r=[src_buf], w=[scr_bf, colA], tag=tag + "_sq")
        p.op("DVE", lambda eng: ts_op(eng, negm_col, colA[:, 0:1], 1.0 / D, None,
                                      ALU.mult), r=[colA], w=[negm_b], tag=tag)
        def var1(eng):
            eng.tensor_scalar(out=colA[:, 2:3], in0=negm_col, scalar1=negm_col,
                              scalar2=None, op0=ALU.mult)
            return eng.tensor_scalar(out=colA[:, 3:4], in0=colA[:, 1:2],
                                     scalar1=1.0 / D, scalar2=eps,
                                     op0=ALU.mult, op1=ALU.add)
        p.op("DVE", var1, r=[colA, negm_b], w=[colA], tag=tag + "_var1")
        p.op("DVE", lambda eng: eng.tensor_tensor(
            out=colA[:, 4:5], in0=colA[:, 3:4], in1=colA[:, 2:3],
            op=ALU.subtract), r=[colA], w=[colA], tag=tag + "_var2")
        p.op("ACT", lambda eng: eng.activation(out=colA[:, 5:6], in_=colA[:, 4:5],
                                               func=AF.Sqrt),
             r=[colA], w=[colA], tag=tag + "_sqrt")
        p.op("DVE", lambda eng: eng.reciprocal(out=rstd_col, in_=colA[:, 5:6]),
             r=[colA], w=[rstd_b], tag=tag + "_rcp")
        def nm(eng):
            return eng.tensor_tensor(out=negm_col, in0=negm_col, in1=rstd_col,
                                     op=ALU.mult)
        p.op("DVE", nm, r=[rstd_b, colA], w=[negm_b], tag=tag + "_nm")

    import os
    dbg_taps = {}
    if os.environ.get("K_DEBUG"):
        for nm, shape, dt in [("d_xa", (128, 1024), F32), ("d_pi", (128, 16), F32),
                              ("d_phq", (128, 1024), F32), ("d_qt", (128, 1024), BF16),
                              ("d_kt", (128, 1024), BF16), ("d_att", (128, 1024), BF16),
                              ("d_h", (128, 1024), F32), ("d_vaug", (128, 1040), BF16),
                              ("d_am", (128, 128), BF16), ("d_beta", (128, 16), F32)]:
            dbg_taps[nm] = p.dram_out(nm, shape, dt)

    def tap(nm, buf, s, want_s=0):
        if dbg_taps and s == want_s and nm in dbg_taps:
            od = p.dma(dbg_taps[nm].ap(), buf[:], r=[buf], w=[dbg_taps[nm]],
                       engine="POOL", tag="tap" + nm)
            out_dmas.append(od)

    rstd1 = p.sbuf("rstd1", [128, 1], F32)
    negm1 = p.sbuf("negm1", [128, 1], F32)
    rstd2 = p.sbuf("rstd2", [128, 1], F32)
    negm2 = p.sbuf("negm2", [128, 1], F32)
    rr_col = p.sbuf("rr_col", [128, 1], F32)
    rden = p.sbuf("rden", [128, 2], F32)

    for s in range(NSUB):
        x_t = xb[s % 2]
        seqpos = s % (T // 128)
        if s > 0 and seqpos == 0:
            for i in range(8):
                p.op("DVE", (lambda i: lambda eng: eng.memset(
                    s2aug[i][:], 0.0))(i), w=[s2aug[i]], tag=f"rst2_{s}_{i}")
                p.op("DVE", (lambda i: lambda eng: eng.memset(
                    s_aug[i][:], 0.0))(i), w=[s_aug[i]], tag=f"rst_{s}_{i}")
        p.dma(x_t[:], xv[s], r=[x_in], w=[x_t], tag=f"ldx{s}")

        ln_stats(x_t, x_t[:], rstd1, negm1, 1e-5, f"ln1_{s}")
        p.op("ACT", (lambda x_t: lambda eng: eng.activation(
            out=xa_f[:], in_=x_t[:], func=AF.Identity, scale=rstd1[:],
            bias=negm1[:]))(x_t),
            r=[x_t, rstd1, negm1], w=[xa_f], tag=f"xa_f{s}")

        tap("d_xa", xa_f, s)
        def trf32(eng):
            instr = None
            for ci in range(8):
                instr = eng.transpose(pP[:, ci * 128:(ci + 1) * 128],
                                      xa_f[:, ci * 128:(ci + 1) * 128],
                                      c_ident32[:])
            return instr
        p.op("TE", trf32, r=[xa_f, c_ident32], w=[pP], tag=f"tr{s}")
        for g in range(2):
            p.op("ACT", (lambda g: lambda eng: eng.activation(
                out=xa_fm32[:, g * 512:(g + 1) * 512],
                in_=pP[:, g * 512:(g + 1) * 512], func=AF.Copy))(g),
                r=[pP], w=[xa_fm32], tag=f"trev{s}_{g}")
        p.op("DVE", lambda eng: eng.tensor_copy(xa_fm[:], xa_fm32[:]),
             r=[xa_fm32], w=[xa_fm], tag=f"xafm_bf{s}")

        p.mm(pM[:], [(xa_fm32[:, ci * 128:(ci + 1) * 128],
                      wb[:, ci * 16:(ci + 1) * 16]) for ci in range(8)],
             r=[xa_fm32, wb], w=[pM], tag=f"betamm{s}")
        def betafn(eng):
            return eng.tensor_tensor(out=beta[:], in0=pM[:], in1=bb_b[:, 0:16],
                                     op=ALU.add)
        p.op("DVE", betafn, r=[pM, bb_b], w=[beta], tag=f"beta{s}")
        p.op("ACT", lambda eng: eng.activation(out=beta[:], in_=beta[:],
                                               func=AF.Sigmoid),
             r=[beta], w=[beta], tag=f"sig{s}")
        p.op("DVE", lambda eng: ts_op(eng, beta[:], beta[:], 0.8, 0.999,
                                      ALU.max, ALU.min),
             r=[beta], w=[beta], tag=f"clip{s}")
        p.op("ACT", lambda eng: eng.activation(out=lnb[:], in_=beta[:], func=AF.Ln),
             r=[beta], w=[lnb], tag=f"lnb{s}")
        p.mm(pM[:], [(c_ut[:], lnb[:])], r=[c_ut, lnb], w=[pM], tag=f"cum{s}")
        p.mm(pM2[0:1, 0:16], [(c_ones_col[:], lnb[:])], r=[c_ones_col, lnb],
             w=[pM2], tag=f"tot{s}")
        p.op("ACT", lambda eng: eng.activation(out=pi_t[:], in_=pM[:], func=AF.Exp),
             r=[pM], w=[pi_t], tag=f"pi{s}")
        p.op("ACT", lambda eng: eng.activation(out=piC[:], in_=pM2[0:1, 0:16],
                                               func=AF.Exp),
             r=[pM2], w=[piC], tag=f"piC{s}")
        p.op("DVE", lambda eng: eng.reciprocal(out=rpi_t[:], in_=pi_t[:]),
             r=[pi_t], w=[rpi_t], tag=f"rpi{s}")
        tap("d_pi", pi_t, s)
        tap("d_beta", beta, s)
        p.mm(pM3[0:64, 16:32], [(c_ones_row[0:1, 0:64], piC[0:1, :])],
             r=[c_ones_row, piC], w=[pM3], tag=f"piCb{s}")
        p.op("DVE", lambda eng: eng.tensor_copy(piC_b[:], pM3[0:64, 16:32]),
             r=[pM3], w=[piC_b], tag=f"piCbe{s}")

        cos_ap = cos_t[:, seqpos * 64:(seqpos + 1) * 64].unsqueeze(1)\
            .to_broadcast([128, 16, 64])
        sin_ap = sin_t[:, seqpos * 64:(seqpos + 1) * 64].unsqueeze(1)\
            .to_broadcast([128, 16, 64])

        def qk_path(wmat, phi_dst, pi_ap, dst_bf, dst_fm, tg,
                    cos_ap=cos_ap, sin_ap=sin_ap):
            for nh in range(2):
                p.mm(pP[:, nh * 512:(nh + 1) * 512],
                     [(xa_fm[:, ci * 128:(ci + 1) * 128],
                       wmat[:, ci * 1024 + nh * 512: ci * 1024 + nh * 512 + 512])
                      for ci in range(8)],
                     r=[xa_fm, wmat], w=[pP], tag=tg + "_mm")
            def rotfn(eng):
                eng.tensor_scalar(
                    out=rot.ap().rearrange("p (h d) -> p h d", h=16)[:, :, 0:32],
                    in0=pP.ap().rearrange("p (h d) -> p h d", h=16)[:, :, 32:64],
                    scalar1=-1.0, scalar2=None, op0=ALU.mult)
                return eng.tensor_copy(
                    rot.ap().rearrange("p (h d) -> p h d", h=16)[:, :, 32:64],
                    pP.ap().rearrange("p (h d) -> p h d", h=16)[:, :, 0:32])
            p.op("DVE", rotfn, r=[pP], w=[rot], tag=tg + "_rot")
            def ropefn(eng):
                eng.tensor_tensor(
                    out=rot.ap().rearrange("p (h d) -> p h d", h=16),
                    in0=rot.ap().rearrange("p (h d) -> p h d", h=16),
                    in1=sin_ap, op=ALU.mult)
                return eng.tensor_tensor(
                    out=phi_dst.ap().rearrange("p (h d) -> p h d", h=16),
                    in0=pP.ap().rearrange("p (h d) -> p h d", h=16),
                    in1=cos_ap, op=ALU.mult)
            p.op("DVE", ropefn, r=[pP, rot, cos_t, sin_t], w=[phi_dst, rot],
                 tag=tg + "_rope")
            p.op("DVE", lambda eng: eng.tensor_tensor(
                out=phi_dst[:], in0=phi_dst[:], in1=rot[:], op=ALU.add),
                r=[phi_dst, rot], w=[phi_dst], tag=tg + "_rope2")
            def phifn1(eng):
                return ts_op(eng, rot[:], phi_dst[:], 0.0, None, ALU.min)
            p.op("DVE", phifn1, r=[phi_dst], w=[rot], tag=tg + "_min")
            p.op("ACT", lambda eng: eng.activation(out=rot[:], in_=rot[:],
                                                   func=AF.Exp),
                 r=[rot], w=[rot], tag=tg + "_exp")
            p.op("DVE", lambda eng: eng.tensor_scalar(
                out=phi_dst[:], in0=phi_dst[:], scalar1=0.0, scalar2=None,
                op0=ALU.max), r=[phi_dst], w=[phi_dst], tag=tg + "_phimax")
            p.op("DVE", lambda eng: eng.tensor_tensor(
                out=phi_dst[:], in0=phi_dst[:], in1=rot[:], op=ALU.add),
                r=[phi_dst, rot], w=[phi_dst], tag=tg + "_phi")
            def tildefn(eng):
                return eng.tensor_tensor(
                    out=dst_bf.ap().rearrange("p (h d) -> p h d", h=16),
                    in0=phi_dst.ap().rearrange("p (h d) -> p h d", h=16),
                    in1=pi_ap, op=ALU.mult)
            p.op("DVE", tildefn, r=[phi_dst, pi_t, rpi_t], w=[dst_bf],
                 tag=tg + "_tilde")
            for g in range(2):
                def trf(eng, g=g):
                    instr = None
                    for i in range(4):
                        ci = g * 4 + i
                        instr = eng.transpose(pT[:, i * 128:(i + 1) * 128],
                                              dst_bf[:, ci * 128:(ci + 1) * 128],
                                              c_ident[:])
                    return instr
                p.op("TE", trf, r=[dst_bf, c_ident], w=[pT], tag=tg + f"_tr{g}")
                p.op("ACT", (lambda g: lambda eng: eng.activation(
                    out=dst_fm[:, g * 512:(g + 1) * 512], in_=pT[:],
                    func=AF.Copy))(g), r=[pT], w=[dst_fm], tag=tg + f"_trev{g}")

        pi_ap = pi_t.ap().unsqueeze(2).to_broadcast([128, 16, 64])
        rpi_ap = rpi_t.ap().unsqueeze(2).to_broadcast([128, 16, 64])
        qk_path(wq, phq, pi_ap, qt_bf, qt_fm, f"q{s}")
        qk_path(wk, phk, rpi_ap, kt_bf, kt_fm, f"k{s}")
        tap("d_phq", phq, s)
        tap("d_qt", qt_bf, s)
        tap("d_kt", kt_bf, s)

        for nh in range(2):
            p.mm(pP[:, nh * 512:(nh + 1) * 512],
                 [(xa_fm[:, ci * 128:(ci + 1) * 128],
                   wv[:, ci * 1024 + nh * 512: ci * 1024 + nh * 512 + 512])
                  for ci in range(8)],
                 r=[xa_fm, wv], w=[pP], tag=f"v{s}_mm")
        def vev(eng):
            return eng.activation(
                out=v_aug.ap().rearrange("p (h e) -> p h e", h=16)[:, :, 0:64],
                in_=pP.ap().rearrange("p (h e) -> p h e", h=16),
                func=AF.Copy)
        p.op("ACT", vev, r=[pP], w=[v_aug], tag=f"vev{s}")
        tap("d_vaug", v_aug, s)

        for pr in range(8):
            h0, h1 = 2 * pr, 2 * pr + 1
            for hi, h in enumerate((h0, h1)):
                base = (h % 2) * 64
                ci = h // 2
                p.mm(pA[:], [(kt_fm[base:base + 64, ci * 128:(ci + 1) * 128],
                              qt_fm[base:base + 64, ci * 128:(ci + 1) * 128])],
                     r=[kt_fm, qt_fm], w=[pA], tag=f"M1_{s}_{h}")
                p.op("DVE", (lambda hi: lambda eng: eng.tensor_tensor(
                    out=amask[hi][:], in0=pA[:], in1=c_ut[:], op=ALU.mult))(hi),
                    r=[pA, c_ut], w=[amask[hi]], tag=f"mask{s}_{h}")
            if pr == 0:
                tap("d_am", amask[0], s)
            def m32(eng, pr=pr, h0=h0, h1=h1):
                ci = pr
                eng.matmul(pO[:, 0:65], lhsT=qt_fm[:, ci * 128:(ci + 1) * 128],
                           rhs=s2aug[pr][:, 0:65], start=True, stop=False)
                eng.matmul(pO[:, 0:65], lhsT=amask[0][:],
                           rhs=v_aug[:, h0 * 65:(h0 + 1) * 65], start=False,
                           stop=True)
                eng.matmul(pO[:, 65:130], lhsT=qt_fm[:, ci * 128:(ci + 1) * 128],
                           rhs=s2aug[pr][:, 65:130], start=True, stop=False)
                return eng.matmul(pO[:, 65:130], lhsT=amask[1][:],
                                  rhs=v_aug[:, h1 * 65:(h1 + 1) * 65],
                                  start=False, stop=True)
            p.op("TE", m32, r=[qt_fm, s2aug[pr], amask[0], amask[1], v_aug],
                 w=[pO], tag=f"M32_{s}_{pr}")
            def denf(eng):
                eng.tensor_scalar(out=rden[:, 0:1], in0=pO[:, 64:65], scalar1=1e-6,
                                  scalar2=None, op0=ALU.add)
                return eng.tensor_scalar(out=rden[:, 1:2], in0=pO[:, 129:130],
                                         scalar1=1e-6, scalar2=None, op0=ALU.add)
            p.op("DVE", denf, r=[pO], w=[rden], tag=f"den{s}_{pr}")
            p.op("DVE", lambda eng: eng.reciprocal(out=rden[:], in_=rden[:]),
                 r=[rden], w=[rden], tag=f"rden{s}_{pr}")
            def oev(eng, pr=pr, h0=h0, h1=h1):
                eng.tensor_scalar(out=att[:, h0 * 64:(h0 + 1) * 64],
                                  in0=pO[:, 0:64], scalar1=rden[:, 0:1],
                                  scalar2=None, op0=ALU.mult)
                return eng.tensor_scalar(out=att[:, h1 * 64:(h1 + 1) * 64],
                                         in0=pO[:, 65:129], scalar1=rden[:, 1:2],
                                         scalar2=None, op0=ALU.mult)
            p.op("DVE", oev, r=[pO, rden], w=[att], tag=f"oev{s}_{pr}")
            for hi, h in enumerate((h0, h1)):
                p.mm(pS[:], [(kt_bf[:, h * 64:(h + 1) * 64],
                              v_aug[:, h * 65:(h + 1) * 65])],
                     r=[kt_bf, v_aug], w=[pS], tag=f"M4_{s}_{h}")
                base = (h % 2) * 64
                p.op("DVE", (lambda pr, base: lambda eng: eng.tensor_tensor(
                    out=s_scr[:], in0=s_aug[pr][base:base + 64, :], in1=pS[:],
                    op=ALU.add))(pr, base),
                    r=[pS, s_aug[pr]], w=[s_scr], tag=f"supd{s}_{h}a")
                p.op("DVE", (lambda pr, h, base: lambda eng: eng.tensor_scalar(
                    out=s_aug[pr][base:base + 64, :], in0=s_scr[:],
                    scalar1=piC_b[:, h:h + 1], scalar2=None, op0=ALU.mult))(pr, h, base),
                    r=[s_scr, piC_b], w=[s_aug[pr]], tag=f"supd{s}_{h}b")
                p.op("DVE", (lambda pr, hi, base: lambda eng: eng.tensor_copy(
                    s2aug[pr][base:base + 64, base + hi:base + 65 + hi],
                    s_aug[pr][base:base + 64, :]))(pr, hi, base),
                    r=[s_aug[pr]], w=[s2aug[pr]], tag=f"supd{s}_{h}c")

        tap("d_att", att, s)
        for g in range(2):
            def trf(eng, g=g):
                instr = None
                for i in range(4):
                    ci = g * 4 + i
                    instr = eng.transpose(pT[:, i * 128:(i + 1) * 128],
                                          att[:, ci * 128:(ci + 1) * 128],
                                          c_ident[:])
                return instr
            p.op("TE", trf, r=[att, c_ident], w=[pT], tag=f"atr{s}_{g}")
            p.op("ACT", (lambda g: lambda eng: eng.activation(
                out=att_fm[:, g * 512:(g + 1) * 512], in_=pT[:], func=AF.Copy))(g),
                r=[pT], w=[att_fm], tag=f"atrev{s}_{g}")
        for nh in range(2):
            p.mm(pP[:, nh * 512:(nh + 1) * 512],
                 [(att_fm[:, ci * 128:(ci + 1) * 128],
                   wo[:, ci * 1024 + nh * 512: ci * 1024 + nh * 512 + 512])
                  for ci in range(8)],
                 r=[att_fm, wo], w=[pP], tag=f"wo{s}_mm")
        p.op("DVE", lambda eng: eng.tensor_tensor(
            out=hbuf[:], in0=pP[:], in1=xa_f[:], op=ALU.add),
            r=[pP, xa_f], w=[hbuf], tag=f"h{s}a")
        p.op("DVE", lambda eng: eng.tensor_tensor(
            out=hbuf[:], in0=hbuf[:], in1=bo_b[:], op=ALU.add),
            r=[hbuf, bo_b], w=[hbuf], tag=f"h{s}b")
        tap("d_h", hbuf, s)
        p.op("ACT", lambda eng: eng.activation(out=scr_bf[:], in_=hbuf[:],
                                               func=AF.Square,
                                               accum_out=colB[:, 0:1]),
             r=[hbuf], w=[scr_bf, colB], tag=f"rms_sq{s}")
        def rmsfn(eng):
            return ts_op(eng, colB[:, 1:2], colB[:, 0:1], 1.0 / D, 1e-6,
                         ALU.mult, ALU.add)
        p.op("DVE", rmsfn, r=[colB], w=[colB], tag=f"rms{s}")
        p.op("ACT", lambda eng: eng.activation(out=colB[:, 2:3], in_=colB[:, 1:2],
                                               func=AF.Sqrt),
             r=[colB], w=[colB], tag=f"rms_sqrt{s}")
        p.op("DVE", lambda eng: eng.reciprocal(out=rr_col[:], in_=colB[:, 2:3]),
             r=[colB], w=[rr_col], tag=f"rms_rcp{s}")
        p.op("DVE", lambda eng: eng.tensor_scalar(
            out=obuf[:], in0=hbuf[:], scalar1=rr_col[:], scalar2=None,
            op0=ALU.mult), r=[hbuf, rr_col], w=[obuf], tag=f"osc{s}a")
        p.op("DVE", lambda eng: eng.tensor_tensor(
            out=obuf[:], in0=obuf[:], in1=sc_b[:], op=ALU.mult),
            r=[obuf, sc_b], w=[obuf], tag=f"osc{s}b")

        ln_stats(obuf, obuf[:], rstd2, negm2, 1e-5, f"ln2_{s}")
        p.op("ACT", lambda eng: eng.activation(
            out=obuf[:], in_=obuf[:], func=AF.Identity, scale=rstd2[:], bias=negm2[:]),
            r=[obuf, rstd2, negm2], w=[obuf], tag=f"lnf{s}")
        p.op("DVE", (lambda x_t: lambda eng: eng.tensor_tensor(
            out=obuf[:], in0=obuf[:], in1=x_t[:], op=ALU.add))(x_t),
             r=[obuf, x_t], w=[obuf], tag=f"res{s}")
        od = p.dma(ov[s], obuf[:], r=[obuf, x_out], w=[x_out], engine="POOL",
                   tag=f"stx{s}")
        out_dmas.append(od)

    return out_dmas



_L1_CACHE = {}


def _prep_inputs_l1(inputs, core):
    c = host_constants()
    m = {
        "x_in": np.ascontiguousarray(inputs["L0_attn_x"][2 * core:2 * core + 2]),
        "WqT": np.ascontiguousarray(inputs["L0_attn_Wq"].T),
        "WkT": np.ascontiguousarray(inputs["L0_attn_Wk"].T),
        "WvT": np.ascontiguousarray(inputs["L0_attn_Wv"].T),
        "WoT": np.ascontiguousarray(inputs["L0_attn_Wo"].T),
        "WbT": np.ascontiguousarray(inputs["L0_attn_Wb"].T),
        "attn_bb": inputs["L0_attn_bb"].reshape(1, 16),
        "attn_bo": inputs["L0_attn_bo"].reshape(1, 1024),
        "attn_scale": inputs["L0_attn_scale"].reshape(1, 1024),
    }
    m.update(c)
    return m




def _mk_single(emit, name):
    p = Prog(name)
    x_in = p.dram_in("x_in", (BPC, T, D), F32)
    x_out = p.dram_out("x_out", (BPC, T, D), F32)
    with p.scope():
        ods = emit(p, x_in, x_out)
    p.barrier_outputs(ods)
    return p.build()


def build_launch1():
    return _mk_single(emit_launch1, "l1")


def build_launch2():
    return _mk_single(emit_launch2, "l2")


def build_launch3():
    return _mk_single(emit_launch3, "l3")


def build_launch4():
    return _mk_single(emit_launch4, "l4")


def build_fused():
    p = Prog("fused")
    x_in = p.dram_in("x_in", (BPC, T, D), F32)
    x_out = p.dram_out("x_out", (BPC, T, D), F32)
    s1 = p.dram_scratch("s1", (BPC, T, D), F32)
    s2 = p.dram_scratch("s2", (BPC, T, D), F32)
    s3 = p.dram_scratch("s3", (BPC, T, D), F32)
    with p.scope():
        emit_launch1(p, x_in, s1)
    p.barrier("b1")
    with p.scope():
        emit_launch2(p, s1, s2)
    p.barrier("b2")
    with p.scope():
        emit_launch3(p, s2, s3)
    p.barrier("b3")
    with p.scope():
        ods = emit_launch4(p, s3, x_out)
    p.barrier_outputs(ods)
    return p.build()


def _prep_inputs_fused(inputs, core):
    m = {}
    dummy = dict(inputs)
    dummy["x_prev"] = inputs["L0_attn_x"]
    for prep in (_prep_inputs_l1, _prep_inputs_l2, _prep_inputs_l3,
                 _prep_inputs_l4):
        mm_ = prep(dummy, core)
        mm_.pop("x_in", None)
        m.update(mm_)
    m["x_in"] = np.ascontiguousarray(inputs["L0_attn_x"][2 * core:2 * core + 2])
    return m


def kernel(**inputs):
    nc = build_fused()
    in_maps = [_prep_inputs_fused(inputs, c) for c in range(NCORES)]
    res = run_bass_kernel_spmd(nc, in_maps, core_ids=list(range(NCORES)))
    return np.concatenate([r["x_out"] for r in res.results], axis=0)


def kernel_unfused(**inputs):
    nc1 = build_launch1()
    in_maps = [_prep_inputs_l1(inputs, c) for c in range(NCORES)]
    res = run_bass_kernel_spmd(nc1, in_maps, core_ids=list(range(NCORES)))
    x1 = np.concatenate([r["x_out"] for r in res.results], axis=0)
    nc2 = build_launch2()
    inputs2 = dict(inputs); inputs2["x_prev"] = x1
    in_maps = [_prep_inputs_l2(inputs2, c) for c in range(NCORES)]
    res = run_bass_kernel_spmd(nc2, in_maps, core_ids=list(range(NCORES)))
    x2 = np.concatenate([r["x_out"] for r in res.results], axis=0)
    nc3 = build_launch3()
    inputs3 = dict(inputs); inputs3["x_prev"] = x2
    in_maps = [_prep_inputs_l3(inputs3, c) for c in range(NCORES)]
    res = run_bass_kernel_spmd(nc3, in_maps, core_ids=list(range(NCORES)))
    x3 = np.concatenate([r["x_out"] for r in res.results], axis=0)
    nc4 = build_launch4()
    inputs4 = dict(inputs); inputs4["x_prev"] = x3
    in_maps = [_prep_inputs_l4(inputs4, c) for c in range(NCORES)]
    res = run_bass_kernel_spmd(nc4, in_maps, core_ids=list(range(NCORES)))
    x4 = np.concatenate([r["x_out"] for r in res.results], axis=0)
    return x4



def emit_launch2(p, x_in, x_out):
    FO = 22
    NR = T // 128
    xv = x_in.ap().rearrange("b (n p) d -> (b n) p d", p=128)
    ov = x_out.ap().rearrange("b (n p) d -> (b n) p d", p=128)

    zs = [p.dram_scratch(f"zs{i}", (NT, D), F32) for i in range(2)]
    ium_s = p.dram_scratch("ium", (FO, 128, NT), BF16)
    h2n_s = p.dram_scratch("h2n", (8, 128, NT), BF16)
    zs_v = [p.split_views(zs[i], 4) for i in range(2)]
    ium_v = p.split_views(ium_s, 16)
    h2n_v = p.split_views(h2n_s, 8)

    ident32_h = p.dram_in("c_ident_f32", (128, 128), F32)
    ones_row_h = p.dram_in("c_ones_row", (1, 128), F32)
    mask_h = p.dram_in("c_ffn_mask", (8, 128, 1024), F32)

    c_id32 = p.sbuf("id32", [128, 128], F32)
    c_idbf = p.sbuf("idbf", [128, 128], BF16)
    c_ones_row = p.sbuf("ones_row", [1, 128], F32)
    masks = p.sbuf("masks", [128, 8 * 1024], BF16)
    p.dma(c_id32[:], ident32_h.ap(), r=[ident32_h], w=[c_id32])
    p.op("DVE", lambda eng: eng.tensor_copy(c_idbf[:], c_id32[:]),
         r=[c_id32], w=[c_idbf], tag="idbf")
    p.dma(c_ones_row[:], ones_row_h.ap(), r=[ones_row_h], w=[c_ones_row])
    xp = {}
    for nm in ("cq8", "sq8", "cki", "ski"):
        hh = p.dram_in("c_xp_" + nm, (T, 64), BF16)
        t_ = p.sbuf(nm, [128, NR * 64], BF16)
        p.dma(t_.ap().rearrange("p (n d) -> p n d", n=NR),
              hh.ap().rearrange("(n p) d -> p n d", p=128), r=[hh], w=[t_],
              tag="ld" + nm)
        xp[nm] = t_

    qkv_h = p.dram_in("ffn_wqkv", (2, 1024, 1536), F32)
    wo_h = p.dram_in("ffn_wo", (2, 512, 1024), F32)
    w1a_h = p.dram_in("ffn_w1a", (2, 1024, FO * 128), F32)
    w1g_h = p.dram_in("ffn_w1g", (2, 1024, FO * 128), F32)
    w2_h = p.dram_in("ffn_w2", (2, FO * 128, 1024), F32)
    b1a_h = p.dram_in("ffn_b1a", (2, 128, FO), F32)
    b1g_h = p.dram_in("ffn_b1g", (2, 128, FO), F32)
    rows_h = p.dram_in("ffn_rows", (2, 6, 1024), F32)
    qs_h = p.dram_in("ffn_qs", (2, 1, 64), F32)
    ks_h = p.dram_in("ffn_ks", (2, 1, 64), F32)

    ARW = 22528
    arena = p.sbuf("arena", [128, ARW], BF16)
    stg = [p.sbuf(f"stg{i}", [128, 1536], F32) for i in range(2)]
    b1a_c = p.sbuf("b1a_c", [128, FO], F32)
    b1g_c = p.sbuf("b1g_c", [128, FO], F32)
    rows_r = p.sbuf("rows_r", [1, 6 * 1024], F32)
    lg_b = p.sbuf("lg_b", [128, 1024], F32)
    lb_b = p.sbuf("lb_b", [128, 1024], F32)
    b2_b = p.sbuf("b2_b", [128, 1024], F32)
    qs_r = p.sbuf("qs_r", [1, 64], F32)
    ks_r = p.sbuf("ks_r", [1, 64], F32)
    qs_b = p.sbuf("qs_b", [128, 64], F32)
    ks_b = p.sbuf("ks_b", [128, 64], F32)
    for mi in range(8):
        p.dma(stg[0][:, 0:1024], mask_h.ap()[mi], r=[mask_h], w=[stg[0]],
              tag=f"ldm{mi}")
        p.op("DVE", (lambda mi: lambda eng: eng.tensor_copy(
            masks[:, mi * 1024:(mi + 1) * 1024], stg[0][:, 0:1024]))(mi),
            r=[stg[0]], w=[masks], tag=f"castm{mi}")

    kcache = p.sbuf("kcache", [128, 4096], BF16)
    vcache = p.sbuf("vcache", [128, 4096], BF16)
    h2n_ch = p.sbuf("h2n_ch", [128, 8 * 512], BF16)
    ium_chL = [p.sbuf(f"ium_ch{i}", [128, FO * 128], BF16) for i in range(2)]
    zt = [p.sbuf(f"zt{i}", [128, 1024], F32) for i in range(2)]
    zsh = p.sbuf("zsh", [128, 1024], F32)
    zt_bf = p.sbuf("zt_bf", [128, 1024], BF16)
    zsh_bf = p.sbuf("zsh_bf", [128, 1024], BF16)
    hsh_fm = p.sbuf("hsh_fm", [128, 1024], BF16)
    q_t = p.sbuf("q_t", [128, 512], F32)
    k_t = p.sbuf("k_t", [128, 512], F32)
    sq_scr = p.sbuf("sq_scr", [128, 512], F32)
    q1 = p.sbuf("q1", [128, 512], F32)
    rotb = p.sbuf("rotb", [128, 512], F32)
    tmp1 = sq_scr
    qk_bf = p.sbuf("qk_bf", [128, 512], BF16)
    q_fm = p.sbuf("q_fm", [128, 512], BF16)
    ssq = p.sbuf("ssq", [128, 8], F32)
    rln = p.sbuf("rln", [128, 8], F32)
    nmx = p.sbuf("nmx", [128, 1], F32)
    den = p.sbuf("den", [128, 1], F32)
    rden8 = p.sbuf("rden8", [128, 8], F32)
    sc0 = p.sbuf("scsb0", [128, 1024], F32)
    prob_L = [p.sbuf(f"prob{i}", [128, 1024], BF16) for i in range(2)]
    probT_L = [p.sbuf(f"probT{i}", [128, 1024], BF16) for i in range(2)]
    prob = prob_L[0]
    probT = probT_L[0]
    nmx_L = [p.sbuf(f"nmxL{i}", [128, 1], F32) for i in range(2)]
    den_L = [p.sbuf(f"denL{i}", [128, 1], F32) for i in range(2)]
    o_t = p.sbuf("o_t", [128, 512], BF16)
    o_fm = p.sbuf("o_fm", [128, 512], BF16)
    zo = p.sbuf("zo", [128, 1024], F32)
    h2 = p.sbuf("h2", [128, 1024], F32)
    sc_L = [sc0, h2]
    h2n = h2
    h2n_bf = zsh_bf
    aa = o_t
    gg = o_fm
    scr_bf = prob
    colA = p.sbuf("colA", [128, 8], F32)
    rstd = p.sbuf("rstd", [128, 1], F32)
    negm = p.sbuf("negm", [128, 1], F32)

    pP = p.psum("pP", [128, 512], F32)
    pS = p.psum("pS", [128, 1024], F32)
    pT = p.psum("pT", [128, 512], BF16)
    pO = p.psum("pO", [128, 512], F32)
    zP = p.psum("zP", [128, 1024], F32)
    pB = p.psum("pB", [128, 512], F32)

    out_dmas = []
    import os as _os2
    dbg = {}
    if _os2.environ.get("K_DEBUG2"):
        for nm, shape, dt in [("d_qfm", (128, 512), BF16),
                              ("d_kc", (128, 4096), BF16),
                              ("d_prob", (128, 1024), BF16),
                              ("d_ot", (128, 512), BF16),
                              ("d_hsh", (128, 1024), BF16),
                              ("d_zo", (128, 1024), F32)]:
            dbg[nm] = p.dram_out(nm, shape, dt)

    def tap2(nm, buf, cond):
        if dbg and cond and nm in dbg:
            od = p.dma(dbg[nm].ap(), buf[:], r=[buf], w=[dbg[nm]],
                       engine="POOL", tag="tap" + nm)
            out_dmas.append(od)

    def ln_stats(src_buf, src_ap, rstd_b, negm_b, eps, tag):
        rc, ncol = rstd_b[:], negm_b[:]
        p.op("DVE", (lambda src_ap: lambda eng: eng.tensor_reduce(
            out=colA[:, 0:1], in_=src_ap, axis=AX.X, op=ALU.add,
            negate=True))(src_ap),
            r=[src_buf], w=[colA], tag=tag + "_s")
        p.op("ACT", (lambda src_ap: lambda eng: eng.activation(
            out=scr_bf[:], in_=src_ap, func=AF.Square,
            accum_out=colA[:, 1:2]))(src_ap),
            r=[src_buf], w=[scr_bf, colA], tag=tag + "_q")
        p.op("DVE", lambda eng: ts_op(eng, ncol, colA[:, 0:1], 1.0 / D, None,
                                      ALU.mult), r=[colA], w=[negm_b], tag=tag)
        def var1(eng):
            eng.tensor_scalar(out=colA[:, 2:3], in0=ncol, scalar1=ncol,
                              scalar2=None, op0=ALU.mult)
            return eng.tensor_scalar(out=colA[:, 3:4], in0=colA[:, 1:2],
                                     scalar1=1.0 / D, scalar2=eps,
                                     op0=ALU.mult, op1=ALU.add)
        p.op("DVE", var1, r=[colA, negm_b], w=[colA], tag=tag + "_v1")
        p.op("DVE", lambda eng: eng.tensor_tensor(
            out=colA[:, 4:5], in0=colA[:, 3:4], in1=colA[:, 2:3],
            op=ALU.subtract), r=[colA], w=[colA], tag=tag + "_v2")
        p.op("ACT", lambda eng: eng.activation(out=colA[:, 5:6],
                                               in_=colA[:, 4:5], func=AF.Sqrt),
             r=[colA], w=[colA], tag=tag + "_sd")
        p.op("DVE", lambda eng: eng.reciprocal(out=rc, in_=colA[:, 5:6]),
             r=[colA], w=[rstd_b], tag=tag + "_r")
        p.op("DVE", lambda eng: eng.tensor_tensor(out=ncol, in0=ncol, in1=rc,
                                                  op=ALU.mult),
             r=[rstd_b, negm_b, colA], w=[negm_b], tag=tag + "_n")

    def bcast_row(dst, src_ap, n, r, tag):
        for si in range((n + 511) // 512):
            c0, c1 = si * 512, min(n, (si + 1) * 512)
            p.mm(pB[:, 0:c1 - c0], [(c_ones_row[0:1, :], src_ap[0:1, c0:c1])],
                 r=list(r) + [c_ones_row], w=[pB], tag=tag)
            p.op("DVE", (lambda c0, c1: lambda eng: eng.tensor_copy(
                dst[:, c0:c1], pB[:, 0:c1 - c0]))(c0, c1),
                r=[pB], w=[dst], tag=tag + "e")

    def tr8(src_buf, src_off, dst_buf, dst_off, n, tag):
        for g in range((n + 3) // 4):
            cnt = min(4, n - g * 4)
            def trf(eng, g=g, cnt=cnt, src_off=src_off):
                instr = None
                for i in range(cnt):
                    ci = g * 4 + i
                    instr = eng.transpose(
                        pT[:, i * 128:(i + 1) * 128],
                        src_buf[:, src_off + ci * 128: src_off + (ci + 1) * 128],
                        c_idbf[:])
                return instr
            p.op("TE", trf, r=[src_buf, c_idbf], w=[pT], tag=tag + f"t{g}")
            p.op("ACT", (lambda g, cnt, dst_off: lambda eng: eng.activation(
                out=dst_buf[:, dst_off + g * 512: dst_off + g * 512 + cnt * 128],
                in_=pT[:, 0:cnt * 128], func=AF.Copy))(g, cnt, dst_off),
                r=[pT], w=[dst_buf], tag=tag + f"e{g}")

    for s in range(NSUB):
        x_t = zt[s % 2]
        p.dma(x_t[:], xv[s], r=[x_in], w=[x_t], tag=f"p0ld{s}")
        ln_stats(x_t, x_t[:], rstd, negm, 1e-5, f"p0ln{s}")
        p.op("ACT", (lambda x_t: lambda eng: eng.activation(
            out=zo[:], in_=x_t[:], func=AF.Identity, scale=rstd[:],
            bias=negm[:]))(x_t), r=[x_t, rstd, negm], w=[zo], tag=f"p0a{s}")
        p.dma(zs[0].ap().rearrange("(n p) d -> n p d", p=128)[s], zo[:],
              r=[zo], w=[zs_v[0][s // 8]], engine="POOL", tag=f"p0st{s}")

    zin, zout = zs[0], zs[1]
    zin_v, zout_v = zs_v[0], zs_v[1]
    import os as _os
    _L2N = int(_os.environ.get("K_L2N", "2"))
    for l in range(_L2N):
        qv = qkv_h.ap()[l].rearrange("(n p) c -> n p c", p=128)
        for ci in range(8):
            st = stg[ci % 2]
            p.dma(st[:, 0:1536], qv[ci], r=[qkv_h], w=[st], tag=f"qkvld{l}{ci}")
            p.op("DVE", (lambda st, ci: lambda eng: eng.tensor_copy(
                arena[:, ci * 1536:(ci + 1) * 1536], st[:, 0:1536]))(st, ci),
                r=[st], w=[arena], tag=f"qkvc{l}{ci}")
        wv_ = wo_h.ap()[l].rearrange("(n p) c -> n p c", p=128)
        for ci in range(4):
            st = stg[ci % 2]
            p.dma(st[:, 0:1024], wv_[ci], r=[wo_h], w=[st], tag=f"wold{l}{ci}")
            p.op("DVE", (lambda st, ci: lambda eng: eng.tensor_copy(
                arena[:, 12288 + ci * 1024: 12288 + (ci + 1) * 1024],
                st[:, 0:1024]))(st, ci), r=[st], w=[arena], tag=f"woc{l}{ci}")
        p.dma(b1a_c[:], b1a_h.ap()[l], r=[b1a_h], w=[b1a_c], tag=f"b1a{l}")
        p.dma(b1g_c[:], b1g_h.ap()[l], r=[b1g_h], w=[b1g_c], tag=f"b1g{l}")
        p.dma(rows_r.ap().rearrange("o (n d) -> o n d", n=6),
              rows_h.ap()[l].unsqueeze(0), r=[rows_h], w=[rows_r],
              tag=f"rows{l}")
        bcast_row(lg_b, rows_r.ap()[:, 0:1024], 1024, [rows_r], f"lgb{l}")
        bcast_row(lb_b, rows_r.ap()[:, 1024:2048], 1024, [rows_r], f"lbb{l}")
        bcast_row(b2_b, rows_r.ap()[:, 2048:3072], 1024, [rows_r], f"b2b{l}")
        p.dma(qs_r[:], qs_h.ap()[l], r=[qs_h], w=[qs_r], tag=f"qs{l}")
        p.dma(ks_r[:], ks_h.ap()[l], r=[ks_h], w=[ks_r], tag=f"ks{l}")
        bcast_row(qs_b, qs_r.ap(), 64, [qs_r], f"qsb{l}")
        bcast_row(ks_b, ks_r.ap(), 64, [ks_r], f"ksb{l}")

        ziv = zin.ap().rearrange("(n p) d -> n p d", p=128)
        zflat = zin.ap()
        zov = zout.ap().rearrange("(n p) d -> n p d", p=128)

        for s in range(NSUB):
            seqpos = s % NR
            w_ = seqpos // 4
            ppos = seqpos % 4
            su = seqpos % 4
            if seqpos == 0:
                p.op("DVE", lambda eng: eng.memset(kcache[:], 0.0),
                     w=[kcache], tag=f"kc0_{l}_{s}")
                p.op("DVE", lambda eng: eng.memset(vcache[:], 0.0),
                     w=[vcache], tag=f"vc0_{l}_{s}")
            elif su == 0:
                p.op("DVE", lambda eng: eng.tensor_copy(
                    kcache[:, 0:2048], kcache[:, 2048:4096]),
                    r=[kcache], w=[kcache], tag=f"kcs_{l}_{s}")
                p.op("DVE", lambda eng: eng.tensor_copy(
                    vcache[:, 0:2048], vcache[:, 2048:4096]),
                    r=[vcache], w=[vcache], tag=f"vcs_{l}_{s}")
            z_t = zt[s % 2]
            zr = ([zin_v[s // 8]] if (s % 8 or s == 0)
                  else [zin_v[s // 8], zin_v[(s - 1) // 8]])
            p.dma(z_t[:], ziv[s], r=[zin_v[s // 8]], w=[z_t], tag=f"p1ld{l}_{s}")
            if seqpos == 0:
                p.op("DVE", lambda eng: eng.memset(zsh[:], 0.0), w=[zsh],
                     tag=f"zsh0{l}_{s}")
                p.dma(zsh[1:128, :], zflat[s * 128: s * 128 + 127, :],
                      r=zr, w=[zsh], tag=f"p1lds{l}_{s}")
            else:
                p.dma(zsh[:], zflat[s * 128 - 1: s * 128 + 127, :],
                      r=zr, w=[zsh], tag=f"p1lds{l}_{s}")
            p.op("DVE", (lambda z_t: lambda eng: eng.tensor_copy(
                zt_bf[:], z_t[:]))(z_t), r=[z_t], w=[zt_bf], tag=f"ztb{l}_{s}")
            p.op("DVE", lambda eng: eng.tensor_copy(zsh_bf[:], zsh[:]),
                 r=[zsh], w=[zsh_bf], tag=f"zshb{l}_{s}")
            tr8(zt_bf, 0, hsh_fm, 0, 4, f"trz{l}_{s}")
            tr8(zsh_bf, 512, hsh_fm, 512, 4, f"trs{l}_{s}")

            for which in range(3):
                pairs = [(hsh_fm[:, ci * 128:(ci + 1) * 128],
                          arena[:, ci * 1536 + which * 512:
                                ci * 1536 + which * 512 + 512])
                         for ci in range(8)]
                p.mm(pP[:], pairs, r=[hsh_fm, arena], w=[pP],
                     tag=f"qkv{l}_{s}_{which}")
                if which == 0:
                    p.op("ACT", lambda eng: eng.activation(
                        out=q_t[:], in_=pP[:], func=AF.Copy),
                        r=[pP], w=[q_t], tag=f"qe{l}_{s}")
                elif which == 1:
                    p.op("ACT", lambda eng: eng.activation(
                        out=k_t[:], in_=pP[:], func=AF.Copy),
                        r=[pP], w=[k_t], tag=f"ke{l}_{s}")
                else:
                    p.op("ACT", (lambda su: lambda eng: eng.activation(
                        out=vcache[:, 2048 + su * 512: 2048 + (su + 1) * 512],
                        in_=pP[:], func=AF.Copy))(su),
                        r=[pP], w=[vcache], tag=f"ve{l}_{s}")

            def qk_norm(src, sc_b, ctab, stab, dst_bf, tg, seqpos=seqpos):
                p.op("ACT", (lambda src: lambda eng: eng.activation(
                    out=sq_scr[:], in_=src[:], func=AF.Square))(src),
                    r=[src], w=[sq_scr], tag=tg + "sq")
                p.op("DVE", lambda eng: eng.tensor_reduce(
                    out=ssq[:], in_=sq_scr.ap().rearrange(
                        "p (h d) -> p h d", h=8), axis=AX.X, op=ALU.add),
                    r=[sq_scr], w=[ssq], tag=tg + "rd")
                p.op("ACT", lambda eng: eng.activation(
                    out=ssq[:], in_=ssq[:], func=AF.Sqrt),
                    r=[ssq], w=[ssq], tag=tg + "sd")
                p.op("DVE", lambda eng: ts_op(eng, ssq[:], ssq[:], 1e-12, None,
                                              ALU.max), r=[ssq], w=[ssq],
                     tag=tg + "mx")
                p.op("DVE", lambda eng: eng.reciprocal(out=rln[:], in_=ssq[:]),
                     r=[ssq], w=[rln], tag=tg + "rc")
                p.op("DVE", (lambda src: lambda eng: eng.tensor_tensor(
                    out=q1.ap().rearrange("p (h d) -> p h d", h=8),
                    in0=src.ap().rearrange("p (h d) -> p h d", h=8),
                    in1=rln.ap().unsqueeze(2).to_broadcast([128, 8, 64]),
                    op=ALU.mult))(src), r=[src, rln], w=[q1], tag=tg + "l2")
                p.op("DVE", (lambda sc_b: lambda eng: eng.tensor_tensor(
                    out=q1.ap().rearrange("p (h d) -> p h d", h=8),
                    in0=q1.ap().rearrange("p (h d) -> p h d", h=8),
                    in1=sc_b.ap().unsqueeze(1).to_broadcast([128, 8, 64]),
                    op=ALU.mult))(sc_b), r=[q1, sc_b], w=[q1], tag=tg + "qs")
                def rotf(eng):
                    eng.tensor_scalar(
                        out=rotb.ap().rearrange("p (h d) -> p h d", h=8)[:, :, 0:32],
                        in0=q1.ap().rearrange("p (h d) -> p h d", h=8)[:, :, 32:64],
                        scalar1=-1.0, scalar2=None, op0=ALU.mult)
                    return eng.tensor_copy(
                        rotb.ap().rearrange("p (h d) -> p h d", h=8)[:, :, 32:64],
                        q1.ap().rearrange("p (h d) -> p h d", h=8)[:, :, 0:32])
                p.op("DVE", rotf, r=[q1], w=[rotb], tag=tg + "rot")
                cap = ctab[:, seqpos * 64:(seqpos + 1) * 64].unsqueeze(1)\
                    .to_broadcast([128, 8, 64])
                sap = stab[:, seqpos * 64:(seqpos + 1) * 64].unsqueeze(1)\
                    .to_broadcast([128, 8, 64])
                p.op("DVE", (lambda cap: lambda eng: eng.tensor_tensor(
                    out=tmp1.ap().rearrange("p (h d) -> p h d", h=8),
                    in0=q1.ap().rearrange("p (h d) -> p h d", h=8),
                    in1=cap, op=ALU.mult))(cap), r=[q1], w=[tmp1],
                    tag=tg + "c")
                p.op("DVE", (lambda sap: lambda eng: eng.tensor_tensor(
                    out=rotb.ap().rearrange("p (h d) -> p h d", h=8),
                    in0=rotb.ap().rearrange("p (h d) -> p h d", h=8),
                    in1=sap, op=ALU.mult))(sap), r=[rotb], w=[rotb],
                    tag=tg + "s")
                p.op("DVE", (lambda dst_bf: lambda eng: eng.tensor_tensor(
                    out=dst_bf[:], in0=tmp1[:], in1=rotb[:], op=ALU.add))(dst_bf),
                    r=[tmp1, rotb], w=[dst_bf], tag=tg + "a")

            qk_norm(q_t, qs_b, xp["cq8"].h, xp["sq8"].h, qk_bf, f"qn{l}_{s}")
            tr8(qk_bf, 0, q_fm, 0, 4, f"qtr{l}_{s}")
            tap2("d_qfm", q_fm, l == 0 and s == 0)
            tap2("d_hsh", hsh_fm, l == 0 and s == 0)
            qk_norm(k_t, ks_b, xp["cki"].h, xp["ski"].h, qk_bf, f"kn{l}_{s}")
            def trfk(eng):
                instr = None
                for ci in range(4):
                    instr = eng.transpose(
                        pT[:, ci * 128:(ci + 1) * 128],
                        qk_bf[:, ci * 128:(ci + 1) * 128], c_idbf[:])
                return instr
            p.op("TE", trfk, r=[qk_bf, c_idbf], w=[pT], tag=f"ktr{l}_{s}")
            p.op("ACT", (lambda su: lambda eng: eng.activation(
                out=kcache[:, 2048 + su * 512: 2048 + (su + 1) * 512],
                in_=pT[:], func=AF.Copy))(su),
                r=[pT], w=[kcache], tag=f"kce{l}_{s}")

            mi = (0 if w_ == 0 else 1) * 4 + ppos
            tap2("d_kc", kcache, l == 0 and s == 0)
            for hh in range(8):
                prob = prob_L[hh % 2]
                probT = probT_L[hh % 2]
                nmx = nmx_L[hh % 2]
                den = den_L[hh % 2]
                base = (hh % 2) * 64
                ci = hh // 2
                kap = kcache.ap().rearrange(
                    "p (b ci x) -> p b ci x", b=8, ci=4)[
                    base:base + 64, :, ci, :]
                scb = sc_L[hh % 2]
                for nh in range(2):
                    p.mm(pS[:, nh * 512:(nh + 1) * 512],
                         [(q_fm[base:base + 64, ci * 128:(ci + 1) * 128],
                           kap[:, nh * 4:(nh + 1) * 4, :])],
                         r=[q_fm, kcache], w=[pS], tag=f"sc{l}_{s}_{hh}_{nh}")
                p.op("ACT", (lambda scb: lambda eng: eng.activation(
                    out=scb[:], in_=pS[:], func=AF.Copy))(scb),
                    r=[pS], w=[scb], tag=f"sce{l}_{s}_{hh}")
                p.op("DVE", (lambda mi, scb: lambda eng: eng.tensor_tensor(
                    out=scb[:], in0=scb[:],
                    in1=masks[:, mi * 1024:(mi + 1) * 1024], op=ALU.add))(mi, scb),
                    r=[scb, masks], w=[scb], tag=f"msk{l}_{s}_{hh}")
                p.op("DVE", (lambda nmx, scb: lambda eng: eng.tensor_reduce(
                    out=nmx[:], in_=scb[:], axis=AX.X, op=ALU.max,
                    negate=True))(nmx, scb),
                    r=[scb], w=[nmx], tag=f"mx{l}_{s}_{hh}")
                p.op("ACT", (lambda prob, nmx, den, scb: lambda eng: eng.activation(
                    out=prob[:], in_=scb[:], func=AF.Exp, bias=nmx[:],
                    accum_out=den[:]))(prob, nmx, den, scb),
                    r=[scb, nmx], w=[prob, den],
                    tag=f"ex{l}_{s}_{hh}")
                p.op("DVE", (lambda hh, den: lambda eng: eng.reciprocal(
                    out=rden8[:, hh:hh + 1], in_=den[:]))(hh, den),
                    r=[den], w=[rden8], tag=f"rd{l}_{s}_{hh}")
                tap2("d_prob", prob, l == 0 and s == 0 and hh == 0)
                tr8(prob, 0, probT, 0, 8, f"ptr{l}_{s}_{hh}")
                pairs = [(probT[:, su2 * 128:(su2 + 1) * 128],
                          vcache[:, su2 * 512 + hh * 64: su2 * 512 + hh * 64 + 64])
                         for su2 in range(8)]
                p.mm(pO[:, hh * 64:(hh + 1) * 64], pairs, r=[probT, vcache],
                     w=[pO], tag=f"av{l}_{s}_{hh}")
            def oev(eng):
                instr = None
                for hh in range(8):
                    instr = eng.tensor_scalar(
                        out=o_t[:, hh * 64:(hh + 1) * 64],
                        in0=pO[:, hh * 64:(hh + 1) * 64],
                        scalar1=rden8[:, hh:hh + 1], scalar2=None, op0=ALU.mult)
                return instr
            p.op("DVE", oev, r=[pO, rden8], w=[o_t], tag=f"oev{l}_{s}")
            tap2("d_ot", o_t, l == 0 and s == 0)
            tr8(o_t, 0, o_fm, 0, 4, f"otr{l}_{s}")
            for nh in range(2):
                pairs = [(o_fm[:, ci2 * 128:(ci2 + 1) * 128],
                          arena[:, 12288 + ci2 * 1024 + nh * 512:
                                12288 + ci2 * 1024 + nh * 512 + 512])
                         for ci2 in range(4)]
                p.mm(zP[:, nh * 512:(nh + 1) * 512], pairs, r=[o_fm, arena],
                     w=[zP], tag=f"wo{l}_{s}_{nh}")
            p.op("DVE", (lambda z_t: lambda eng: eng.tensor_tensor(
                out=zo[:], in0=zP[:], in1=z_t[:], op=ALU.add))(z_t),
                r=[zP, z_t], w=[zo], tag=f"zr{l}_{s}")
            tap2("d_zo", zo, l == 0 and s == 0)
            p.dma(zov[s], zo[:], r=[zo], w=[zout_v[s // 8]], engine="POOL",
                  tag=f"p1st{l}_{s}")

        zi2 = zout.ap().rearrange("(n p) d -> n p d", p=128)
        z2flat = zout.ap()
        for s in range(NSUB):
            seqpos = s % NR
            z_t = zt[s % 2]
            zr2 = ([zout_v[s // 8]] if (s % 8 or s == 0)
                   else [zout_v[s // 8], zout_v[(s - 1) // 8]])
            p.dma(z_t[:], zi2[s], r=[zout_v[s // 8]], w=[z_t],
                  tag=f"h2ld{l}_{s}")
            if seqpos == 0:
                p.op("DVE", lambda eng: eng.memset(zsh[:], 0.0), w=[zsh],
                     tag=f"h2z0{l}_{s}")
                p.dma(zsh[1:128, :], z2flat[s * 128: s * 128 + 127, :],
                      r=zr2, w=[zsh], tag=f"h2lds{l}_{s}")
            else:
                p.dma(zsh[:], z2flat[s * 128 - 1: s * 128 + 127, :],
                      r=zr2, w=[zsh], tag=f"h2lds{l}_{s}")
            p.op("DVE", (lambda z_t: lambda eng: eng.tensor_copy(
                h2[:, 0:512], z_t[:, 0:512]))(z_t), r=[z_t], w=[h2],
                tag=f"h2a{l}_{s}")
            p.op("DVE", lambda eng: eng.tensor_copy(h2[:, 512:1024],
                                                    zsh[:, 512:1024]),
                 r=[zsh], w=[h2], tag=f"h2b{l}_{s}")
            ln_stats(h2, h2[:], rstd, negm, 1e-5, f"h2ln{l}_{s}")
            p.op("ACT", lambda eng: eng.activation(
                out=h2n[:], in_=h2[:], func=AF.Identity, scale=rstd[:],
                bias=negm[:]), r=[h2, rstd, negm], w=[h2n], tag=f"h2n{l}_{s}")
            p.op("DVE", lambda eng: eng.tensor_tensor(
                out=h2n[:], in0=h2n[:], in1=lg_b[:], op=ALU.mult),
                r=[h2n, lg_b], w=[h2n], tag=f"h2g{l}_{s}")
            p.op("DVE", lambda eng: eng.tensor_tensor(
                out=h2n_bf[:], in0=h2n[:], in1=lb_b[:], op=ALU.add),
                r=[h2n, lb_b], w=[h2n_bf], tag=f"h2o{l}_{s}")
            tr8(h2n_bf, 0, zt_bf, 0, 8, f"h2t{l}_{s}")
            p.dma(h2n_s.ap()[:, :, s * 128:(s + 1) * 128].rearrange(
                "c p x -> p c x"),
                zt_bf.ap().rearrange("p (c x) -> p c x", c=8),
                r=[zt_bf], w=[h2n_v[s // 4]], engine="POOL", tag=f"h2st{l}_{s}")

        iv = ium_s.ap()
        for half in range(2):
            fos = list(range(half * 11, (half + 1) * 11))
            for j, fo in enumerate(fos):
                st = stg[j % 2]
                w1av = w1a_h.ap()[l, :, fo * 128:(fo + 1) * 128]\
                    .rearrange("(nc p) f -> p nc f", p=128)
                p.dma(st.ap()[:, 0:1024].rearrange("p (nc f) -> p nc f", nc=8),
                      w1av, r=[w1a_h], w=[st], tag=f"w1al{l}_{fo}")
                p.op("DVE", (lambda st, j: lambda eng: eng.tensor_copy(
                    arena[:, j * 2048: j * 2048 + 1024], st[:, 0:1024]))(st, j),
                    r=[st], w=[arena], tag=f"w1ac{l}_{fo}")
                w1gv = w1g_h.ap()[l, :, fo * 128:(fo + 1) * 128]\
                    .rearrange("(nc p) f -> p nc f", p=128)
                p.dma(st.ap()[:, 0:1024].rearrange("p (nc f) -> p nc f", nc=8),
                      w1gv, r=[w1g_h], w=[st], tag=f"w1gl{l}_{fo}")
                p.op("DVE", (lambda st, j: lambda eng: eng.tensor_copy(
                    arena[:, j * 2048 + 1024: j * 2048 + 2048],
                    st[:, 0:1024]))(st, j),
                    r=[st], w=[arena], tag=f"w1gc{l}_{fo}")
            for ch in range(8):
                p.dma(h2n_ch.ap().rearrange("p (c x) -> p c x", c=8),
                      h2n_s.ap()[:, :, ch * 512:(ch + 1) * 512].rearrange(
                          "c p x -> p c x"),
                      r=[h2n_v[ch]], w=[h2n_ch], tag=f"h2cl{l}_{half}_{ch}")
                for j, fo in enumerate(fos):
                    pa = [(arena[:, j * 2048 + ci * 128: j * 2048 + (ci + 1) * 128],
                           h2n_ch[:, ci * 512:(ci + 1) * 512])
                          for ci in range(8)]
                    pg = [(arena[:, j * 2048 + 1024 + ci * 128:
                                 j * 2048 + 1024 + (ci + 1) * 128],
                           h2n_ch[:, ci * 512:(ci + 1) * 512])
                          for ci in range(8)]
                    p.mm(pS[:, 0:512], pa, r=[arena, h2n_ch], w=[pS],
                         tag=f"w1am{l}_{fo}_{ch}")
                    p.mm(pS[:, 512:1024], pg, r=[arena, h2n_ch], w=[pS],
                         tag=f"w1gm{l}_{fo}_{ch}")
                    p.op("ACT", (lambda fo: lambda eng: eng.activation(
                        out=aa[:], in_=pS[:, 0:512], func=AF.Identity,
                        bias=b1a_c[:, fo:fo + 1]))(fo),
                        r=[pS, b1a_c], w=[aa], tag=f"aae{l}_{fo}_{ch}")
                    p.op("ACT", (lambda fo: lambda eng: eng.activation(
                        out=gg[:], in_=pS[:, 512:1024], func=AF.Gelu,
                        bias=b1g_c[:, fo:fo + 1]))(fo),
                        r=[pS, b1g_c], w=[gg], tag=f"gge{l}_{fo}_{ch}")
                    p.op("DVE", lambda eng: eng.tensor_tensor(
                        out=aa[:], in0=aa[:], in1=gg[:], op=ALU.mult),
                        r=[aa, gg], w=[aa], tag=f"ium{l}_{fo}_{ch}")
                    p.dma(iv[fo, :, ch * 512:(ch + 1) * 512], aa[:],
                          r=[aa], w=[ium_v[(fo // 11) * 8 + ch]],
                          engine="POOL", tag=f"iust{l}_{fo}_{ch}")

        for j in range(FO):
            st = stg[j % 2]
            w2v = w2_h.ap()[l, j * 128:(j + 1) * 128, :]
            p.dma(st[:, 0:1024], w2v, r=[w2_h], w=[st], tag=f"w2l{l}_{j}")
            p.op("DVE", (lambda st, j: lambda eng: eng.tensor_copy(
                arena[:, j * 1024:(j + 1) * 1024], st[:, 0:1024]))(st, j),
                r=[st], w=[arena], tag=f"w2c{l}_{j}")
        for ch in range(8):
            for tsub in range(4):
                s = ch * 4 + tsub
                ich = ium_chL[s % 2]
                for fo in range(FO):
                    p.dma(ich[:, fo * 128:(fo + 1) * 128],
                          iv[fo, :, s * 128:(s + 1) * 128],
                          r=[ium_v[(fo // 11) * 8 + ch]], w=[ich],
                          engine=("SYNC" if fo % 2 == 0 else "POOL"),
                          tag=f"iuld{l}_{s}_{fo}")
                for nh in range(2):
                    pairs = [(ich[:, fo * 128:(fo + 1) * 128],
                              arena[:, fo * 1024 + nh * 512:
                                    fo * 1024 + nh * 512 + 512])
                             for fo in range(FO)]
                    p.mm(zP[:, nh * 512:(nh + 1) * 512], pairs,
                         r=[ich, arena], w=[zP], tag=f"w2m{l}_{s}_{nh}")
                z_t = zt[s % 2]
                p.dma(z_t[:], zi2[s], r=[zout_v[s // 8]], w=[z_t],
                      tag=f"p2bl{l}_{s}")
                p.op("DVE", lambda eng: eng.tensor_tensor(
                    out=zo[:], in0=zP[:], in1=b2_b[:], op=ALU.add),
                    r=[zP, b2_b], w=[zo], tag=f"zb2{l}_{s}")
                p.op("DVE", (lambda z_t: lambda eng: eng.tensor_tensor(
                    out=zo[:], in0=zo[:], in1=z_t[:], op=ALU.add))(z_t),
                    r=[zo, z_t], w=[zo], tag=f"zr2{l}_{s}")
                p.dma(zin.ap().rearrange(
                    "(n p) d -> n p d", p=128)[s], zo[:],
                    r=[zo], w=[zin_v[s // 8]], engine="POOL", tag=f"p2st{l}_{s}")

    zfv = zin.ap().rearrange("(n p) d -> n p d", p=128)
    for s in range(NSUB):
        z_t = zt[s % 2]
        p.dma(z_t[:], zfv[s], r=[zin_v[s // 8]], w=[z_t], tag=f"p3ld{s}")
        p.dma(zsh[:], xv[s], r=[x_in], w=[zsh], tag=f"p3lx{s}")
        ln_stats(z_t, z_t[:], rstd, negm, 1e-5, f"p3ln{s}")
        p.op("ACT", (lambda z_t: lambda eng: eng.activation(
            out=zo[:], in_=z_t[:], func=AF.Identity, scale=rstd[:],
            bias=negm[:]))(z_t), r=[z_t, rstd, negm], w=[zo], tag=f"p3a{s}")
        p.op("DVE", lambda eng: eng.tensor_tensor(
            out=zo[:], in0=zo[:], in1=zsh[:], op=ALU.add),
            r=[zo, zsh], w=[zo], tag=f"p3r{s}")
        od = p.dma(ov[s], zo[:], r=[zo], w=[x_out], engine="POOL",
                   tag=f"p3st{s}")
        out_dmas.append(od)

    return out_dmas


def _prep_inputs_l2(inputs, core):
    c = host_constants2()
    wqkv = inputs["L0_ffn_wqkv"].astype(np.float32)
    w1 = inputs["L0_ffn_w1"]
    w1a = np.zeros((2, 1024, 2816), np.float32)
    w1g = np.zeros((2, 1024, 2816), np.float32)
    w1a[:, :, 0:2730] = w1[:, :, 0:2730]
    w1g[:, :, 0:2730] = w1[:, :, 2730:5460]
    w2 = np.zeros((2, 2816, 1024), np.float32)
    w2[:, 0:2730, :] = inputs["L0_ffn_w2"]
    b1 = inputs["L0_ffn_b1"]
    b1a = np.zeros((2, 2816), np.float32)
    b1g = np.zeros((2, 2816), np.float32)
    b1a[:, 0:2730] = b1[:, 0:2730]
    b1g[:, 0:2730] = b1[:, 2730:5460]
    b1a = np.ascontiguousarray(b1a.reshape(2, 22, 128).transpose(0, 2, 1))
    b1g = np.ascontiguousarray(b1g.reshape(2, 22, 128).transpose(0, 2, 1))
    rows = np.stack([np.concatenate([
        inputs["L0_ffn_ln_g"][l], inputs["L0_ffn_ln_b"][l],
        inputs["L0_ffn_b2"][l], np.zeros(1024 * 3, np.float32)])
        for l in range(2)]).reshape(2, 6, 1024)
    m = {
        "x_in": np.ascontiguousarray(inputs["x_prev"][2 * core:2 * core + 2]),
        "ffn_wqkv": wqkv,
        "ffn_wo": inputs["L0_ffn_wo"].astype(np.float32),
        "ffn_w1a": w1a, "ffn_w1g": w1g, "ffn_w2": w2,
        "ffn_b1a": b1a, "ffn_b1g": b1g,
        "ffn_rows": rows,
        "ffn_qs": inputs["L0_ffn_q_scale"].reshape(2, 1, 64).astype(np.float32),
        "ffn_ks": inputs["L0_ffn_k_scale"].reshape(2, 1, 64).astype(np.float32),
    }
    m.update(c)
    return m


def host_constants2():
    c = {}
    ident = np.eye(128, dtype=np.float32)
    c["c_ident_f32"] = ident
    c["c_ones_row"] = np.ones((1, 128), dtype=np.float32)
    Dh, W = 64, 512
    half = np.arange(0, Dh, 2, dtype=np.float32)
    inv = 1.0 / 10000.0 ** (half / Dh)
    t = np.arange(T, dtype=np.float32)
    fr = t[:, None] * inv[None, :]
    fr2 = np.concatenate([fr, fr], -1)
    base = (half + 0.4 * Dh) / (1.4 * Dh)
    power = (t - T // 2) / float(W // 2)
    sc = base[None, :] ** power[:, None]
    sc = np.concatenate([sc, sc], -1)
    cos, sin = np.cos(fr2), np.sin(fr2)
    c["c_xp_cq8"] = (8.0 * cos * sc).astype(ml_dtypes.bfloat16)
    c["c_xp_sq8"] = (8.0 * sin * sc).astype(ml_dtypes.bfloat16)
    c["c_xp_cki"] = (cos / sc).astype(ml_dtypes.bfloat16)
    c["c_xp_ski"] = (sin / sc).astype(ml_dtypes.bfloat16)
    masks = np.full((8, 128, 1024), -1e30, np.float32)
    sr = np.arange(1024)
    for pzn in range(4):
        for tl in range(128):
            tpos = pzn * 128 + tl
            ok = (sr >= tpos) & (sr <= 512 + tpos)
            masks[4 + pzn, tl, ok] = 0.0
            ok0 = (sr >= 512) & (sr - 512 <= tpos)
            masks[pzn, tl, ok0] = 0.0
    c["c_ffn_mask"] = masks
    return c



def emit_launch3(p, x_in, x_out):
    NR = T // 128
    NCH = NT // 512
    CT = 16
    S = 16
    xv = x_in.ap().rearrange("b (n p) d -> (b n) p d", p=128)
    ov = x_out.ap().rearrange("b (n p) d -> (b n) p d", p=128)

    zm = [p.dram_scratch(f"zm{i}", (NT, D), F32) for i in range(2)]
    zm_v = [p.split_views(zm[i], 8) for i in range(2)]
    wouts = p.dram_scratch("wouts", (16, 128, 1024), BF16)
    wres_s = p.dram_scratch("wres_s", (8, 128, 2048), BF16)

    ones_row_h = p.dram_in("c_ones_row", (1, 128), F32)
    ident32_h = p.dram_in("c_ident_f32", (128, 128), F32)
    esel_h = p.dram_in("c_esel", (16, 2048), F32)
    c_id32 = p.sbuf("id32", [128, 128], F32)
    c_idbf = p.sbuf("idbf", [128, 128], BF16)
    c_ones_row = p.sbuf("ones_row", [1, 128], F32)
    c_esel = p.sbuf("esel", [16, 2048], F32)
    p.dma(c_id32[:], ident32_h.ap(), r=[ident32_h], w=[c_id32])
    p.op("DVE", lambda eng: eng.tensor_copy(c_idbf[:], c_id32[:]),
         r=[c_id32], w=[c_idbf], tag="idbf")
    p.dma(c_ones_row[:], ones_row_h.ap(), r=[ones_row_h], w=[c_ones_row])
    p.dma(c_esel[:], esel_h.ap(), r=[esel_h], w=[c_esel])

    win_h = p.dram_in("mb_W_in", (1024, 4096), F32)
    wout_h = p.dram_in("mb_W_out", (2048, 1024), F32)
    wxd_h = p.dram_in("mb_Wx_dt", (2048, 64), F32)
    wxb_h = p.dram_in("mb_Wx_B", (2048, 16), F32)
    wxc_h = p.dram_in("mb_Wx_C", (2048, 16), F32)
    wdt_h = p.dram_in("mb_W_dt", (64, 2048), F32)
    convw_h = p.dram_in("mb_conv_w", (128, 16 * 4), F32)
    cols_h = p.dram_in("mb_cols", (128, 16 * 4), F32)
    A_h = p.dram_in("mb_A", (128, 16 * 16), F32)

    w_in = p.sbuf("w_in", [128, 8 * 2048], BF16)
    wres_st = [p.sbuf(f"wres{i}", [128, 1024], BF16) for i in range(2)]
    wx_dt = p.sbuf("wx_dt", [128, CT * 64], BF16)
    wx_B = p.sbuf("wx_B", [128, CT * 16], BF16)
    wx_C = p.sbuf("wx_C", [128, CT * 16], BF16)
    w_dt = p.sbuf("w_dt", [64, 2048], BF16)
    convw = p.sbuf("convw", [128, 64], F32)
    colsb = p.sbuf("colsb", [128, 64], F32)
    A_sb = p.sbuf("A_sb", [128, 256], F32)
    wo_st = [p.sbuf(f"wo_st{i}", [128, 512], BF16) for i in range(2)]

    stg = [p.sbuf("stg0", [128, 2048], F32)]
    stg = [stg[0], stg[0]]

    zc = p.sbuf("zc", [128, 2 * 1024], F32)
    zn_fm = p.sbuf("zn_fm", [128, 8 * 512], BF16)
    u_bf = p.sbuf("u_bf", [128, CT * 512], BF16)
    d_bf = p.sbuf("d_bf", [128, CT * 512], BF16)
    du_bf = p.sbuf("du_bf", [128, CT * 512], BF16)
    res_bf = p.sbuf("res_bf", [128, CT * 512], BF16)
    y_bf = p.sbuf("y_bf", [128, CT * 512], BF16)
    x_halo = p.sbuf("x_halo", [128, 515], BF16)
    tails = p.sbuf("tails", [128, CT * 3], BF16)
    conv_f = p.sbuf("conv_f", [128, 512], F32)
    conv_g = p.sbuf("conv_g", [128, 512], F32)
    dt_bf = p.sbuf("dt_bf", [64, 512], BF16)
    B_sb = p.sbuf("B_sb", [16, 512], F32)
    C_sb = p.sbuf("C_sb", [16, 512], F32)
    B_bc = p.sbuf("B_bc", [128, 512], BF16)
    C_bc = p.sbuf("C_bc", [128, 512], BF16)
    d_t = [[p.sbuf(f"d_t{i}_{k}", [128, 512], BF16) for k in range(4)]
           for i in range(2)]
    b_t = [p.sbuf(f"b_t{k}", [128, 512], BF16) for k in range(4)]
    h4 = [p.sbuf(f"h4_{i}", [128, 2048], BF16) for i in range(2)]
    m4 = p.sbuf("m4", [128, 2048], BF16)
    yd4L = h4
    hstate = p.sbuf("hstate", [128, 256], F32)
    sp_t0 = conv_f
    sp_a = conv_g
    zout_t = p.sbuf("zout_t", [128, 1024], F32)
    scr_bf = p.sbuf("scr_bf", [128, 1024], BF16)
    zn_t = scr_bf
    colA = p.sbuf("colA", [128, 8], F32)
    rstd = p.sbuf("rstd", [128, 1], F32)
    negm = p.sbuf("negm", [128, 1], F32)

    pX = p.psum("pX", [128, 512], F32)
    pD = p.psum("pD", [64, 512], F32)
    pBB = p.psum("pBB", [128, 512], F32)
    pT = p.psum("pT", [128, 512], BF16)
    pZ = p.psum("pZ", [128, 2048], F32)

    out_dmas = []
    import os as _os3
    dbg = {}
    if _os3.environ.get("K_DEBUG3"):
        for nm, shape, dt in [("d_u", (128, CT * 512), BF16),
                              ("d_dl", (128, CT * 512), BF16),
                              ("d_res", (128, CT * 512), BF16),
                              ("d_y", (128, CT * 512), BF16),
                              ("d_B", (16, 512), F32),
                              ("d_C", (16, 512), F32),
                              ("d_znfm", (128, 8 * 512), BF16),
                              ("d_yfin", (128, CT * 512), BF16)]:
            dbg[nm] = p.dram_out(nm, shape, dt)

    def tap3(nm, buf, cond):
        if dbg and cond and nm in dbg:
            od = p.dma(dbg[nm].ap(), buf[:], r=[buf], w=[dbg[nm]],
                       engine="POOL", tag="tap" + nm)
            out_dmas.append(od)

    def ln_stats(src_buf, src_ap, eps, tag):
        p.op("DVE", (lambda src_ap: lambda eng: eng.tensor_reduce(
            out=colA[:, 0:1], in_=src_ap, axis=AX.X, op=ALU.add,
            negate=True))(src_ap), r=[src_buf], w=[colA], tag=tag + "_s")
        p.op("ACT", (lambda src_ap: lambda eng: eng.activation(
            out=scr_bf[:], in_=src_ap, func=AF.Square,
            accum_out=colA[:, 1:2]))(src_ap),
            r=[src_buf], w=[scr_bf, colA], tag=tag + "_q")
        p.op("DVE", lambda eng: ts_op(eng, negm[:], colA[:, 0:1], 1.0 / D, None,
                                      ALU.mult), r=[colA], w=[negm], tag=tag)
        def var1(eng):
            eng.tensor_scalar(out=colA[:, 2:3], in0=negm[:], scalar1=negm[:],
                              scalar2=None, op0=ALU.mult)
            return eng.tensor_scalar(out=colA[:, 3:4], in0=colA[:, 1:2],
                                     scalar1=1.0 / D, scalar2=eps,
                                     op0=ALU.mult, op1=ALU.add)
        p.op("DVE", var1, r=[colA, negm], w=[colA], tag=tag + "_v1")
        p.op("DVE", lambda eng: eng.tensor_tensor(
            out=colA[:, 4:5], in0=colA[:, 3:4], in1=colA[:, 2:3],
            op=ALU.subtract), r=[colA], w=[colA], tag=tag + "_v2")
        p.op("ACT", lambda eng: eng.activation(out=colA[:, 5:6],
                                               in_=colA[:, 4:5], func=AF.Sqrt),
             r=[colA], w=[colA], tag=tag + "_sd")
        p.op("DVE", lambda eng: eng.reciprocal(out=rstd[:], in_=colA[:, 5:6]),
             r=[colA], w=[rstd], tag=tag + "_r")
        p.op("DVE", lambda eng: eng.tensor_tensor(out=negm[:], in0=negm[:],
                                                  in1=rstd[:], op=ALU.mult),
             r=[rstd, negm, colA], w=[negm], tag=tag + "_n")

    wiv = win_h.ap().rearrange("(n p) c -> n p c", p=128)
    for ci in range(8):
        st = stg[0]
        p.dma(st[:], wiv[ci][:, 0:2048], r=[win_h], w=[st], tag=f"wil{ci}")
        p.op("DVE", (lambda st, ci: lambda eng: eng.tensor_copy(
            w_in[:, ci * 2048:(ci + 1) * 2048], st[:]))(st, ci),
            r=[st], w=[w_in], tag=f"wic{ci}")
        p.dma(st[:], wiv[ci][:, 2048:4096], r=[win_h], w=[st], tag=f"wrl{ci}")
        p.op("DVE", (lambda st: lambda eng: eng.tensor_copy(
            u_bf[:, 0:2048], st[:]))(st), r=[st], w=[u_bf], tag=f"wrc{ci}")
        p.dma(wres_s.ap()[ci], u_bf[:, 0:2048], r=[u_bf], w=[wres_s],
              engine="POOL", tag=f"wrs{ci}")
    wov = wout_h.ap().rearrange("(n p) c -> n p c", p=128)
    for ci in range(16):
        st = stg[ci % 2]
        p.dma(st[:, 0:1024], wov[ci], r=[wout_h], w=[st], tag=f"wol{ci}")
        p.op("DVE", (lambda st: lambda eng: eng.tensor_copy(
            scr_bf[:, 0:1024], st[:, 0:1024]))(st), r=[st], w=[scr_bf],
            tag=f"woc{ci}")
        p.dma(wouts.ap()[ci], scr_bf[:, 0:1024], r=[scr_bf], w=[wouts],
              engine="POOL", tag=f"wos{ci}")
    for (hh, dst, cols) in ((wxd_h, wx_dt, 64), (wxb_h, wx_B, 16),
                            (wxc_h, wx_C, 16)):
        vv = hh.ap().rearrange("(n p) c -> n p c", p=128)
        for ci in range(16):
            st = stg[ci % 2]
            p.dma(st[:, 0:cols], vv[ci], r=[hh], w=[st], tag=f"wx{cols}_{ci}")
            p.op("DVE", (lambda st, ci, dst, cols: lambda eng: eng.tensor_copy(
                dst[:, ci * cols:(ci + 1) * cols], st[:, 0:cols]))(st, ci, dst, cols),
                r=[st], w=[dst], tag=f"wxc{cols}_{ci}")
    st = stg[0]
    p.dma(st[:, 0:2048].rearrange("p c -> p c")[0:64, :], wdt_h.ap(),
          r=[wdt_h], w=[st], tag="wdtl")
    p.op("DVE", (lambda st: lambda eng: eng.tensor_copy(
        w_dt[:], st[0:64, 0:2048]))(st), r=[st], w=[w_dt], tag="wdtc")
    p.dma(convw[:], convw_h.ap(), r=[convw_h], w=[convw], tag="convwl")
    p.dma(colsb[:], cols_h.ap(), r=[cols_h], w=[colsb], tag="colsl")
    p.dma(A_sb[:], A_h.ap(), r=[A_h], w=[A_sb], tag="al")

    zv0 = zm[0].ap().rearrange("(n p) d -> n p d", p=128)
    for s in range(NSUB):
        x_t = zc
        p.dma(zc[:, 0:1024], xv[s], r=[x_in], w=[zc], tag=f"p0ld{s}")
        ln_stats(zc, zc[:, 0:1024], 1e-5, f"p0ln{s}")
        p.op("ACT", lambda eng: eng.activation(
            out=zout_t[:], in_=zc[:, 0:1024], func=AF.Identity, scale=rstd[:],
            bias=negm[:]), r=[zc, rstd, negm], w=[zout_t], tag=f"p0a{s}")
        p.dma(zv0[s], zout_t[:], r=[zout_t], w=[zm_v[0][s // 4]], engine="POOL",
              tag=f"p0st{s}")

    SQD = float(np.sqrt(D))
    zin, zout = zm[0], zm[1]
    zin_v, zout_v = zm_v[0], zm_v[1]
    for ps in range(4):
        ziv = zin.ap().rearrange("(n p) d -> n p d", p=128)
        zov = zout.ap().rearrange("(n p) d -> n p d", p=128)
        for seq in range(BPC):
            p.op("DVE", lambda eng: eng.memset(tails[:], 0.0), w=[tails],
                 tag=f"t0_{ps}_{seq}")
            p.op("DVE", lambda eng: eng.memset(hstate[:], 0.0), w=[hstate],
                 tag=f"h0_{ps}_{seq}")
            for ch in range(4):
                gch = seq * 4 + ch
                for su in range(4):
                    s = gch * 4 + su
                    sl = (su % 2) * 1024
                    p.dma(zc[:, sl:sl + 1024], ziv[s], r=[zin_v[s // 4]],
                          w=[zc], tag=f"zl{ps}_{gch}_{su}")
                    ln_src = zc[:, sl:sl + 1024]
                    p.op("ACT", (lambda ln_src: lambda eng: eng.activation(
                        out=scr_bf[:], in_=ln_src, func=AF.Square,
                        accum_out=colA[:, 1:2]))(ln_src),
                        r=[zc], w=[scr_bf, colA], tag=f"rq{ps}_{gch}_{su}")
                    p.op("ACT", lambda eng: eng.activation(
                        out=colA[:, 2:3], in_=colA[:, 1:2], func=AF.Sqrt),
                        r=[colA], w=[colA], tag=f"rsd{ps}_{gch}_{su}")
                    p.op("DVE", lambda eng: ts_op(
                        eng, colA[:, 3:4], colA[:, 2:3], 1e-12, None, ALU.max),
                        r=[colA], w=[colA], tag=f"rmx{ps}_{gch}_{su}")
                    p.op("DVE", lambda eng: eng.reciprocal(
                        out=colA[:, 4:5], in_=colA[:, 3:4]),
                        r=[colA], w=[colA], tag=f"rrc{ps}_{gch}_{su}")
                    p.op("DVE", lambda eng: ts_op(
                        eng, rstd[:], colA[:, 4:5], SQD, None, ALU.mult),
                        r=[colA], w=[rstd], tag=f"rsc{ps}_{gch}_{su}")
                    p.op("ACT", (lambda ln_src: lambda eng: eng.activation(
                        out=zn_t[:], in_=ln_src, func=AF.Identity,
                        scale=rstd[:]))(ln_src),
                        r=[zc, rstd], w=[zn_t], tag=f"ra{ps}_{gch}_{su}")
                    for g in range(2):
                        def trf(eng, g=g):
                            instr = None
                            for i in range(4):
                                ci = g * 4 + i
                                instr = eng.transpose(
                                    pT[:, i * 128:(i + 1) * 128],
                                    zn_t[:, ci * 128:(ci + 1) * 128], c_idbf[:])
                            return instr
                        p.op("TE", trf, r=[zn_t, c_idbf], w=[pT],
                             tag=f"zt{ps}_{gch}_{su}{g}")
                        def ev(eng, g=g, su=su):
                            instr = None
                            for i in range(4):
                                ci = g * 4 + i
                                instr = eng.activation(
                                    out=zn_fm[:, ci * 512 + su * 128:
                                              ci * 512 + su * 128 + 128],
                                    in_=pT[:, i * 128:(i + 1) * 128],
                                    func=AF.Copy)
                            return instr
                        p.op("ACT", ev, r=[pT], w=[zn_fm],
                             tag=f"ze{ps}_{gch}_{su}{g}")

                for ct in range(CT):
                    wrb = wres_st[ct % 2]
                    p.dma(wrb.ap().rearrange("p (c x) -> p c x", c=8),
                          wres_s.ap()[:, :, ct * 128:(ct + 1) * 128].rearrange(
                              "c p x -> p c x"),
                          r=[wres_s], w=[wrb], tag=f"wrld{ps}_{gch}_{ct}")
                    pairs = [(w_in[:, ci * 2048 + ct * 128: ci * 2048 + (ct + 1) * 128],
                              zn_fm[:, ci * 512:(ci + 1) * 512])
                             for ci in range(8)]
                    p.mm(pX[:], pairs, r=[w_in, zn_fm], w=[pX],
                         tag=f"xin{ps}_{gch}_{ct}")
                    p.op("DVE", (lambda ct: lambda eng: eng.tensor_copy(
                        x_halo[:, 0:3], tails[:, ct * 3:(ct + 1) * 3]))(ct),
                        r=[tails], w=[x_halo], tag=f"halo{ps}_{gch}_{ct}")
                    p.op("ACT", lambda eng: eng.activation(
                        out=x_halo[:, 3:515], in_=pX[:], func=AF.Copy),
                        r=[pX], w=[x_halo], tag=f"xe{ps}_{gch}_{ct}")
                    p.op("DVE", (lambda ct: lambda eng: eng.tensor_copy(
                        tails[:, ct * 3:(ct + 1) * 3], x_halo[:, 512:515]))(ct),
                        r=[x_halo], w=[tails], tag=f"tsv{ps}_{gch}_{ct}")
                    p.op("DVE", (lambda ct: lambda eng: eng.tensor_scalar(
                        out=conv_f[:], in0=x_halo[:, 0:512],
                        scalar1=convw[:, ct * 4 + 0: ct * 4 + 1], scalar2=None,
                        op0=ALU.mult))(ct),
                        r=[x_halo, convw], w=[conv_f], tag=f"cv0{ps}_{gch}_{ct}")
                    for k in range(1, 4):
                        src_b, dst_b = (conv_f, conv_g) if k % 2 == 1 else (conv_g, conv_f)
                        p.op("DVE", (lambda ct, k, src_b, dst_b: lambda eng:
                             eng.scalar_tensor_tensor(
                                 out=dst_b[:], in0=x_halo[:, k:k + 512],
                                 scalar=convw[:, ct * 4 + k: ct * 4 + k + 1],
                                 in1=src_b[:], op0=ALU.mult, op1=ALU.add))(
                                     ct, k, src_b, dst_b),
                            r=[x_halo, convw, src_b], w=[dst_b],
                            tag=f"cv{k}{ps}_{gch}_{ct}")
                    p.op("ACT", (lambda ct: lambda eng: eng.activation(
                        out=u_bf[:, ct * 512:(ct + 1) * 512], in_=conv_g[:],
                        func=AF.Silu, bias=colsb[:, ct * 4: ct * 4 + 1]))(ct),
                        r=[conv_g, colsb], w=[u_bf], tag=f"silu{ps}_{gch}_{ct}")
                    pairs = [(wrb[:, ci * 128:(ci + 1) * 128],
                              zn_fm[:, ci * 512:(ci + 1) * 512])
                             for ci in range(8)]
                    p.mm(pX[:], pairs, r=[wrb, zn_fm], w=[pX],
                         tag=f"res{ps}_{gch}_{ct}")
                    p.op("ACT", (lambda ct: lambda eng: eng.activation(
                        out=res_bf[:, ct * 512:(ct + 1) * 512], in_=pX[:],
                        func=AF.Silu))(ct),
                        r=[pX], w=[res_bf], tag=f"rese{ps}_{gch}_{ct}")

                tap3("d_znfm", zn_fm, ps == 0 and gch == 0)
                tap3("d_u", u_bf, ps == 0 and gch == 0)
                tap3("d_res", res_bf, ps == 0 and gch == 0)
                p.mm(pD[:], [(wx_dt[:, ct2 * 64:(ct2 + 1) * 64],
                              u_bf[:, ct2 * 512:(ct2 + 1) * 512])
                             for ct2 in range(CT)],
                     r=[wx_dt, u_bf], w=[pD], tag=f"dt{ps}_{gch}")
                p.op("ACT", lambda eng: eng.activation(
                    out=dt_bf[:], in_=pD[:], func=AF.Copy),
                    r=[pD], w=[dt_bf], tag=f"dte{ps}_{gch}")
                p.mm(pBB[0:16, :], [(wx_B[:, ct2 * 16:(ct2 + 1) * 16],
                                     u_bf[:, ct2 * 512:(ct2 + 1) * 512])
                                    for ct2 in range(CT)],
                     r=[wx_B, u_bf], w=[pBB], tag=f"bm{ps}_{gch}")
                p.op("DVE", lambda eng: eng.tensor_copy(B_sb[:], pBB[0:16, :]),
                     r=[pBB], w=[B_sb], tag=f"be{ps}_{gch}")
                p.mm(pBB[0:16, :], [(wx_C[:, ct2 * 16:(ct2 + 1) * 16],
                                     u_bf[:, ct2 * 512:(ct2 + 1) * 512])
                                    for ct2 in range(CT)],
                     r=[wx_C, u_bf], w=[pBB], tag=f"cm{ps}_{gch}")
                p.op("DVE", lambda eng: eng.tensor_copy(C_sb[:], pBB[0:16, :]),
                     r=[pBB], w=[C_sb], tag=f"ce{ps}_{gch}")

                for ct in range(CT):
                    p.mm(pX[:], [(w_dt[:, ct * 128:(ct + 1) * 128], dt_bf[:])],
                         r=[w_dt, dt_bf], w=[pX], tag=f"dl{ps}_{gch}_{ct}")
                    p.op("DVE", (lambda ct: lambda eng: eng.tensor_scalar(
                        out=sp_t0[:], in0=pX[:],
                        scalar1=colsb[:, ct * 4 + 1: ct * 4 + 2], scalar2=None,
                        op0=ALU.add))(ct),
                        r=[pX, colsb], w=[sp_t0], tag=f"sp0{ps}_{gch}_{ct}")
                    p.op("ACT", lambda eng: eng.activation(
                        out=sp_a[:], in_=sp_t0[:], func=AF.Abs),
                        r=[sp_t0], w=[sp_a], tag=f"sp1{ps}_{gch}_{ct}")
                    p.op("ACT", lambda eng: eng.activation(
                        out=sp_a[:], in_=sp_a[:], func=AF.Exp, scale=-1.0),
                        r=[sp_a], w=[sp_a], tag=f"sp2{ps}_{gch}_{ct}")
                    p.op("ACT", lambda eng: eng.activation(
                        out=sp_a[:], in_=sp_a[:], func=AF.Ln, bias=1.0),
                        r=[sp_a], w=[sp_a], tag=f"sp3{ps}_{gch}_{ct}")
                    p.op("DVE", lambda eng: eng.tensor_scalar(
                        out=sp_t0[:], in0=sp_t0[:], scalar1=0.0, scalar2=None,
                        op0=ALU.max), r=[sp_t0], w=[sp_t0],
                        tag=f"sp4{ps}_{gch}_{ct}")
                    p.op("DVE", (lambda ct: lambda eng: eng.tensor_tensor(
                        out=d_bf[:, ct * 512:(ct + 1) * 512], in0=sp_t0[:],
                        in1=sp_a[:], op=ALU.add))(ct),
                        r=[sp_t0, sp_a], w=[d_bf], tag=f"sp5{ps}_{gch}_{ct}")
                    p.op("DVE", (lambda ct: lambda eng: eng.tensor_tensor(
                        out=du_bf[:, ct * 512:(ct + 1) * 512],
                        in0=d_bf[:, ct * 512:(ct + 1) * 512],
                        in1=u_bf[:, ct * 512:(ct + 1) * 512], op=ALU.mult))(ct),
                        r=[d_bf, u_bf], w=[du_bf], tag=f"du{ps}_{gch}_{ct}")

                tap3("d_dl", d_bf, ps == 0 and gch == 0)
                tap3("d_B", B_sb, ps == 0 and gch == 0)
                tap3("d_C", C_sb, ps == 0 and gch == 0)
                hs_v = hstate.ap().rearrange("p (ct s) -> p ct s", s=16)
                du_v = du_bf.ap().rearrange("p (c t) -> p c t", t=512)
                y_v = y_bf.ap().rearrange("p (c t) -> p c t", t=512)
                for j in range(S):
                    p.mm(pBB[:], [(c_esel[:, j * 128:(j + 1) * 128], B_sb[:])],
                         r=[c_esel, B_sb], w=[pBB], tag=f"bb{ps}_{gch}_{j}")
                    p.op("ACT", lambda eng: eng.activation(
                        out=B_bc[:], in_=pBB[:], func=AF.Copy),
                        r=[pBB], w=[B_bc], tag=f"bbe{ps}_{gch}_{j}")
                    p.mm(pBB[:], [(c_esel[:, j * 128:(j + 1) * 128], C_sb[:])],
                         r=[c_esel, C_sb], w=[pBB], tag=f"cb{ps}_{gch}_{j}")
                    p.op("ACT", lambda eng: eng.activation(
                        out=C_bc[:], in_=pBB[:], func=AF.Copy),
                        r=[pBB], w=[C_bc], tag=f"cbe{ps}_{gch}_{j}")
                    Cbc = C_bc.ap().unsqueeze(1).to_broadcast([128, 4, 512])
                    for g in range(4):
                        par = (j * 4 + g) % 2
                        dT = d_t[par]
                        hh4 = h4[par]
                        for k in range(4):
                            ct = 4 * g + k
                            p.op("ACT", (lambda ct, j, k, dT: lambda eng:
                                 eng.activation(
                                     out=dT[k][:],
                                     in_=d_bf[:, ct * 512:(ct + 1) * 512],
                                     func=AF.Exp,
                                     scale=A_sb[:, ct * 16 + j: ct * 16 + j + 1]))(
                                         ct, j, k, dT),
                                r=[d_bf, A_sb], w=[dT[k]],
                                tag=f"dc{ps}_{gch}_{j}_{4*g+k}")
                            p.op("DVE", (lambda ct, k: lambda eng:
                                 eng.tensor_tensor(
                                     out=b_t[k][:],
                                     in0=du_bf[:, ct * 512:(ct + 1) * 512],
                                     in1=B_bc[:], op=ALU.mult))(ct, k),
                                r=[du_bf, B_bc], w=[b_t[k]],
                                tag=f"bt{ps}_{gch}_{j}_{4*g+k}")
                        for k in range(4):
                            ct = 4 * g + k
                            p.op("DVE", (lambda ct, j, k, dT, hh4: lambda eng:
                                 eng.tensor_tensor_scan(
                                     out=hh4[:, k * 512:(k + 1) * 512],
                                     data0=dT[k][:],
                                     data1=b_t[k][:],
                                     initial=hstate[:, ct * 16 + j: ct * 16 + j + 1],
                                     op0=ALU.mult, op1=ALU.add))(ct, j, k, dT, hh4),
                                r=[dT[k], b_t[k], hstate], w=[hh4],
                                tag=f"sc{ps}_{gch}_{j}_{ct}")
                        p.op("DVE", (lambda g, j, hh4: lambda eng: eng.tensor_copy(
                            hs_v[:, 4 * g:4 * g + 4, j:j + 1],
                            hh4.ap().rearrange("p (k t) -> p k t", t=512)
                            [:, :, 511:512]))(g, j, hh4),
                            r=[hh4], w=[hstate], tag=f"hs{ps}_{gch}_{j}_{g}")
                        if j == 0:
                            p.op("DVE", (lambda g, hh4, Cbc: lambda eng:
                                 eng.tensor_tensor(
                                     out=y_v[:, 4 * g:4 * g + 4, :],
                                     in0=hh4.ap().rearrange(
                                         "p (k t) -> p k t", k=4),
                                     in1=Cbc, op=ALU.mult))(g, hh4, Cbc),
                                r=[hh4, C_bc], w=[y_bf],
                                tag=f"y0{ps}_{gch}_{j}_{g}")
                        else:
                            p.op("DVE", (lambda g, hh4, Cbc: lambda eng:
                                 eng.tensor_tensor(
                                     out=m4.ap().rearrange(
                                         "p (k t) -> p k t", k=4),
                                     in0=hh4.ap().rearrange(
                                         "p (k t) -> p k t", k=4),
                                     in1=Cbc, op=ALU.mult))(g, hh4, Cbc),
                                r=[hh4, C_bc], w=[m4],
                                tag=f"mt{ps}_{gch}_{j}_{g}")
                            p.op("DVE", (lambda g: lambda eng: eng.tensor_tensor(
                                out=y_bf[:, g * 2048:(g + 1) * 2048],
                                in0=y_bf[:, g * 2048:(g + 1) * 2048],
                                in1=m4[:], op=ALU.add))(g),
                                r=[y_bf, m4], w=[y_bf],
                                tag=f"ya{ps}_{gch}_{j}_{g}")

                tap3("d_y", y_bf, ps == 0 and gch == 0)
                for ct in range(CT):
                    k = ct % 4
                    yd4 = yd4L[(ct // 4) % 2]
                    p.op("DVE", (lambda ct, k, yd4: lambda eng: eng.tensor_scalar(
                        out=yd4[:, k * 512:(k + 1) * 512],
                        in0=u_bf[:, ct * 512:(ct + 1) * 512],
                        scalar1=colsb[:, ct * 4 + 2: ct * 4 + 3], scalar2=None,
                        op0=ALU.mult))(ct, k, yd4),
                        r=[u_bf, colsb], w=[yd4], tag=f"yd{ps}_{gch}_{ct}")
                    if k == 3:
                        g = ct // 4
                        p.op("DVE", (lambda g, yd4: lambda eng: eng.tensor_tensor(
                            out=yd4[:], in0=y_bf[:, g * 2048:(g + 1) * 2048],
                            in1=yd4[:], op=ALU.add))(g, yd4),
                            r=[y_bf, yd4], w=[yd4], tag=f"yu{ps}_{gch}_{g}")
                        p.op("DVE", (lambda g, yd4: lambda eng: eng.tensor_tensor(
                            out=du_bf[:, g * 2048:(g + 1) * 2048], in0=yd4[:],
                            in1=res_bf[:, g * 2048:(g + 1) * 2048],
                            op=ALU.mult))(g, yd4),
                            r=[yd4, res_bf], w=[du_bf], tag=f"yf{ps}_{gch}_{g}")

                tap3("d_yfin", du_bf, ps == 0 and gch == 0)
                for half in range(2):
                    for ct in range(CT):
                        wst = wo_st[ct % 2]
                        p.dma(wst[:], wouts.ap()[ct, :, half * 512:(half + 1) * 512],
                              r=[wouts], w=[wst], tag=f"wosl{ps}_{gch}_{half}_{ct}")
                        def wmf(eng, ct=ct, wst=wst):
                            instr = None
                            for su in range(4):
                                instr = eng.matmul(
                                    pZ[:, su * 512:(su + 1) * 512],
                                    lhsT=du_bf[:, ct * 512 + su * 128:
                                               ct * 512 + su * 128 + 128],
                                    rhs=wst[:], start=(ct == 0), stop=(ct == CT - 1))
                            return instr
                        p.op("TE", wmf, r=[du_bf, wo_st[ct % 2]], w=[pZ],
                             tag=f"wom{ps}_{gch}_{half}_{ct}")
                    for su in range(4):
                        s = gch * 4 + su
                        p.op("DVE", (lambda su, half: lambda eng: eng.tensor_copy(
                            zout_t[:, half * 512:(half + 1) * 512],
                            pZ[:, su * 512:(su + 1) * 512]))(su, half),
                            r=[pZ], w=[zout_t], tag=f"zo{ps}_{gch}_{half}_{su}")
                        p.dma(zov[s][:, half * 512:(half + 1) * 512],
                              zout_t[:, half * 512:(half + 1) * 512],
                              r=[zout_t], w=[zout_v[s // 4]],
                              engine="POOL", tag=f"zst{ps}_{gch}_{half}_{su}")
        zin, zout = zout, zin
        zin_v, zout_v = zout_v, zin_v

    zfv = zin.ap().rearrange("(n p) d -> n p d", p=128)
    for s in range(NSUB):
        p.dma(zc[:, 0:1024], zfv[s], r=[zin_v[s // 4]], w=[zc], tag=f"p9ld{s}")
        p.dma(zc[:, 1024:2048], xv[s], r=[x_in], w=[zc], tag=f"p9lx{s}")
        ln_stats(zc, zc[:, 0:1024], 1e-5, f"p9ln{s}")
        p.op("ACT", lambda eng: eng.activation(
            out=zout_t[:], in_=zc[:, 0:1024], func=AF.Identity, scale=rstd[:],
            bias=negm[:]), r=[zc, rstd, negm], w=[zout_t], tag=f"p9a{s}")
        p.op("DVE", lambda eng: eng.tensor_tensor(
            out=zout_t[:], in0=zout_t[:], in1=zc[:, 1024:2048], op=ALU.add),
            r=[zout_t, zc], w=[zout_t], tag=f"p9r{s}")
        od = p.dma(ov[s], zout_t[:], r=[zout_t], w=[x_out], engine="POOL",
                   tag=f"p9st{s}")
        out_dmas.append(od)

    return out_dmas


def _prep_inputs_l3(inputs, core):
    c = {}
    c["c_ident_f32"] = np.eye(128, dtype=np.float32)
    c["c_ones_row"] = np.ones((1, 128), dtype=np.float32)
    esel = np.zeros((16, 2048), np.float32)
    for j in range(16):
        esel[j, j * 128:(j + 1) * 128] = 1.0
    c["c_esel"] = esel
    W_x = inputs["L1_attn_W_x"].astype(np.float32)
    conv_w = inputs["L1_attn_conv_w"].astype(np.float32)[:, 0, :]
    def colmat(v, k):
        return np.ascontiguousarray(
            v.reshape(16, 128, k).transpose(1, 0, 2).reshape(128, 16 * k))
    cols = np.zeros((2048, 4), np.float32)
    cols[:, 0] = inputs["L1_attn_conv_b"]
    cols[:, 1] = inputs["L1_attn_b_dt"]
    cols[:, 2] = inputs["L1_attn_Dp"]
    A = -np.exp(inputs["L1_attn_A_log"].astype(np.float32))
    m = {
        "x_in": np.ascontiguousarray(inputs["x_prev"][2 * core:2 * core + 2]),
        "mb_W_in": inputs["L1_attn_W_in"].astype(np.float32),
        "mb_W_out": inputs["L1_attn_W_out"].astype(np.float32),
        "mb_Wx_dt": np.ascontiguousarray(W_x[:, 0:64]),
        "mb_Wx_B": np.ascontiguousarray(W_x[:, 64:80]),
        "mb_Wx_C": np.ascontiguousarray(W_x[:, 80:96]),
        "mb_W_dt": inputs["L1_attn_W_dt"].astype(np.float32),
        "mb_conv_w": colmat(conv_w, 4),
        "mb_cols": colmat(cols, 4),
        "mb_A": colmat(A, 16),
    }
    m.update(c)
    return m



def emit_launch4(p, x_in, x_out):
    NR = T // 128
    E, SL, HH = 16, 16, 256
    xv = x_in.ap().rearrange("b (n p) d -> (b n) p d", p=128)
    ov = x_out.ap().rearrange("b (n p) d -> (b n) p d", p=128)

    ident32_h = p.dram_in("c_ident_f32", (128, 128), F32)
    ones_row_h = p.dram_in("c_ones_row", (1, 128), F32)
    ones_col_h = p.dram_in("c_ones_col", (128, 1), F32)
    c_id32 = p.sbuf("id32", [128, 128], F32)
    c_idbf = p.sbuf("idbf", [128, 128], BF16)
    c_ones_row = p.sbuf("ones_row", [1, 128], F32)
    c_ones_col = p.sbuf("ones_col", [128, 1], F32)
    p.dma(c_id32[:], ident32_h.ap(), r=[ident32_h], w=[c_id32])
    p.op("DVE", lambda eng: eng.tensor_copy(c_idbf[:], c_id32[:]),
         r=[c_id32], w=[c_idbf], tag="idbf")
    p.dma(c_ones_row[:], ones_row_h.ap(), r=[ones_row_h], w=[c_ones_row])
    p.dma(c_ones_col[:], ones_col_h.ap(), r=[ones_col_h], w=[c_ones_col])

    mu_h = p.dram_in("moe_mu", (1024, 256), F32)
    gb_h = p.dram_in("moe_gb", (2, 1024), F32)
    sc_h = p.dram_in("moe_scale", (1, 1), F32)
    w1_h = p.dram_in("moe_w1", (E, 1024, 256), F32)
    w2_h = p.dram_in("moe_w2", (E, 256, 1024), F32)
    b1_h = p.dram_in("moe_b1cols", (128, 32), F32)
    b2_h = p.dram_in("moe_b2", (16, 1024), F32)

    mu_bf = p.sbuf("mu_bf", [128, 8 * 256], BF16)
    mu_f = p.sbuf("mu_f", [128, 8 * 256], F32)
    g_b = p.sbuf("g_b", [128, 1024], F32)
    b_b = p.sbuf("b_b", [128, 1024], F32)
    sc_col = p.sbuf("sc_col", [128, 1], F32)
    b1c = p.sbuf("b1c", [128, 32], F32)
    b2_bf = p.sbuf("b2_bf", [16, 1024], BF16)
    w1s = [p.sbuf(f"w1s{i}", [128, 8 * 256], BF16) for i in range(2)]
    w2s = [p.sbuf(f"w2s{i}", [128, 2 * 1024], BF16) for i in range(2)]
    stg = p.sbuf("stg", [128, 2304], F32)

    xn_seq = p.sbuf("xn_seq", [128, NR * 1024], BF16)
    ltm = p.sbuf("ltm", [128, NR * 256], F32)
    lesm = p.sbuf("lesm", [128, 2 * T], F32)
    disp = p.sbuf("disp", [128, 2 * T], BF16)
    disp_tm = p.sbuf("disp_tm", [128, NR * 256], BF16)
    comb_esm = p.sbuf("comb_esm", [128, 2 * T], BF16)
    comb_tm = p.sbuf("comb_tm", [128, NR * 256], BF16)
    si_dm = p.sbuf("si_dm", [128, 8 * 256], BF16)
    so_esm = p.sbuf("so_esm", [128, 2 * 1024], BF16)
    so_e = p.sbuf("so_e", [16, 1024], BF16)
    h_e = p.sbuf("h_e", [128, 2 * 16], BF16)
    cb_fm = p.sbuf("cb_fm", [16, 128], BF16)
    cbt = p.sbuf("cbt", [128, 16], F32)
    xt = p.sbuf("xt", [128, 1024], F32)
    xn_t = p.sbuf("xn_t", [128, 1024], F32)
    xq_bf = p.sbuf("xq_bf", [128, 1024], BF16)
    ffo = p.sbuf("ffo", [128, 1024], F32)
    zo = p.sbuf("zo", [128, 1024], F32)
    scr_bf = p.sbuf("scr_bf", [128, 1024], BF16)
    colA = p.sbuf("colA", [128, 8], F32)
    rstd = p.sbuf("rstd", [128, 1], F32)
    negm = p.sbuf("negm", [128, 1], F32)
    nmx = p.sbuf("nmx", [128, 1], F32)
    den = p.sbuf("den", [128, 1], F32)
    rden = p.sbuf("rden", [128, 1], F32)
    comb_t = p.sbuf("comb_t", [128, 256], BF16)

    pL = p.psum("pL", [128, 512], F32)
    pT = p.psum("pT", [128, 512], BF16)
    pF = p.psum("pF", [128, 512], F32)
    pH = p.psum("pH", [128, 2 * 16], F32)
    pSO = p.psum("pSO", [16, 1024], F32)
    pZ4 = p.psum("pZ4", [128, 1024], F32)

    out_dmas = []

    def ln_stats(src_buf, src_ap, eps, tag):
        p.op("DVE", (lambda src_ap: lambda eng: eng.tensor_reduce(
            out=colA[:, 0:1], in_=src_ap, axis=AX.X, op=ALU.add,
            negate=True))(src_ap), r=[src_buf], w=[colA], tag=tag + "_s")
        p.op("ACT", (lambda src_ap: lambda eng: eng.activation(
            out=scr_bf[:], in_=src_ap, func=AF.Square,
            accum_out=colA[:, 1:2]))(src_ap),
            r=[src_buf], w=[scr_bf, colA], tag=tag + "_q")
        p.op("DVE", lambda eng: ts_op(eng, negm[:], colA[:, 0:1], 1.0 / D, None,
                                      ALU.mult), r=[colA], w=[negm], tag=tag)
        def var1(eng):
            eng.tensor_scalar(out=colA[:, 2:3], in0=negm[:], scalar1=negm[:],
                              scalar2=None, op0=ALU.mult)
            return eng.tensor_scalar(out=colA[:, 3:4], in0=colA[:, 1:2],
                                     scalar1=1.0 / D, scalar2=eps,
                                     op0=ALU.mult, op1=ALU.add)
        p.op("DVE", var1, r=[colA, negm], w=[colA], tag=tag + "_v1")
        p.op("DVE", lambda eng: eng.tensor_tensor(
            out=colA[:, 4:5], in0=colA[:, 3:4], in1=colA[:, 2:3],
            op=ALU.subtract), r=[colA], w=[colA], tag=tag + "_v2")
        p.op("ACT", lambda eng: eng.activation(out=colA[:, 5:6],
                                               in_=colA[:, 4:5], func=AF.Sqrt),
             r=[colA], w=[colA], tag=tag + "_sd")
        p.op("DVE", lambda eng: eng.reciprocal(out=rstd[:], in_=colA[:, 5:6]),
             r=[colA], w=[rstd], tag=tag + "_r")
        p.op("DVE", lambda eng: eng.tensor_tensor(out=negm[:], in0=negm[:],
                                                  in1=rstd[:], op=ALU.mult),
             r=[rstd, negm, colA], w=[negm], tag=tag + "_n")

    muv = mu_h.ap().rearrange("(n p) c -> n p c", p=128)
    for ci in range(8):
        p.dma(stg[:, 0:256], muv[ci], r=[mu_h], w=[stg], tag=f"mul{ci}")
        p.op("DVE", (lambda ci: lambda eng: eng.tensor_copy(
            mu_f[:, ci * 256:(ci + 1) * 256], stg[:, 0:256]))(ci),
            r=[stg], w=[mu_f], tag=f"muc{ci}")
    for ci in range(8):
        p.op("ACT", (lambda ci: lambda eng: eng.activation(
            out=scr_bf[:, 0:256], in_=mu_f[:, ci * 256:(ci + 1) * 256],
            func=AF.Square))(ci), r=[mu_f], w=[scr_bf], tag=f"musq{ci}")
        p.op("DVE", (lambda ci: lambda eng: eng.tensor_copy(
            stg[:, 256 + ci * 256: 256 + (ci + 1) * 256],
            scr_bf[:, 0:256]))(ci), r=[scr_bf], w=[stg], tag=f"musqc{ci}")
    p.mm(pL[0:1, 0:256], [(c_ones_col[:], stg[:, 256 + ci * 256:
                                              256 + (ci + 1) * 256])
                          for ci in range(8)],
         r=[c_ones_col, stg], w=[pL], tag="muss")
    p.op("ACT", lambda eng: eng.activation(out=stg[0:1, 0:256],
                                           in_=pL[0:1, 0:256], func=AF.Sqrt),
         r=[pL], w=[stg], tag="musd")
    p.op("DVE", lambda eng: ts_op(eng, stg[0:1, 512:768], stg[0:1, 0:256],
                                  1e-12, None, ALU.max),
         r=[stg], w=[stg], tag="mumx")
    p.op("DVE", lambda eng: eng.reciprocal(out=stg[0:1, 0:256],
                                           in_=stg[0:1, 512:768]),
         r=[stg], w=[stg], tag="murc")
    p.mm(pL[:, 0:256], [(c_ones_row[0:1, :], stg[0:1, 0:256])],
         r=[c_ones_row, stg], w=[pL], tag="murb")
    p.op("DVE", lambda eng: eng.tensor_copy(stg[:, 1024:1280], pL[:, 0:256]),
         r=[pL], w=[stg], tag="murbe")
    for ci in range(8):
        p.op("DVE", (lambda ci: lambda eng: eng.tensor_tensor(
            out=mu_bf[:, ci * 256:(ci + 1) * 256],
            in0=mu_f[:, ci * 256:(ci + 1) * 256],
            in1=stg[:, 1024:1280], op=ALU.mult))(ci),
            r=[mu_f, stg], w=[mu_bf], tag=f"muu{ci}")
    p.dma(stg[0:1, 0:2048], gb_h.ap().rearrange("a b -> (a b)").unsqueeze(0),
          r=[gb_h], w=[stg], tag="gbl")
    for (dst, off, tg) in ((g_b, 0, "gb"), (b_b, 1024, "bb")):
        for si in range(2):
            p.mm(pL[:], [(c_ones_row[0:1, :],
                          stg[0:1, off + si * 512: off + (si + 1) * 512])],
                 r=[c_ones_row, stg], w=[pL], tag=tg + f"m{si}")
            p.op("DVE", (lambda dst, si: lambda eng: eng.tensor_copy(
                dst[:, si * 512:(si + 1) * 512], pL[:]))(dst, si),
                r=[pL], w=[dst], tag=tg + f"e{si}")
    p.dma(stg[0:1, 0:1], sc_h.ap(), r=[sc_h], w=[stg], tag="scl")
    p.mm(pL[:, 0:1], [(c_ones_row[0:1, :], stg[0:1, 0:1])],
         r=[c_ones_row, stg], w=[pL], tag="scm")
    p.op("DVE", lambda eng: eng.tensor_copy(sc_col[:], pL[:, 0:1]),
         r=[pL], w=[sc_col], tag="sce")
    p.dma(b1c[:], b1_h.ap(), r=[b1_h], w=[b1c], tag="b1l")
    p.dma(stg[0:16, 0:1024], b2_h.ap(), r=[b2_h], w=[stg], tag="b2l")
    p.op("DVE", lambda eng: eng.tensor_copy(b2_bf[:], stg[0:16, 0:1024]),
         r=[stg], w=[b2_bf], tag="b2c")

    for seq in range(BPC):
        for su in range(NR):
            s = seq * NR + su
            p.dma(xt[:], xv[s], r=[x_in], w=[xt], tag=f"xl{seq}_{su}")
            ln_stats(xt, xt[:], 1e-5, f"ln{seq}_{su}")
            p.op("ACT", lambda eng: eng.activation(
                out=xn_t[:], in_=xt[:], func=AF.Identity, scale=rstd[:],
                bias=negm[:]), r=[xt, rstd, negm], w=[xn_t], tag=f"xa{seq}_{su}")
            p.op("DVE", lambda eng: eng.tensor_tensor(
                out=xn_t[:], in0=xn_t[:], in1=g_b[:], op=ALU.mult),
                r=[xn_t, g_b], w=[xn_t], tag=f"xg{seq}_{su}")
            p.op("DVE", lambda eng: eng.tensor_tensor(
                out=xn_t[:], in0=xn_t[:], in1=b_b[:], op=ALU.add),
                r=[xn_t, b_b], w=[xn_t], tag=f"xb{seq}_{su}")
            p.op("DVE", (lambda su: lambda eng: eng.tensor_copy(
                xn_seq[:, su * 1024:(su + 1) * 1024], xn_t[:]))(su),
                r=[xn_t], w=[xn_seq], tag=f"xc{seq}_{su}")
            p.op("ACT", lambda eng: eng.activation(
                out=scr_bf[:], in_=xn_t[:], func=AF.Square,
                accum_out=colA[:, 6:7]), r=[xn_t], w=[scr_bf, colA],
                tag=f"q2{seq}_{su}")
            p.op("ACT", lambda eng: eng.activation(
                out=colA[:, 7:8], in_=colA[:, 6:7], func=AF.Sqrt),
                r=[colA], w=[colA], tag=f"q2s{seq}_{su}")
            p.op("DVE", lambda eng: ts_op(eng, colA[:, 6:7], colA[:, 7:8],
                                          1e-12, None, ALU.max),
                 r=[colA], w=[colA], tag=f"q2m{seq}_{su}")
            p.op("DVE", lambda eng: eng.reciprocal(out=colA[:, 7:8],
                                                   in_=colA[:, 6:7]),
                 r=[colA], w=[colA], tag=f"q2r{seq}_{su}")
            p.op("DVE", lambda eng: eng.tensor_scalar(
                out=colA[:, 6:7], in0=colA[:, 7:8], scalar1=sc_col[:],
                scalar2=None, op0=ALU.mult), r=[colA, sc_col], w=[colA],
                tag=f"q2c{seq}_{su}")
            p.op("DVE", lambda eng: eng.tensor_scalar(
                out=xq_bf[:], in0=xn_t[:], scalar1=colA[:, 6:7], scalar2=None,
                op0=ALU.mult), r=[xn_t, colA], w=[xq_bf], tag=f"q2a{seq}_{su}")
            for g in range(2):
                def trf(eng, g=g):
                    instr = None
                    for i in range(4):
                        ci = g * 4 + i
                        instr = eng.transpose(
                            pT[:, i * 128:(i + 1) * 128],
                            xq_bf[:, ci * 128:(ci + 1) * 128], c_idbf[:])
                    return instr
                p.op("TE", trf, r=[xq_bf, c_idbf], w=[pT], tag=f"xtr{seq}_{su}{g}")
                p.op("ACT", (lambda g: lambda eng: eng.activation(
                    out=scr_bf[:, g * 512:(g + 1) * 512], in_=pT[:],
                    func=AF.Copy))(g), r=[pT], w=[scr_bf], tag=f"xte{seq}_{su}{g}")
            p.mm(pL[:, 0:256], [(scr_bf[:, ci * 128:(ci + 1) * 128],
                                 mu_bf[:, ci * 256:(ci + 1) * 256])
                                for ci in range(8)],
                 r=[scr_bf, mu_bf], w=[pL], tag=f"lg{seq}_{su}")
            p.op("DVE", (lambda su: lambda eng: eng.tensor_copy(
                ltm[:, su * 256:(su + 1) * 256], pL[:, 0:256]))(su),
                r=[pL], w=[ltm], tag=f"lge{seq}_{su}")
            p.op("DVE", lambda eng: eng.tensor_reduce(
                out=nmx[:], in_=pL[:, 0:256], axis=AX.X, op=ALU.max,
                negate=True), r=[pL], w=[nmx], tag=f"cmx{seq}_{su}")
            p.op("ACT", lambda eng: eng.activation(
                out=comb_t[:], in_=pL[:, 0:256], func=AF.Exp, bias=nmx[:],
                accum_out=den[:]), r=[pL, nmx], w=[comb_t, den],
                tag=f"cex{seq}_{su}")
            p.op("DVE", lambda eng: eng.reciprocal(out=rden[:], in_=den[:]),
                 r=[den], w=[rden], tag=f"crd{seq}_{su}")
            p.op("DVE", lambda eng: eng.tensor_scalar(
                out=comb_t[:], in0=comb_t[:], scalar1=rden[:], scalar2=None,
                op0=ALU.mult), r=[comb_t, rden], w=[comb_t], tag=f"cn{seq}_{su}")
            p.op("DVE", (lambda su: lambda eng: eng.tensor_copy(
                comb_tm[:, su * 256:(su + 1) * 256], comb_t[:]))(su),
                r=[comb_t], w=[comb_tm], tag=f"cc{seq}_{su}")
            def trc(eng):
                eng.transpose(pT[:, 0:128], comb_t[:, 0:128], c_idbf[:])
                return eng.transpose(pT[:, 128:256], comb_t[:, 128:256],
                                     c_idbf[:])
            p.op("TE", trc, r=[comb_t, c_idbf], w=[pT], tag=f"ctr{seq}_{su}")
            def cev(eng, su=su):
                eng.activation(out=comb_esm[:, su * 128: su * 128 + 128],
                               in_=pT[:, 0:128], func=AF.Copy)
                return eng.activation(
                    out=comb_esm[:, T + su * 128: T + su * 128 + 128],
                    in_=pT[:, 128:256], func=AF.Copy)
            p.op("ACT", cev, r=[pT], w=[comb_esm], tag=f"cte{seq}_{su}")
            def trl(eng, su=su):
                eng.transpose(pF[:, 0:128], ltm[:, su * 256: su * 256 + 128],
                              c_id32[:])
                return eng.transpose(pF[:, 128:256],
                                     ltm[:, su * 256 + 128: su * 256 + 256],
                                     c_id32[:])
            p.op("TE", trl, r=[ltm, c_id32], w=[pF], tag=f"ltr{seq}_{su}")
            def lev(eng, su=su):
                eng.activation(out=lesm[:, su * 128: su * 128 + 128],
                               in_=pF[:, 0:128], func=AF.Copy)
                return eng.activation(
                    out=lesm[:, T + su * 128: T + su * 128 + 128],
                    in_=pF[:, 128:256], func=AF.Copy)
            p.op("ACT", lev, r=[pF], w=[lesm], tag=f"lte{seq}_{su}")

        for ec in range(2):
            sl_ = lambda b: b[:, ec * T:(ec + 1) * T]
            p.op("DVE", (lambda ec: lambda eng: eng.tensor_reduce(
                out=nmx[:], in_=lesm[:, ec * T:(ec + 1) * T], axis=AX.X,
                op=ALU.max, negate=True))(ec),
                r=[lesm], w=[nmx], tag=f"dmx{seq}_{ec}")
            p.op("ACT", (lambda ec: lambda eng: eng.activation(
                out=disp[:, ec * T:(ec + 1) * T],
                in_=lesm[:, ec * T:(ec + 1) * T], func=AF.Exp, bias=nmx[:],
                accum_out=den[:]))(ec),
                r=[lesm, nmx], w=[disp, den], tag=f"dex{seq}_{ec}")
            p.op("DVE", lambda eng: eng.reciprocal(out=rden[:], in_=den[:]),
                 r=[den], w=[rden], tag=f"drd{seq}_{ec}")
            p.op("DVE", (lambda ec: lambda eng: eng.tensor_scalar(
                out=disp[:, ec * T:(ec + 1) * T],
                in0=disp[:, ec * T:(ec + 1) * T], scalar1=rden[:],
                scalar2=None, op0=ALU.mult))(ec),
                r=[disp, rden], w=[disp], tag=f"dn{seq}_{ec}")
            for su in range(NR):
                def trd(eng, ec=ec, su=su):
                    return eng.transpose(
                        pT[:, 0:128],
                        disp[:, ec * T + su * 128: ec * T + su * 128 + 128],
                        c_idbf[:])
                p.op("TE", trd, r=[disp, c_idbf], w=[pT], tag=f"dtr{seq}_{ec}_{su}")
                p.op("ACT", (lambda ec, su: lambda eng: eng.activation(
                    out=disp_tm[:, su * 256 + ec * 128:
                                su * 256 + ec * 128 + 128],
                    in_=pT[:, 0:128], func=AF.Copy))(ec, su),
                    r=[pT], w=[disp_tm], tag=f"dte{seq}_{ec}_{su}")

        for dc in range(8):
            p.mm(pL[:, 0:256], [(xn_seq[:, su2 * 1024 + dc * 128:
                                        su2 * 1024 + dc * 128 + 128],
                                 disp_tm[:, su2 * 256:(su2 + 1) * 256])
                                for su2 in range(NR)],
                 r=[xn_seq, disp_tm], w=[pL], tag=f"si{seq}_{dc}")
            p.op("DVE", (lambda dc: lambda eng: eng.tensor_copy(
                si_dm[:, dc * 256:(dc + 1) * 256], pL[:, 0:256]))(dc),
                r=[pL], w=[si_dm], tag=f"sie{seq}_{dc}")

        for e in range(E):
            w1b = w1s[e % 2]
            w2b = w2s[e % 2]
            w1v = w1_h.ap()[e].rearrange("(n p) c -> n p c", p=128)
            for ci in range(8):
                p.dma(stg[:, 0:256], w1v[ci], r=[w1_h], w=[stg],
                      tag=f"w1l{seq}_{e}_{ci}")
                p.op("DVE", (lambda ci, w1b: lambda eng: eng.tensor_copy(
                    w1b[:, ci * 256:(ci + 1) * 256], stg[:, 0:256]))(ci, w1b),
                    r=[stg], w=[w1b], tag=f"w1c{seq}_{e}_{ci}")
            w2v = w2_h.ap()[e].rearrange("(n p) c -> n p c", p=128)
            for ci in range(2):
                p.dma(stg[:, 0:1024], w2v[ci], r=[w2_h], w=[stg],
                      tag=f"w2l{seq}_{e}_{ci}")
                p.op("DVE", (lambda ci, w2b: lambda eng: eng.tensor_copy(
                    w2b[:, ci * 1024:(ci + 1) * 1024], stg[:, 0:1024]))(ci, w2b),
                    r=[stg], w=[w2b], tag=f"w2c{seq}_{e}_{ci}")
            for hc in range(2):
                p.mm(pH[:, hc * 16:(hc + 1) * 16],
                     [(w1b[:, ci * 256 + hc * 128: ci * 256 + hc * 128 + 128],
                       si_dm[:, ci * 256 + e * 16: ci * 256 + e * 16 + 16])
                      for ci in range(8)],
                     r=[w1b, si_dm], w=[pH], tag=f"he{seq}_{e}_{hc}")
                p.op("ACT", (lambda e, hc: lambda eng: eng.activation(
                    out=h_e[:, hc * 16:(hc + 1) * 16],
                    in_=pH[:, hc * 16:(hc + 1) * 16], func=AF.Gelu,
                    bias=b1c[:, e * 2 + hc: e * 2 + hc + 1]))(e, hc),
                    r=[pH, b1c], w=[h_e], tag=f"hg{seq}_{e}_{hc}")
            for nh in range(2):
                p.mm(pSO[:, nh * 512:(nh + 1) * 512],
                     [(h_e[:, hc * 16:(hc + 1) * 16],
                       w2b[:, hc * 1024 + nh * 512: hc * 1024 + nh * 512 + 512])
                      for hc in range(2)],
                     r=[h_e, w2b], w=[pSO], tag=f"so{seq}_{e}_{nh}")
            p.op("DVE", lambda eng: eng.tensor_copy(so_e[:], pSO[:]),
                 r=[pSO], w=[so_e], tag=f"soe{seq}_{e}")
            p.dma(so_esm[(e % 8) * 16:(e % 8) * 16 + 16,
                         (e // 8) * 1024:(e // 8) * 1024 + 1024],
                  so_e[:], r=[so_e], w=[so_esm], engine="POOL",
                  tag=f"sod{seq}_{e}")

        for su in range(NR):
            s = seq * NR + su
            p.op("DVE", (lambda su: lambda eng: eng.tensor_reduce(
                out=cbt[:], in_=comb_tm.ap().rearrange(
                    "p (n e s) -> p n e s", n=NR, e=E)[:, su], axis=AX.X,
                op=ALU.add))(su), r=[comb_tm], w=[cbt], tag=f"cb{seq}_{su}")
            def trb(eng):
                return eng.transpose(pF[0:16, 0:128], cbt[:], c_id32[:])
            p.op("TE", trb, r=[cbt, c_id32], w=[pF], tag=f"cbt{seq}_{su}")
            p.op("DVE", lambda eng: eng.tensor_copy(cb_fm[:], pF[0:16, 0:128]),
                 r=[pF], w=[cb_fm], tag=f"cbe{seq}_{su}")
            def fmm(eng, su=su):
                instr = None
                for nh in range(2):
                    for ec in range(2):
                        instr = eng.matmul(
                            pZ4[:, nh * 512:(nh + 1) * 512],
                            lhsT=comb_esm[:, ec * T + su * 128:
                                          ec * T + su * 128 + 128],
                            rhs=so_esm[:, ec * 1024 + nh * 512:
                                       ec * 1024 + nh * 512 + 512],
                            start=(ec == 0), stop=False)
                    instr = eng.matmul(
                        pZ4[:, nh * 512:(nh + 1) * 512],
                        lhsT=cb_fm[:], rhs=b2_bf[:, nh * 512:(nh + 1) * 512],
                        start=False, stop=True)
                return instr
            p.op("TE", fmm, r=[comb_esm, so_esm, cb_fm, b2_bf], w=[pZ4],
                 tag=f"fm{seq}_{su}")
            p.op("DVE", lambda eng: eng.tensor_copy(ffo[:], pZ4[:]),
                 r=[pZ4], w=[ffo], tag=f"fe{seq}_{su}")
            p.dma(xt[:], xv[s], r=[x_in], w=[xt], tag=f"x2l{seq}_{su}")
            ln_stats(ffo, ffo[:], 1e-5, f"fln{seq}_{su}")
            p.op("ACT", lambda eng: eng.activation(
                out=zo[:], in_=ffo[:], func=AF.Identity, scale=rstd[:],
                bias=negm[:]), r=[ffo, rstd, negm], w=[zo], tag=f"fa{seq}_{su}")
            p.op("DVE", lambda eng: eng.tensor_tensor(
                out=zo[:], in0=zo[:], in1=xt[:], op=ALU.add),
                r=[zo, xt], w=[zo], tag=f"fr{seq}_{su}")
            od = p.dma(ov[s], zo[:], r=[zo], w=[x_out], engine="POOL",
                       tag=f"fst{seq}_{su}")
            out_dmas.append(od)

    return out_dmas


def _prep_inputs_l4(inputs, core):
    c = {}
    c["c_ident_f32"] = np.eye(128, dtype=np.float32)
    c["c_ones_row"] = np.ones((1, 128), dtype=np.float32)
    c["c_ones_col"] = np.ones((128, 1), dtype=np.float32)
    b1 = inputs["L1_ffn_b1"].astype(np.float32)
    b1c = np.zeros((128, 32), np.float32)
    for e in range(16):
        for hc in range(2):
            b1c[:, e * 2 + hc] = b1[e, hc * 128:(hc + 1) * 128]
    m = {
        "x_in": np.ascontiguousarray(inputs["x_prev"][2 * core:2 * core + 2]),
        "moe_mu": np.ascontiguousarray(
            inputs["L1_ffn_mu"].reshape(1024, 256).astype(np.float32)),
        "moe_gb": np.stack([inputs["L1_ffn_ln_gamma"],
                            inputs["L1_ffn_ln_beta"]]).astype(np.float32),
        "moe_scale": inputs["L1_ffn_scale"].reshape(1, 1).astype(np.float32),
        "moe_w1": inputs["L1_ffn_w1"].astype(np.float32),
        "moe_w2": inputs["L1_ffn_w2"].astype(np.float32),
        "moe_b1cols": b1c,
        "moe_b2": inputs["L1_ffn_b2"].astype(np.float32),
    }
    m.update(c)
    return m


# version 17
# speedup vs baseline: 1.1995x; 1.1995x over previous
import numpy as np
import ml_dtypes
from contextlib import ExitStack

import concourse.bass as bass
import concourse.mybir as mybir
from concourse.bass_utils import run_bass_kernel_spmd

F32 = mybir.dt.float32
BF16 = mybir.dt.bfloat16
ALU = mybir.AluOpType
AF = mybir.ActivationFunctionType
AX = mybir.AxisListType

NCORES = 8
BPC = 2
T = 2048
D = 1024
NT = BPC * T
NSUB = NT // 128



class Buf:
    __slots__ = ("h", "name", "last_w", "readers")

    def __init__(self, h, name):
        self.h = h
        self.name = name
        self.last_w = None
        self.readers = []

    def __getitem__(self, sl):
        return self.h[sl]

    def ap(self):
        return self.h.ap() if hasattr(self.h, "ap") else self.h


class Op:
    __slots__ = ("engine", "fn", "deps", "idx", "wanted", "is_dma", "rank",
                 "tag", "slot", "seq")

    def __init__(self, engine, fn, idx, is_dma, tag):
        self.engine = engine
        self.fn = fn
        self.deps = []
        self.idx = idx
        self.wanted = False
        self.is_dma = is_dma
        self.rank = None
        self.tag = tag
        self.slot = None
        self.seq = None


ENGINES = ("SYNC", "TE", "DVE", "ACT", "POOL")


class Prog:
    def __init__(self, name):
        self.name = name
        self.nc = bass.Bass("TRN2", target_bir_lowering=False)
        self.es = ExitStack()
        self.stacks = [self.es]
        self.ops = {e: [] for e in ENGINES}
        self._uid = 0
        self.NSEM = 16
        self._dma_rr = {"SYNC": 0, "POOL": 0}
        self._dma_seq = {("SYNC", i): 0 for i in range(self.NSEM)}
        self._dma_seq.update({("POOL", i): 0 for i in range(self.NSEM)})
        self._dram_cache = {}
        self._last_dma = {}

    from contextlib import contextmanager as _ctxm

    @_ctxm
    def scope(self):
        st = ExitStack()
        self.stacks.append(st)
        try:
            yield
        finally:
            self.stacks.pop()
            st.close()

    def barrier(self, tag="bar"):
        lasts = []
        for e in ENGINES:
            if self.ops[e]:
                o = self.ops[e][-1]
                if not o.is_dma:
                    lasts.append(o)
        lasts.extend(self._last_dma.values())
        for e in ENGINES:
            if not self.ops[e]:
                continue
            def fn(eng):
                return eng.nop()
            o = Op(e, fn, len(self.ops[e]), False, tag + e)
            o.deps = [d for d in lasts if d.engine != e or d.is_dma]
            self.ops[e].append(o)

    def _nm(self, base):
        self._uid += 1
        return f"{base}_{self._uid}"

    def sbuf(self, name, shape, dtype):
        h = self.stacks[-1].enter_context(
            self.nc.sbuf_tensor(self._nm(name), shape, dtype))
        return Buf(h, name)

    def psum(self, name, shape, dtype):
        h = self.stacks[-1].enter_context(
            self.nc.psum_tensor(self._nm(name), shape, dtype))
        return Buf(h, name)

    def dram_in(self, name, shape, dtype):
        if name in self._dram_cache:
            return self._dram_cache[name]
        h = self.nc.dram_tensor(name, list(shape), dtype, kind="ExternalInput")
        b = Buf(h, name)
        self._dram_cache[name] = b
        return b

    def dram_out(self, name, shape, dtype):
        h = self.nc.dram_tensor(name, list(shape), dtype, kind="ExternalOutput")
        return Buf(h, name)

    def dram_scratch(self, name, shape, dtype):
        h = self.nc.dram_tensor(self._nm(name), list(shape), dtype, kind="Internal")
        return Buf(h, name)

    def op(self, engine, fn, r=(), w=(), tag="", dma=False):
        o = Op(engine, fn, len(self.ops[engine]), dma, tag)
        if dma:
            o.slot = self._dma_rr[engine] % self.NSEM
            self._dma_rr[engine] += 1
            self._dma_seq[(engine, o.slot)] += 1
            o.seq = self._dma_seq[(engine, o.slot)]
            self._last_dma[(engine, o.slot)] = o
        deps = []
        for b in r:
            if b.last_w is not None:
                deps.append(b.last_w)
        for b in w:
            if b.last_w is not None:
                deps.append(b.last_w)
            deps.extend(b.readers)
        o.deps = deps
        self.ops[engine].append(o)
        for b in r:
            b.readers.append(o)
        for b in w:
            b.last_w = o
            b.readers = []
        return o

    def dma(self, out_ap, in_ap, r=(), w=(), engine="SYNC", tag="dma"):
        def fn(eng):
            return eng.dma_start(out=out_ap, in_=in_ap)
        return self.op(engine, fn, r=r, w=w, tag=tag, dma=True)

    def build(self):
        for e in ENGINES:
            for o in self.ops[e]:
                kept = []
                for d in o.deps:
                    if d.engine == "TE" and o.engine == "TE" and not d.is_dma:
                        continue
                    if (d.engine == o.engine and d.is_dma and o.is_dma
                            and d.slot == o.slot):
                        continue
                    kept.append(d)
                    d.wanted = True
                o.deps = kept
        for e in ENGINES:
            rk = 0
            for o in self.ops[e]:
                if o.wanted and not o.is_dma:
                    rk += 1
                    o.rank = rk
        sems = {}
        dsems = {}
        with ExitStack() as ses:
            for e in ENGINES:
                if any((not o.is_dma) for o in self.ops[e]):
                    sems[e] = ses.enter_context(self.nc.semaphore(f"sem_{e}"))
            for q in ("SYNC", "POOL"):
                if any(o.is_dma for o in self.ops[q]):
                    for i in range(self.NSEM):
                        dsems[(q, i)] = ses.enter_context(
                            self.nc.semaphore(f"dsem_{q}_{i}"))
            block = ses.enter_context(self.nc.Block())

            def section(eng_obj, e):
                waited = {}
                for o in self.ops[e]:
                    need = {}
                    for d in o.deps:
                        if d.is_dma:
                            key = (d.engine, d.slot)
                            sem, v = dsems[key], 16 * d.seq
                        else:
                            key = d.engine
                            sem, v = sems[key], d.rank
                        if need.get(key, (None, 0))[1] < v:
                            need[key] = (sem, v)
                    if o.is_dma and o.seq > 1:
                        key = (o.engine, o.slot)
                        v = 16 * (o.seq - 1)
                        if need.get(key, (None, 0))[1] < v:
                            need[key] = (dsems[key], v)
                    for key, (sem, v) in need.items():
                        if waited.get(key, 0) < v:
                            eng_obj.wait_ge(sem, v)
                            waited[key] = v
                    instr = o.fn(eng_obj)
                    if o.is_dma:
                        instr.then_inc(dsems[(o.engine, o.slot)], 16)
                    elif o.wanted:
                        if instr is None:
                            raise RuntimeError(f"op {o.tag} on {e} returned no instr")
                        instr.then_inc(sems[o.engine], 1)

            if self.ops["SYNC"]:
                block.sync(lambda eng: section(eng, "SYNC"))
            if self.ops["TE"]:
                block.tensor(lambda eng: section(eng, "TE"))
            if self.ops["DVE"]:
                block.vector(lambda eng: section(eng, "DVE"))
            if self.ops["ACT"]:
                block.scalar(lambda eng: section(eng, "ACT"))
            if self.ops["POOL"]:
                block.gpsimd(lambda eng: section(eng, "POOL"))
        self.es.close()
        return self.nc


    def barrier_outputs(self, out_dma_ops):
        def fn(eng):
            return eng.nop()
        o = self.op("SYNC", fn, tag="final_barrier")
        o.deps = list(out_dma_ops)
        for d in out_dma_ops:
            d.wanted = True

    def mm(self, psum_ap, pairs, r, w, tag="mm", extra_mms=None):
        def fn(eng):
            n = len(pairs)
            instr = None
            for i, (lh, rh) in enumerate(pairs):
                instr = eng.matmul(psum_ap, lhsT=lh, rhs=rh,
                                   start=(i == 0), stop=(i == n - 1))
            return instr
        return self.op("TE", fn, r=r, w=w, tag=tag)

    def transpose_to(self, psum_buf, psum_off, src_ap, ident_ap, r, w, n=1,
                     tag="tr"):
        raise NotImplementedError


def ts_op(eng, out, in0, s1, s2, op0, op1=ALU.bypass):
    return eng.tensor_scalar(out=out, in0=in0, scalar1=s1, scalar2=s2,
                             op0=op0, op1=op1)



def host_constants():
    c = {}
    ident = np.eye(128, dtype=np.float32)
    c["c_ident_bf"] = ident.astype(ml_dtypes.bfloat16)
    c["c_ident_f32"] = ident
    c["c_ut_f32"] = np.triu(np.ones((128, 128), dtype=np.float32))
    c["c_ones_row"] = np.ones((1, 128), dtype=np.float32)
    c["c_ones_col"] = np.ones((128, 1), dtype=np.float32)
    Dh = 64
    inv = 1.0 / 10000.0 ** (np.arange(0, Dh, 2, dtype=np.float32) / Dh)
    t = np.arange(T, dtype=np.float32)
    fr = t[:, None] * inv[None, :]
    emb = np.concatenate([fr, fr], axis=-1)
    c["c_rope_cos"] = np.cos(emb).astype(np.float32)
    c["c_rope_sin"] = np.sin(emb).astype(np.float32)
    return c



def w_chunked_bf16(p, name, rows, cols, stage_bufs, dst, st_idx):
    src = p.dram_in(name, (rows, cols), F32)
    nch = rows // 128
    rview = src.ap().rearrange("(n p) c -> n p c", p=128)
    for ci in range(nch):
        stg = stage_bufs[st_idx[0] % len(stage_bufs)]
        st_idx[0] += 1
        p.dma(stg[:, 0:cols], rview[ci], r=[src], w=[stg], tag=f"ld_{name}{ci}")
        cpy = (lambda stg, ci: lambda eng: eng.tensor_copy(
            dst[:, ci * cols:(ci + 1) * cols], stg[:, 0:cols]))(stg, ci)
        p.op("DVE", cpy, r=[stg], w=[dst], tag=f"cast_{name}{ci}")
    return dst


def bcast_rows(p, pm_buf, dst, src_ap, ones_row, ncols, r, tag="bcast"):
    nspl = (ncols + 511) // 512
    for si in range(nspl):
        c0, c1 = si * 512, min(ncols, (si + 1) * 512)
        w = c1 - c0
        p.mm(pm_buf[:, 0:w], [(ones_row[0:1, :], src_ap[0:1, c0:c1])],
             r=list(r), w=[pm_buf], tag=tag)
        def ev(eng, c0=c0, c1=c1, w=w):
            return eng.tensor_copy(dst[:, c0:c1], pm_buf[:, 0:w])
        p.op("DVE", ev, r=[pm_buf], w=[dst], tag=tag + "_ev")


def emit_launch1(p, x_in, x_out):
    nc = p.nc
    xv = x_in.ap().rearrange("b (n p) d -> (b n) p d", p=128)
    ov = x_out.ap().rearrange("b (n p) d -> (b n) p d", p=128)

    ident = p.dram_in("c_ident_bf", (128, 128), BF16)
    ident32_h = p.dram_in("c_ident_f32", (128, 128), F32)
    ut_h = p.dram_in("c_ut_f32", (128, 128), F32)
    ones_row_h = p.dram_in("c_ones_row", (1, 128), F32)
    ones_col_h = p.dram_in("c_ones_col", (128, 1), F32)
    cos_h = p.dram_in("c_rope_cos", (T, 64), F32)
    sin_h = p.dram_in("c_rope_sin", (T, 64), F32)

    c_ident = p.sbuf("ident", [128, 128], BF16)
    c_ident32 = p.sbuf("ident32", [128, 128], F32)
    c_ut = p.sbuf("ut", [128, 128], F32)
    c_ut_bf = p.sbuf("utbf", [128, 128], BF16)
    c_ones_row = p.sbuf("ones_row", [1, 128], F32)
    c_ones_col = p.sbuf("ones_col", [128, 1], F32)
    NROPE = T // 128
    cos_t = p.sbuf("cos", [128, NROPE * 64], F32)
    sin_t = p.sbuf("sin", [128, NROPE * 64], F32)
    p.dma(c_ident[:], ident.ap(), r=[ident], w=[c_ident])
    p.dma(c_ident32[:], ident32_h.ap(), r=[ident32_h], w=[c_ident32])
    p.dma(c_ut[:], ut_h.ap(), r=[ut_h], w=[c_ut])
    p.op("DVE", lambda eng: eng.tensor_copy(c_ut_bf[:], c_ut[:]), r=[c_ut], w=[c_ut_bf])
    p.dma(c_ones_row[:], ones_row_h.ap(), r=[ones_row_h], w=[c_ones_row])
    p.dma(c_ones_col[:], ones_col_h.ap(), r=[ones_col_h], w=[c_ones_col])
    p.dma(cos_t.ap().rearrange("p (n d) -> p n d", n=NROPE),
          cos_h.ap().rearrange("(n p) d -> p n d", p=128), r=[cos_h], w=[cos_t])
    p.dma(sin_t.ap().rearrange("p (n d) -> p n d", n=NROPE),
          sin_h.ap().rearrange("(n p) d -> p n d", p=128), r=[sin_h], w=[sin_t])

    stg = [p.sbuf(f"wstg{i}", [128, 1024], F32) for i in range(2)]
    sti = [0]
    wq = p.sbuf("wq", [128, 8 * 1024], BF16)
    wk = p.sbuf("wk", [128, 8 * 1024], BF16)
    wv = p.sbuf("wv", [128, 8 * 1024], BF16)
    wo = p.sbuf("wo", [128, 8 * 1024], BF16)
    wb = p.sbuf("wb", [128, 8 * 16], F32)
    w_chunked_bf16(p, "WqT", 1024, 1024, stg, wq, sti)
    w_chunked_bf16(p, "WkT", 1024, 1024, stg, wk, sti)
    w_chunked_bf16(p, "WvT", 1024, 1024, stg, wv, sti)
    w_chunked_bf16(p, "WoT", 1024, 1024, stg, wo, sti)
    w_chunked_bf16(p, "WbT", 1024, 16, stg, wb, sti)

    bb_h = p.dram_in("attn_bb", (1, 16), F32)
    bo_h = p.dram_in("attn_bo", (1, 1024), F32)
    sc_h = p.dram_in("attn_scale", (1, 1024), F32)
    bb_r = p.sbuf("bb_r", [1, 16], F32)
    bo_r = p.sbuf("bo_r", [1, 1024], F32)
    sc_r = p.sbuf("sc_r", [1, 1024], F32)
    p.dma(bb_r[:], bb_h.ap(), r=[bb_h], w=[bb_r])
    p.dma(bo_r[:], bo_h.ap(), r=[bo_h], w=[bo_r])
    p.dma(sc_r[:], sc_h.ap(), r=[sc_h], w=[sc_r])

    pB1 = p.psum("pB1", [128, 512], F32)
    bb_b = p.sbuf("bb_b", [128, 16], F32)
    bo_b = p.sbuf("bo_b", [128, 1024], F32)
    sc_b = p.sbuf("sc_b", [128, 1024], F32)
    bcast_rows(p, pB1, bb_b, bb_r.ap(), c_ones_row, 16, r=[bb_r, c_ones_row])
    bcast_rows(p, pB1, bo_b, bo_r.ap(), c_ones_row, 1024, r=[bo_r, c_ones_row])
    bcast_rows(p, pB1, sc_b, sc_r.ap(), c_ones_row, 1024, r=[sc_r, c_ones_row])

    xb = [p.sbuf(f"xb{i}", [128, 1024], F32) for i in range(2)]
    xa_f = p.sbuf("xa_f", [128, 1024], F32)
    xa_fm = p.sbuf("xa_fm", [128, 1024], BF16)
    xa_fm32 = p.sbuf("xa_fm32", [128, 1024], F32)
    scr_bf = p.sbuf("scr_bf", [128, 1024], BF16)
    colA = p.sbuf("colA", [128, 8], F32)
    colB = p.sbuf("colB", [128, 8], F32)
    rot = p.sbuf("rot", [128, 1024], F32)
    phq = p.sbuf("phq", [128, 1024], F32)
    phk = p.sbuf("phk", [128, 1024], F32)
    qt_bf = p.sbuf("qt_bf", [128, 1024], BF16)
    kt_bf = p.sbuf("kt_bf", [128, 1024], BF16)
    qt_fm = p.sbuf("qt_fm", [128, 1024], BF16)
    kt_fm = p.sbuf("kt_fm", [128, 1024], BF16)
    v_aug = p.sbuf("v_aug", [128, 16 * 65], BF16)
    att = p.sbuf("att", [128, 1024], BF16)
    att_fm = p.sbuf("att_fm", [128, 1024], BF16)
    beta = p.sbuf("beta", [128, 16], F32)
    lnb = p.sbuf("lnb", [128, 16], F32)
    pi_t = p.sbuf("pi_t", [128, 16], F32)
    rpi_t = p.sbuf("rpi_t", [128, 16], F32)
    piC = p.sbuf("piC", [1, 16], F32)
    piC_b = p.sbuf("piC_b", [64, 16], F32)
    amask = [p.sbuf(f"amask{i}", [128, 128], BF16) for i in range(2)]
    s2aug = [p.sbuf(f"s2aug{i}", [128, 130], BF16) for i in range(8)]
    s_aug = [p.sbuf(f"saug{i}", [128, 65], F32) for i in range(8)]
    s_scr = p.sbuf("s_scr", [64, 65], F32)
    hbuf = p.sbuf("hbuf", [128, 1024], F32)
    obuf = p.sbuf("obuf", [128, 1024], F32)

    pP = p.psum("pP", [128, 1024], F32)
    pT = p.psum("pT", [128, 512], BF16)
    pA = p.psum("pA", [128, 128], F32)
    pO = p.psum("pO", [128, 130], F32)
    pS = p.psum("pS", [64, 65], F32)
    pM = p.psum("pM", [128, 16], F32)
    pM2 = pB1
    pM3 = pB1

    p.op("DVE", lambda eng: eng.memset(v_aug[:], 1.0), w=[v_aug])
    for i in range(8):
        p.op("DVE", (lambda i: lambda eng: eng.memset(s2aug[i][:], 0.0))(i),
             w=[s2aug[i]])
        p.op("DVE", (lambda i: lambda eng: eng.memset(s_aug[i][:], 0.0))(i),
             w=[s_aug[i]])

    out_dmas = []

    def ln_stats(src_buf, src_ap, rstd_b, negm_b, eps, tag):
        rstd_col = rstd_b[:]
        negm_col = negm_b[:]
        """LayerNorm stats of [128, 1024] fp32 -> rstd & -mean*rstd columns."""
        p.op("DVE", lambda eng: eng.tensor_reduce(
            out=colA[:, 0:1], in_=src_ap, axis=AX.X, op=ALU.add, negate=True),
            r=[src_buf], w=[colA], tag=tag + "_sum")
        p.op("ACT", lambda eng: eng.activation(
            out=scr_bf[:], in_=src_ap, func=AF.Square, accum_out=colA[:, 1:2]),
            r=[src_buf], w=[scr_bf, colA], tag=tag + "_sq")
        p.op("DVE", lambda eng: ts_op(eng, negm_col, colA[:, 0:1], 1.0 / D, None,
                                      ALU.mult), r=[colA], w=[negm_b], tag=tag)
        def var1(eng):
            eng.tensor_scalar(out=colA[:, 2:3], in0=negm_col, scalar1=negm_col,
                              scalar2=None, op0=ALU.mult)
            return eng.tensor_scalar(out=colA[:, 3:4], in0=colA[:, 1:2],
                                     scalar1=1.0 / D, scalar2=eps,
                                     op0=ALU.mult, op1=ALU.add)
        p.op("DVE", var1, r=[colA, negm_b], w=[colA], tag=tag + "_var1")
        p.op("DVE", lambda eng: eng.tensor_tensor(
            out=colA[:, 4:5], in0=colA[:, 3:4], in1=colA[:, 2:3],
            op=ALU.subtract), r=[colA], w=[colA], tag=tag + "_var2")
        p.op("ACT", lambda eng: eng.activation(out=colA[:, 5:6], in_=colA[:, 4:5],
                                               func=AF.Sqrt),
             r=[colA], w=[colA], tag=tag + "_sqrt")
        p.op("DVE", lambda eng: eng.reciprocal(out=rstd_col, in_=colA[:, 5:6]),
             r=[colA], w=[rstd_b], tag=tag + "_rcp")
        def nm(eng):
            return eng.tensor_tensor(out=negm_col, in0=negm_col, in1=rstd_col,
                                     op=ALU.mult)
        p.op("DVE", nm, r=[rstd_b, colA], w=[negm_b], tag=tag + "_nm")

    import os
    dbg_taps = {}
    if os.environ.get("K_DEBUG"):
        for nm, shape, dt in [("d_xa", (128, 1024), F32), ("d_pi", (128, 16), F32),
                              ("d_phq", (128, 1024), F32), ("d_qt", (128, 1024), BF16),
                              ("d_kt", (128, 1024), BF16), ("d_att", (128, 1024), BF16),
                              ("d_h", (128, 1024), F32), ("d_vaug", (128, 1040), BF16),
                              ("d_am", (128, 128), BF16), ("d_beta", (128, 16), F32)]:
            dbg_taps[nm] = p.dram_out(nm, shape, dt)

    def tap(nm, buf, s, want_s=0):
        if dbg_taps and s == want_s and nm in dbg_taps:
            od = p.dma(dbg_taps[nm].ap(), buf[:], r=[buf], w=[dbg_taps[nm]],
                       engine="POOL", tag="tap" + nm)
            out_dmas.append(od)

    rstd1 = p.sbuf("rstd1", [128, 1], F32)
    negm1 = p.sbuf("negm1", [128, 1], F32)
    rstd2 = p.sbuf("rstd2", [128, 1], F32)
    negm2 = p.sbuf("negm2", [128, 1], F32)
    rr_col = p.sbuf("rr_col", [128, 1], F32)
    rden = p.sbuf("rden", [128, 2], F32)

    for s in range(NSUB):
        x_t = xb[s % 2]
        seqpos = s % (T // 128)
        if s > 0 and seqpos == 0:
            for i in range(8):
                p.op("DVE", (lambda i: lambda eng: eng.memset(
                    s2aug[i][:], 0.0))(i), w=[s2aug[i]], tag=f"rst2_{s}_{i}")
                p.op("DVE", (lambda i: lambda eng: eng.memset(
                    s_aug[i][:], 0.0))(i), w=[s_aug[i]], tag=f"rst_{s}_{i}")
        p.dma(x_t[:], xv[s], r=[x_in], w=[x_t], tag=f"ldx{s}")

        ln_stats(x_t, x_t[:], rstd1, negm1, 1e-5, f"ln1_{s}")
        p.op("ACT", (lambda x_t: lambda eng: eng.activation(
            out=xa_f[:], in_=x_t[:], func=AF.Identity, scale=rstd1[:],
            bias=negm1[:]))(x_t),
            r=[x_t, rstd1, negm1], w=[xa_f], tag=f"xa_f{s}")

        tap("d_xa", xa_f, s)
        def trf32(eng):
            instr = None
            for ci in range(8):
                instr = eng.transpose(pP[:, ci * 128:(ci + 1) * 128],
                                      xa_f[:, ci * 128:(ci + 1) * 128],
                                      c_ident32[:])
            return instr
        p.op("TE", trf32, r=[xa_f, c_ident32], w=[pP], tag=f"tr{s}")
        for g in range(2):
            p.op("ACT", (lambda g: lambda eng: eng.activation(
                out=xa_fm32[:, g * 512:(g + 1) * 512],
                in_=pP[:, g * 512:(g + 1) * 512], func=AF.Copy))(g),
                r=[pP], w=[xa_fm32], tag=f"trev{s}_{g}")
        p.op("DVE", lambda eng: eng.tensor_copy(xa_fm[:], xa_fm32[:]),
             r=[xa_fm32], w=[xa_fm], tag=f"xafm_bf{s}")

        p.mm(pM[:], [(xa_fm32[:, ci * 128:(ci + 1) * 128],
                      wb[:, ci * 16:(ci + 1) * 16]) for ci in range(8)],
             r=[xa_fm32, wb], w=[pM], tag=f"betamm{s}")
        def betafn(eng):
            return eng.tensor_tensor(out=beta[:], in0=pM[:], in1=bb_b[:, 0:16],
                                     op=ALU.add)
        p.op("DVE", betafn, r=[pM, bb_b], w=[beta], tag=f"beta{s}")
        p.op("ACT", lambda eng: eng.activation(out=beta[:], in_=beta[:],
                                               func=AF.Sigmoid),
             r=[beta], w=[beta], tag=f"sig{s}")
        p.op("DVE", lambda eng: ts_op(eng, beta[:], beta[:], 0.8, 0.999,
                                      ALU.max, ALU.min),
             r=[beta], w=[beta], tag=f"clip{s}")
        p.op("ACT", lambda eng: eng.activation(out=lnb[:], in_=beta[:], func=AF.Ln),
             r=[beta], w=[lnb], tag=f"lnb{s}")
        p.mm(pM[:], [(c_ut[:], lnb[:])], r=[c_ut, lnb], w=[pM], tag=f"cum{s}")
        p.mm(pM2[0:1, 0:16], [(c_ones_col[:], lnb[:])], r=[c_ones_col, lnb],
             w=[pM2], tag=f"tot{s}")
        p.op("ACT", lambda eng: eng.activation(out=pi_t[:], in_=pM[:], func=AF.Exp),
             r=[pM], w=[pi_t], tag=f"pi{s}")
        p.op("ACT", lambda eng: eng.activation(out=piC[:], in_=pM2[0:1, 0:16],
                                               func=AF.Exp),
             r=[pM2], w=[piC], tag=f"piC{s}")
        p.op("DVE", lambda eng: eng.reciprocal(out=rpi_t[:], in_=pi_t[:]),
             r=[pi_t], w=[rpi_t], tag=f"rpi{s}")
        tap("d_pi", pi_t, s)
        tap("d_beta", beta, s)
        p.mm(pM3[0:64, 16:32], [(c_ones_row[0:1, 0:64], piC[0:1, :])],
             r=[c_ones_row, piC], w=[pM3], tag=f"piCb{s}")
        p.op("DVE", lambda eng: eng.tensor_copy(piC_b[:], pM3[0:64, 16:32]),
             r=[pM3], w=[piC_b], tag=f"piCbe{s}")

        cos_ap = cos_t[:, seqpos * 64:(seqpos + 1) * 64].unsqueeze(1)\
            .to_broadcast([128, 16, 64])
        sin_ap = sin_t[:, seqpos * 64:(seqpos + 1) * 64].unsqueeze(1)\
            .to_broadcast([128, 16, 64])

        def qk_path(wmat, phi_dst, pi_ap, dst_bf, dst_fm, tg,
                    cos_ap=cos_ap, sin_ap=sin_ap):
            for nh in range(2):
                p.mm(pP[:, nh * 512:(nh + 1) * 512],
                     [(xa_fm[:, ci * 128:(ci + 1) * 128],
                       wmat[:, ci * 1024 + nh * 512: ci * 1024 + nh * 512 + 512])
                      for ci in range(8)],
                     r=[xa_fm, wmat], w=[pP], tag=tg + "_mm")
            def rotfn(eng):
                eng.tensor_scalar(
                    out=rot.ap().rearrange("p (h d) -> p h d", h=16)[:, :, 0:32],
                    in0=pP.ap().rearrange("p (h d) -> p h d", h=16)[:, :, 32:64],
                    scalar1=-1.0, scalar2=None, op0=ALU.mult)
                return eng.tensor_copy(
                    rot.ap().rearrange("p (h d) -> p h d", h=16)[:, :, 32:64],
                    pP.ap().rearrange("p (h d) -> p h d", h=16)[:, :, 0:32])
            p.op("DVE", rotfn, r=[pP], w=[rot], tag=tg + "_rot")
            def ropefn(eng):
                eng.tensor_tensor(
                    out=rot.ap().rearrange("p (h d) -> p h d", h=16),
                    in0=rot.ap().rearrange("p (h d) -> p h d", h=16),
                    in1=sin_ap, op=ALU.mult)
                return eng.tensor_tensor(
                    out=phi_dst.ap().rearrange("p (h d) -> p h d", h=16),
                    in0=pP.ap().rearrange("p (h d) -> p h d", h=16),
                    in1=cos_ap, op=ALU.mult)
            p.op("DVE", ropefn, r=[pP, rot, cos_t, sin_t], w=[phi_dst, rot],
                 tag=tg + "_rope")
            p.op("DVE", lambda eng: eng.tensor_tensor(
                out=phi_dst[:], in0=phi_dst[:], in1=rot[:], op=ALU.add),
                r=[phi_dst, rot], w=[phi_dst], tag=tg + "_rope2")
            def phifn1(eng):
                return ts_op(eng, rot[:], phi_dst[:], 0.0, None, ALU.min)
            p.op("DVE", phifn1, r=[phi_dst], w=[rot], tag=tg + "_min")
            p.op("ACT", lambda eng: eng.activation(out=rot[:], in_=rot[:],
                                                   func=AF.Exp),
                 r=[rot], w=[rot], tag=tg + "_exp")
            p.op("DVE", lambda eng: eng.tensor_scalar(
                out=phi_dst[:], in0=phi_dst[:], scalar1=0.0, scalar2=None,
                op0=ALU.max), r=[phi_dst], w=[phi_dst], tag=tg + "_phimax")
            p.op("DVE", lambda eng: eng.tensor_tensor(
                out=phi_dst[:], in0=phi_dst[:], in1=rot[:], op=ALU.add),
                r=[phi_dst, rot], w=[phi_dst], tag=tg + "_phi")
            def tildefn(eng):
                return eng.tensor_tensor(
                    out=dst_bf.ap().rearrange("p (h d) -> p h d", h=16),
                    in0=phi_dst.ap().rearrange("p (h d) -> p h d", h=16),
                    in1=pi_ap, op=ALU.mult)
            p.op("DVE", tildefn, r=[phi_dst, pi_t, rpi_t], w=[dst_bf],
                 tag=tg + "_tilde")
            for g in range(2):
                def trf(eng, g=g):
                    instr = None
                    for i in range(4):
                        ci = g * 4 + i
                        instr = eng.transpose(pT[:, i * 128:(i + 1) * 128],
                                              dst_bf[:, ci * 128:(ci + 1) * 128],
                                              c_ident[:])
                    return instr
                p.op("TE", trf, r=[dst_bf, c_ident], w=[pT], tag=tg + f"_tr{g}")
                p.op("ACT", (lambda g: lambda eng: eng.activation(
                    out=dst_fm[:, g * 512:(g + 1) * 512], in_=pT[:],
                    func=AF.Copy))(g), r=[pT], w=[dst_fm], tag=tg + f"_trev{g}")

        pi_ap = pi_t.ap().unsqueeze(2).to_broadcast([128, 16, 64])
        rpi_ap = rpi_t.ap().unsqueeze(2).to_broadcast([128, 16, 64])
        qk_path(wq, phq, pi_ap, qt_bf, qt_fm, f"q{s}")
        qk_path(wk, phk, rpi_ap, kt_bf, kt_fm, f"k{s}")
        tap("d_phq", phq, s)
        tap("d_qt", qt_bf, s)
        tap("d_kt", kt_bf, s)

        for nh in range(2):
            p.mm(pP[:, nh * 512:(nh + 1) * 512],
                 [(xa_fm[:, ci * 128:(ci + 1) * 128],
                   wv[:, ci * 1024 + nh * 512: ci * 1024 + nh * 512 + 512])
                  for ci in range(8)],
                 r=[xa_fm, wv], w=[pP], tag=f"v{s}_mm")
        def vev(eng):
            return eng.activation(
                out=v_aug.ap().rearrange("p (h e) -> p h e", h=16)[:, :, 0:64],
                in_=pP.ap().rearrange("p (h e) -> p h e", h=16),
                func=AF.Copy)
        p.op("ACT", vev, r=[pP], w=[v_aug], tag=f"vev{s}")
        tap("d_vaug", v_aug, s)

        for pr in range(8):
            h0, h1 = 2 * pr, 2 * pr + 1
            for hi, h in enumerate((h0, h1)):
                base = (h % 2) * 64
                ci = h // 2
                p.mm(pA[:], [(kt_fm[base:base + 64, ci * 128:(ci + 1) * 128],
                              qt_fm[base:base + 64, ci * 128:(ci + 1) * 128])],
                     r=[kt_fm, qt_fm], w=[pA], tag=f"M1_{s}_{h}")
                p.op("DVE", (lambda hi: lambda eng: eng.tensor_tensor(
                    out=amask[hi][:], in0=pA[:], in1=c_ut[:], op=ALU.mult))(hi),
                    r=[pA, c_ut], w=[amask[hi]], tag=f"mask{s}_{h}")
            if pr == 0:
                tap("d_am", amask[0], s)
            def m32(eng, pr=pr, h0=h0, h1=h1):
                ci = pr
                eng.matmul(pO[:, 0:65], lhsT=qt_fm[:, ci * 128:(ci + 1) * 128],
                           rhs=s2aug[pr][:, 0:65], start=True, stop=False)
                eng.matmul(pO[:, 0:65], lhsT=amask[0][:],
                           rhs=v_aug[:, h0 * 65:(h0 + 1) * 65], start=False,
                           stop=True)
                eng.matmul(pO[:, 65:130], lhsT=qt_fm[:, ci * 128:(ci + 1) * 128],
                           rhs=s2aug[pr][:, 65:130], start=True, stop=False)
                return eng.matmul(pO[:, 65:130], lhsT=amask[1][:],
                                  rhs=v_aug[:, h1 * 65:(h1 + 1) * 65],
                                  start=False, stop=True)
            p.op("TE", m32, r=[qt_fm, s2aug[pr], amask[0], amask[1], v_aug],
                 w=[pO], tag=f"M32_{s}_{pr}")
            def denf(eng):
                eng.tensor_scalar(out=rden[:, 0:1], in0=pO[:, 64:65], scalar1=1e-6,
                                  scalar2=None, op0=ALU.add)
                return eng.tensor_scalar(out=rden[:, 1:2], in0=pO[:, 129:130],
                                         scalar1=1e-6, scalar2=None, op0=ALU.add)
            p.op("DVE", denf, r=[pO], w=[rden], tag=f"den{s}_{pr}")
            p.op("DVE", lambda eng: eng.reciprocal(out=rden[:], in_=rden[:]),
                 r=[rden], w=[rden], tag=f"rden{s}_{pr}")
            def oev(eng, pr=pr, h0=h0, h1=h1):
                eng.tensor_scalar(out=att[:, h0 * 64:(h0 + 1) * 64],
                                  in0=pO[:, 0:64], scalar1=rden[:, 0:1],
                                  scalar2=None, op0=ALU.mult)
                return eng.tensor_scalar(out=att[:, h1 * 64:(h1 + 1) * 64],
                                         in0=pO[:, 65:129], scalar1=rden[:, 1:2],
                                         scalar2=None, op0=ALU.mult)
            p.op("DVE", oev, r=[pO, rden], w=[att], tag=f"oev{s}_{pr}")
            for hi, h in enumerate((h0, h1)):
                p.mm(pS[:], [(kt_bf[:, h * 64:(h + 1) * 64],
                              v_aug[:, h * 65:(h + 1) * 65])],
                     r=[kt_bf, v_aug], w=[pS], tag=f"M4_{s}_{h}")
                base = (h % 2) * 64
                p.op("DVE", (lambda pr, base: lambda eng: eng.tensor_tensor(
                    out=s_scr[:], in0=s_aug[pr][base:base + 64, :], in1=pS[:],
                    op=ALU.add))(pr, base),
                    r=[pS, s_aug[pr]], w=[s_scr], tag=f"supd{s}_{h}a")
                p.op("DVE", (lambda pr, h, base: lambda eng: eng.tensor_scalar(
                    out=s_aug[pr][base:base + 64, :], in0=s_scr[:],
                    scalar1=piC_b[:, h:h + 1], scalar2=None, op0=ALU.mult))(pr, h, base),
                    r=[s_scr, piC_b], w=[s_aug[pr]], tag=f"supd{s}_{h}b")
                p.op("DVE", (lambda pr, hi, base: lambda eng: eng.tensor_copy(
                    s2aug[pr][base:base + 64, base + hi:base + 65 + hi],
                    s_aug[pr][base:base + 64, :]))(pr, hi, base),
                    r=[s_aug[pr]], w=[s2aug[pr]], tag=f"supd{s}_{h}c")

        tap("d_att", att, s)
        for g in range(2):
            def trf(eng, g=g):
                instr = None
                for i in range(4):
                    ci = g * 4 + i
                    instr = eng.transpose(pT[:, i * 128:(i + 1) * 128],
                                          att[:, ci * 128:(ci + 1) * 128],
                                          c_ident[:])
                return instr
            p.op("TE", trf, r=[att, c_ident], w=[pT], tag=f"atr{s}_{g}")
            p.op("ACT", (lambda g: lambda eng: eng.activation(
                out=att_fm[:, g * 512:(g + 1) * 512], in_=pT[:], func=AF.Copy))(g),
                r=[pT], w=[att_fm], tag=f"atrev{s}_{g}")
        for nh in range(2):
            p.mm(pP[:, nh * 512:(nh + 1) * 512],
                 [(att_fm[:, ci * 128:(ci + 1) * 128],
                   wo[:, ci * 1024 + nh * 512: ci * 1024 + nh * 512 + 512])
                  for ci in range(8)],
                 r=[att_fm, wo], w=[pP], tag=f"wo{s}_mm")
        p.op("DVE", lambda eng: eng.tensor_tensor(
            out=hbuf[:], in0=pP[:], in1=xa_f[:], op=ALU.add),
            r=[pP, xa_f], w=[hbuf], tag=f"h{s}a")
        p.op("DVE", lambda eng: eng.tensor_tensor(
            out=hbuf[:], in0=hbuf[:], in1=bo_b[:], op=ALU.add),
            r=[hbuf, bo_b], w=[hbuf], tag=f"h{s}b")
        tap("d_h", hbuf, s)
        p.op("ACT", lambda eng: eng.activation(out=scr_bf[:], in_=hbuf[:],
                                               func=AF.Square,
                                               accum_out=colB[:, 0:1]),
             r=[hbuf], w=[scr_bf, colB], tag=f"rms_sq{s}")
        def rmsfn(eng):
            return ts_op(eng, colB[:, 1:2], colB[:, 0:1], 1.0 / D, 1e-6,
                         ALU.mult, ALU.add)
        p.op("DVE", rmsfn, r=[colB], w=[colB], tag=f"rms{s}")
        p.op("ACT", lambda eng: eng.activation(out=colB[:, 2:3], in_=colB[:, 1:2],
                                               func=AF.Sqrt),
             r=[colB], w=[colB], tag=f"rms_sqrt{s}")
        p.op("DVE", lambda eng: eng.reciprocal(out=rr_col[:], in_=colB[:, 2:3]),
             r=[colB], w=[rr_col], tag=f"rms_rcp{s}")
        p.op("DVE", lambda eng: eng.tensor_scalar(
            out=obuf[:], in0=hbuf[:], scalar1=rr_col[:], scalar2=None,
            op0=ALU.mult), r=[hbuf, rr_col], w=[obuf], tag=f"osc{s}a")
        p.op("DVE", lambda eng: eng.tensor_tensor(
            out=obuf[:], in0=obuf[:], in1=sc_b[:], op=ALU.mult),
            r=[obuf, sc_b], w=[obuf], tag=f"osc{s}b")

        ln_stats(obuf, obuf[:], rstd2, negm2, 1e-5, f"ln2_{s}")
        p.op("ACT", lambda eng: eng.activation(
            out=obuf[:], in_=obuf[:], func=AF.Identity, scale=rstd2[:], bias=negm2[:]),
            r=[obuf, rstd2, negm2], w=[obuf], tag=f"lnf{s}")
        p.op("DVE", (lambda x_t: lambda eng: eng.tensor_tensor(
            out=obuf[:], in0=obuf[:], in1=x_t[:], op=ALU.add))(x_t),
             r=[obuf, x_t], w=[obuf], tag=f"res{s}")
        od = p.dma(ov[s], obuf[:], r=[obuf, x_out], w=[x_out], engine="POOL",
                   tag=f"stx{s}")
        out_dmas.append(od)

    return out_dmas



_L1_CACHE = {}


def _prep_inputs_l1(inputs, core):
    c = host_constants()
    m = {
        "x_in": np.ascontiguousarray(inputs["L0_attn_x"][2 * core:2 * core + 2]),
        "WqT": np.ascontiguousarray(inputs["L0_attn_Wq"].T),
        "WkT": np.ascontiguousarray(inputs["L0_attn_Wk"].T),
        "WvT": np.ascontiguousarray(inputs["L0_attn_Wv"].T),
        "WoT": np.ascontiguousarray(inputs["L0_attn_Wo"].T),
        "WbT": np.ascontiguousarray(inputs["L0_attn_Wb"].T),
        "attn_bb": inputs["L0_attn_bb"].reshape(1, 16),
        "attn_bo": inputs["L0_attn_bo"].reshape(1, 1024),
        "attn_scale": inputs["L0_attn_scale"].reshape(1, 1024),
    }
    m.update(c)
    return m




def _mk_single(emit, name):
    p = Prog(name)
    x_in = p.dram_in("x_in", (BPC, T, D), F32)
    x_out = p.dram_out("x_out", (BPC, T, D), F32)
    with p.scope():
        ods = emit(p, x_in, x_out)
    p.barrier_outputs(ods)
    return p.build()


def build_launch1():
    return _mk_single(emit_launch1, "l1")


def build_launch2():
    return _mk_single(emit_launch2, "l2")


def build_launch3():
    return _mk_single(emit_launch3, "l3")


def build_launch4():
    return _mk_single(emit_launch4, "l4")


def build_fused():
    p = Prog("fused")
    x_in = p.dram_in("x_in", (BPC, T, D), F32)
    x_out = p.dram_out("x_out", (BPC, T, D), F32)
    s1 = p.dram_scratch("s1", (BPC, T, D), F32)
    s2 = p.dram_scratch("s2", (BPC, T, D), F32)
    s3 = p.dram_scratch("s3", (BPC, T, D), F32)
    with p.scope():
        emit_launch1(p, x_in, s1)
    p.barrier("b1")
    with p.scope():
        emit_launch2(p, s1, s2)
    p.barrier("b2")
    with p.scope():
        emit_launch3(p, s2, s3)
    p.barrier("b3")
    with p.scope():
        ods = emit_launch4(p, s3, x_out)
    p.barrier_outputs(ods)
    return p.build()


def _prep_inputs_fused(inputs, core):
    m = {}
    dummy = dict(inputs)
    dummy["x_prev"] = inputs["L0_attn_x"]
    for prep in (_prep_inputs_l1, _prep_inputs_l2, _prep_inputs_l3,
                 _prep_inputs_l4):
        mm_ = prep(dummy, core)
        mm_.pop("x_in", None)
        m.update(mm_)
    m["x_in"] = np.ascontiguousarray(inputs["L0_attn_x"][2 * core:2 * core + 2])
    return m


def kernel(**inputs):
    nc = build_fused()
    in_maps = [_prep_inputs_fused(inputs, c) for c in range(NCORES)]
    res = run_bass_kernel_spmd(nc, in_maps, core_ids=list(range(NCORES)))
    return np.concatenate([r["x_out"] for r in res.results], axis=0)


def kernel_unfused(**inputs):
    nc1 = build_launch1()
    in_maps = [_prep_inputs_l1(inputs, c) for c in range(NCORES)]
    res = run_bass_kernel_spmd(nc1, in_maps, core_ids=list(range(NCORES)))
    x1 = np.concatenate([r["x_out"] for r in res.results], axis=0)
    nc2 = build_launch2()
    inputs2 = dict(inputs); inputs2["x_prev"] = x1
    in_maps = [_prep_inputs_l2(inputs2, c) for c in range(NCORES)]
    res = run_bass_kernel_spmd(nc2, in_maps, core_ids=list(range(NCORES)))
    x2 = np.concatenate([r["x_out"] for r in res.results], axis=0)
    nc3 = build_launch3()
    inputs3 = dict(inputs); inputs3["x_prev"] = x2
    in_maps = [_prep_inputs_l3(inputs3, c) for c in range(NCORES)]
    res = run_bass_kernel_spmd(nc3, in_maps, core_ids=list(range(NCORES)))
    x3 = np.concatenate([r["x_out"] for r in res.results], axis=0)
    nc4 = build_launch4()
    inputs4 = dict(inputs); inputs4["x_prev"] = x3
    in_maps = [_prep_inputs_l4(inputs4, c) for c in range(NCORES)]
    res = run_bass_kernel_spmd(nc4, in_maps, core_ids=list(range(NCORES)))
    x4 = np.concatenate([r["x_out"] for r in res.results], axis=0)
    return x4



def emit_launch2(p, x_in, x_out):
    FO = 22
    NR = T // 128
    xv = x_in.ap().rearrange("b (n p) d -> (b n) p d", p=128)
    ov = x_out.ap().rearrange("b (n p) d -> (b n) p d", p=128)

    zs = [p.dram_scratch(f"zs{i}", (NT, D), F32) for i in range(2)]
    ium_s = p.dram_scratch("ium", (FO, 128, NT), BF16)
    h2n_s = p.dram_scratch("h2n", (8, 128, NT), BF16)

    ident32_h = p.dram_in("c_ident_f32", (128, 128), F32)
    ones_row_h = p.dram_in("c_ones_row", (1, 128), F32)
    mask_h = p.dram_in("c_ffn_mask", (8, 128, 1024), F32)

    c_id32 = p.sbuf("id32", [128, 128], F32)
    c_idbf = p.sbuf("idbf", [128, 128], BF16)
    c_ones_row = p.sbuf("ones_row", [1, 128], F32)
    masks = p.sbuf("masks", [128, 8 * 1024], BF16)
    mask_st = p.sbuf("mask_st", [128, 1024], F32)
    p.dma(c_id32[:], ident32_h.ap(), r=[ident32_h], w=[c_id32])
    p.op("DVE", lambda eng: eng.tensor_copy(c_idbf[:], c_id32[:]),
         r=[c_id32], w=[c_idbf], tag="idbf")
    p.dma(c_ones_row[:], ones_row_h.ap(), r=[ones_row_h], w=[c_ones_row])
    for mi in range(8):
        p.dma(mask_st[:], mask_h.ap()[mi], r=[mask_h], w=[mask_st],
              tag=f"ldm{mi}")
        p.op("DVE", (lambda mi: lambda eng: eng.tensor_copy(
            masks[:, mi * 1024:(mi + 1) * 1024], mask_st[:]))(mi),
            r=[mask_st], w=[masks], tag=f"castm{mi}")
    xp = {}
    for nm in ("cq8", "sq8", "cki", "ski"):
        hh = p.dram_in("c_xp_" + nm, (T, 64), BF16)
        t_ = p.sbuf(nm, [128, NR * 64], BF16)
        p.dma(t_.ap().rearrange("p (n d) -> p n d", n=NR),
              hh.ap().rearrange("(n p) d -> p n d", p=128), r=[hh], w=[t_],
              tag="ld" + nm)
        xp[nm] = t_

    qkv_h = p.dram_in("ffn_wqkv", (2, 1024, 1536), F32)
    wo_h = p.dram_in("ffn_wo", (2, 512, 1024), F32)
    w1a_h = p.dram_in("ffn_w1a", (2, 1024, FO * 128), F32)
    w1g_h = p.dram_in("ffn_w1g", (2, 1024, FO * 128), F32)
    w2_h = p.dram_in("ffn_w2", (2, FO * 128, 1024), F32)
    b1a_h = p.dram_in("ffn_b1a", (2, 128, FO), F32)
    b1g_h = p.dram_in("ffn_b1g", (2, 128, FO), F32)
    rows_h = p.dram_in("ffn_rows", (2, 6, 1024), F32)
    qs_h = p.dram_in("ffn_qs", (2, 1, 64), F32)
    ks_h = p.dram_in("ffn_ks", (2, 1, 64), F32)

    ARW = 22528
    arena = p.sbuf("arena", [128, ARW], BF16)
    stg = [p.sbuf(f"stg{i}", [128, 1536], F32) for i in range(2)]
    b1a_c = p.sbuf("b1a_c", [128, FO], F32)
    b1g_c = p.sbuf("b1g_c", [128, FO], F32)
    rows_r = p.sbuf("rows_r", [1, 6 * 1024], F32)
    lg_b = p.sbuf("lg_b", [128, 1024], F32)
    lb_b = p.sbuf("lb_b", [128, 1024], F32)
    b2_b = p.sbuf("b2_b", [128, 1024], F32)
    qs_r = p.sbuf("qs_r", [1, 64], F32)
    ks_r = p.sbuf("ks_r", [1, 64], F32)
    qs_b = p.sbuf("qs_b", [128, 64], F32)
    ks_b = p.sbuf("ks_b", [128, 64], F32)

    kcache = p.sbuf("kcache", [128, 4096], BF16)
    vcache = p.sbuf("vcache", [128, 4096], BF16)
    h2n_ch = p.sbuf("h2n_ch", [128, 8 * 512], BF16)
    ium_ch = p.sbuf("ium_ch", [128, FO * 128], BF16)
    zt = [p.sbuf(f"zt{i}", [128, 1024], F32) for i in range(2)]
    zsh = p.sbuf("zsh", [128, 1024], F32)
    zt_bf = p.sbuf("zt_bf", [128, 1024], BF16)
    zsh_bf = p.sbuf("zsh_bf", [128, 1024], BF16)
    hsh_fm = p.sbuf("hsh_fm", [128, 1024], BF16)
    q_t = p.sbuf("q_t", [128, 512], F32)
    k_t = p.sbuf("k_t", [128, 512], F32)
    sq_scr = p.sbuf("sq_scr", [128, 512], F32)
    q1 = p.sbuf("q1", [128, 512], F32)
    rotb = p.sbuf("rotb", [128, 512], F32)
    tmp1 = sq_scr
    qk_bf = p.sbuf("qk_bf", [128, 512], BF16)
    q_fm = p.sbuf("q_fm", [128, 512], BF16)
    ssq = p.sbuf("ssq", [128, 8], F32)
    rln = p.sbuf("rln", [128, 8], F32)
    nmx = p.sbuf("nmx", [128, 1], F32)
    den = p.sbuf("den", [128, 1], F32)
    rden8 = p.sbuf("rden8", [128, 8], F32)
    sc0 = p.sbuf("scsb0", [128, 1024], F32)
    prob_L = [p.sbuf(f"prob{i}", [128, 1024], BF16) for i in range(2)]
    probT_L = [p.sbuf(f"probT{i}", [128, 1024], BF16) for i in range(2)]
    prob = prob_L[0]
    probT = probT_L[0]
    nmx_L = [p.sbuf(f"nmxL{i}", [128, 1], F32) for i in range(2)]
    den_L = [p.sbuf(f"denL{i}", [128, 1], F32) for i in range(2)]
    o_t = p.sbuf("o_t", [128, 512], BF16)
    o_fm = p.sbuf("o_fm", [128, 512], BF16)
    zo = p.sbuf("zo", [128, 1024], F32)
    h2 = p.sbuf("h2", [128, 1024], F32)
    sc_L = [sc0, h2]
    h2n = h2
    h2n_bf = zsh_bf
    aa = o_t
    gg = o_fm
    scr_bf = prob
    colA = p.sbuf("colA", [128, 8], F32)
    rstd = p.sbuf("rstd", [128, 1], F32)
    negm = p.sbuf("negm", [128, 1], F32)

    pP = p.psum("pP", [128, 512], F32)
    pS = p.psum("pS", [128, 1024], F32)
    pT = p.psum("pT", [128, 512], BF16)
    pO = p.psum("pO", [128, 512], F32)
    zP = p.psum("zP", [128, 1024], F32)
    pB = p.psum("pB", [128, 512], F32)

    out_dmas = []
    import os as _os2
    dbg = {}
    if _os2.environ.get("K_DEBUG2"):
        for nm, shape, dt in [("d_qfm", (128, 512), BF16),
                              ("d_kc", (128, 4096), BF16),
                              ("d_prob", (128, 1024), BF16),
                              ("d_ot", (128, 512), BF16),
                              ("d_hsh", (128, 1024), BF16),
                              ("d_zo", (128, 1024), F32)]:
            dbg[nm] = p.dram_out(nm, shape, dt)

    def tap2(nm, buf, cond):
        if dbg and cond and nm in dbg:
            od = p.dma(dbg[nm].ap(), buf[:], r=[buf], w=[dbg[nm]],
                       engine="POOL", tag="tap" + nm)
            out_dmas.append(od)

    def ln_stats(src_buf, src_ap, rstd_b, negm_b, eps, tag):
        rc, ncol = rstd_b[:], negm_b[:]
        p.op("DVE", (lambda src_ap: lambda eng: eng.tensor_reduce(
            out=colA[:, 0:1], in_=src_ap, axis=AX.X, op=ALU.add,
            negate=True))(src_ap),
            r=[src_buf], w=[colA], tag=tag + "_s")
        p.op("ACT", (lambda src_ap: lambda eng: eng.activation(
            out=scr_bf[:], in_=src_ap, func=AF.Square,
            accum_out=colA[:, 1:2]))(src_ap),
            r=[src_buf], w=[scr_bf, colA], tag=tag + "_q")
        p.op("DVE", lambda eng: ts_op(eng, ncol, colA[:, 0:1], 1.0 / D, None,
                                      ALU.mult), r=[colA], w=[negm_b], tag=tag)
        def var1(eng):
            eng.tensor_scalar(out=colA[:, 2:3], in0=ncol, scalar1=ncol,
                              scalar2=None, op0=ALU.mult)
            return eng.tensor_scalar(out=colA[:, 3:4], in0=colA[:, 1:2],
                                     scalar1=1.0 / D, scalar2=eps,
                                     op0=ALU.mult, op1=ALU.add)
        p.op("DVE", var1, r=[colA, negm_b], w=[colA], tag=tag + "_v1")
        p.op("DVE", lambda eng: eng.tensor_tensor(
            out=colA[:, 4:5], in0=colA[:, 3:4], in1=colA[:, 2:3],
            op=ALU.subtract), r=[colA], w=[colA], tag=tag + "_v2")
        p.op("ACT", lambda eng: eng.activation(out=colA[:, 5:6],
                                               in_=colA[:, 4:5], func=AF.Sqrt),
             r=[colA], w=[colA], tag=tag + "_sd")
        p.op("DVE", lambda eng: eng.reciprocal(out=rc, in_=colA[:, 5:6]),
             r=[colA], w=[rstd_b], tag=tag + "_r")
        p.op("DVE", lambda eng: eng.tensor_tensor(out=ncol, in0=ncol, in1=rc,
                                                  op=ALU.mult),
             r=[rstd_b, negm_b, colA], w=[negm_b], tag=tag + "_n")

    def bcast_row(dst, src_ap, n, r, tag):
        for si in range((n + 511) // 512):
            c0, c1 = si * 512, min(n, (si + 1) * 512)
            p.mm(pB[:, 0:c1 - c0], [(c_ones_row[0:1, :], src_ap[0:1, c0:c1])],
                 r=list(r) + [c_ones_row], w=[pB], tag=tag)
            p.op("DVE", (lambda c0, c1: lambda eng: eng.tensor_copy(
                dst[:, c0:c1], pB[:, 0:c1 - c0]))(c0, c1),
                r=[pB], w=[dst], tag=tag + "e")

    def tr8(src_buf, src_off, dst_buf, dst_off, n, tag):
        for g in range((n + 3) // 4):
            cnt = min(4, n - g * 4)
            def trf(eng, g=g, cnt=cnt, src_off=src_off):
                instr = None
                for i in range(cnt):
                    ci = g * 4 + i
                    instr = eng.transpose(
                        pT[:, i * 128:(i + 1) * 128],
                        src_buf[:, src_off + ci * 128: src_off + (ci + 1) * 128],
                        c_idbf[:])
                return instr
            p.op("TE", trf, r=[src_buf, c_idbf], w=[pT], tag=tag + f"t{g}")
            p.op("ACT", (lambda g, cnt, dst_off: lambda eng: eng.activation(
                out=dst_buf[:, dst_off + g * 512: dst_off + g * 512 + cnt * 128],
                in_=pT[:, 0:cnt * 128], func=AF.Copy))(g, cnt, dst_off),
                r=[pT], w=[dst_buf], tag=tag + f"e{g}")

    for s in range(NSUB):
        x_t = zt[s % 2]
        p.dma(x_t[:], xv[s], r=[x_in], w=[x_t], tag=f"p0ld{s}")
        ln_stats(x_t, x_t[:], rstd, negm, 1e-5, f"p0ln{s}")
        p.op("ACT", (lambda x_t: lambda eng: eng.activation(
            out=zo[:], in_=x_t[:], func=AF.Identity, scale=rstd[:],
            bias=negm[:]))(x_t), r=[x_t, rstd, negm], w=[zo], tag=f"p0a{s}")
        p.dma(zs[0].ap().rearrange("(n p) d -> n p d", p=128)[s], zo[:],
              r=[zo], w=[zs[0]], engine="POOL", tag=f"p0st{s}")

    zin, zout = zs[0], zs[1]
    import os as _os
    _L2N = int(_os.environ.get("K_L2N", "2"))
    for l in range(_L2N):
        qv = qkv_h.ap()[l].rearrange("(n p) c -> n p c", p=128)
        for ci in range(8):
            st = stg[ci % 2]
            p.dma(st[:, 0:1536], qv[ci], r=[qkv_h], w=[st], tag=f"qkvld{l}{ci}")
            p.op("DVE", (lambda st, ci: lambda eng: eng.tensor_copy(
                arena[:, ci * 1536:(ci + 1) * 1536], st[:, 0:1536]))(st, ci),
                r=[st], w=[arena], tag=f"qkvc{l}{ci}")
        wv_ = wo_h.ap()[l].rearrange("(n p) c -> n p c", p=128)
        for ci in range(4):
            st = stg[ci % 2]
            p.dma(st[:, 0:1024], wv_[ci], r=[wo_h], w=[st], tag=f"wold{l}{ci}")
            p.op("DVE", (lambda st, ci: lambda eng: eng.tensor_copy(
                arena[:, 12288 + ci * 1024: 12288 + (ci + 1) * 1024],
                st[:, 0:1024]))(st, ci), r=[st], w=[arena], tag=f"woc{l}{ci}")
        p.dma(b1a_c[:], b1a_h.ap()[l], r=[b1a_h], w=[b1a_c], tag=f"b1a{l}")
        p.dma(b1g_c[:], b1g_h.ap()[l], r=[b1g_h], w=[b1g_c], tag=f"b1g{l}")
        p.dma(rows_r.ap().rearrange("o (n d) -> o n d", n=6),
              rows_h.ap()[l].unsqueeze(0), r=[rows_h], w=[rows_r],
              tag=f"rows{l}")
        bcast_row(lg_b, rows_r.ap()[:, 0:1024], 1024, [rows_r], f"lgb{l}")
        bcast_row(lb_b, rows_r.ap()[:, 1024:2048], 1024, [rows_r], f"lbb{l}")
        bcast_row(b2_b, rows_r.ap()[:, 2048:3072], 1024, [rows_r], f"b2b{l}")
        p.dma(qs_r[:], qs_h.ap()[l], r=[qs_h], w=[qs_r], tag=f"qs{l}")
        p.dma(ks_r[:], ks_h.ap()[l], r=[ks_h], w=[ks_r], tag=f"ks{l}")
        bcast_row(qs_b, qs_r.ap(), 64, [qs_r], f"qsb{l}")
        bcast_row(ks_b, ks_r.ap(), 64, [ks_r], f"ksb{l}")

        ziv = zin.ap().rearrange("(n p) d -> n p d", p=128)
        zflat = zin.ap()
        zov = zout.ap().rearrange("(n p) d -> n p d", p=128)

        for s in range(NSUB):
            seqpos = s % NR
            w_ = seqpos // 4
            ppos = seqpos % 4
            su = seqpos % 4
            if seqpos == 0:
                p.op("DVE", lambda eng: eng.memset(kcache[:], 0.0),
                     w=[kcache], tag=f"kc0_{l}_{s}")
                p.op("DVE", lambda eng: eng.memset(vcache[:], 0.0),
                     w=[vcache], tag=f"vc0_{l}_{s}")
            elif su == 0:
                p.op("DVE", lambda eng: eng.tensor_copy(
                    kcache[:, 0:2048], kcache[:, 2048:4096]),
                    r=[kcache], w=[kcache], tag=f"kcs_{l}_{s}")
                p.op("DVE", lambda eng: eng.tensor_copy(
                    vcache[:, 0:2048], vcache[:, 2048:4096]),
                    r=[vcache], w=[vcache], tag=f"vcs_{l}_{s}")
            z_t = zt[s % 2]
            p.dma(z_t[:], ziv[s], r=[zin], w=[z_t], tag=f"p1ld{l}_{s}")
            if seqpos == 0:
                p.op("DVE", lambda eng: eng.memset(zsh[:], 0.0), w=[zsh],
                     tag=f"zsh0{l}_{s}")
                p.dma(zsh[1:128, :], zflat[s * 128: s * 128 + 127, :],
                      r=[zin], w=[zsh], tag=f"p1lds{l}_{s}")
            else:
                p.dma(zsh[:], zflat[s * 128 - 1: s * 128 + 127, :],
                      r=[zin], w=[zsh], tag=f"p1lds{l}_{s}")
            p.op("DVE", (lambda z_t: lambda eng: eng.tensor_copy(
                zt_bf[:], z_t[:]))(z_t), r=[z_t], w=[zt_bf], tag=f"ztb{l}_{s}")
            p.op("DVE", lambda eng: eng.tensor_copy(zsh_bf[:], zsh[:]),
                 r=[zsh], w=[zsh_bf], tag=f"zshb{l}_{s}")
            tr8(zt_bf, 0, hsh_fm, 0, 4, f"trz{l}_{s}")
            tr8(zsh_bf, 512, hsh_fm, 512, 4, f"trs{l}_{s}")

            for which in range(3):
                pairs = [(hsh_fm[:, ci * 128:(ci + 1) * 128],
                          arena[:, ci * 1536 + which * 512:
                                ci * 1536 + which * 512 + 512])
                         for ci in range(8)]
                p.mm(pP[:], pairs, r=[hsh_fm, arena], w=[pP],
                     tag=f"qkv{l}_{s}_{which}")
                if which == 0:
                    p.op("ACT", lambda eng: eng.activation(
                        out=q_t[:], in_=pP[:], func=AF.Copy),
                        r=[pP], w=[q_t], tag=f"qe{l}_{s}")
                elif which == 1:
                    p.op("ACT", lambda eng: eng.activation(
                        out=k_t[:], in_=pP[:], func=AF.Copy),
                        r=[pP], w=[k_t], tag=f"ke{l}_{s}")
                else:
                    p.op("ACT", (lambda su: lambda eng: eng.activation(
                        out=vcache[:, 2048 + su * 512: 2048 + (su + 1) * 512],
                        in_=pP[:], func=AF.Copy))(su),
                        r=[pP], w=[vcache], tag=f"ve{l}_{s}")

            def qk_norm(src, sc_b, ctab, stab, dst_bf, tg, seqpos=seqpos):
                p.op("ACT", (lambda src: lambda eng: eng.activation(
                    out=sq_scr[:], in_=src[:], func=AF.Square))(src),
                    r=[src], w=[sq_scr], tag=tg + "sq")
                p.op("DVE", lambda eng: eng.tensor_reduce(
                    out=ssq[:], in_=sq_scr.ap().rearrange(
                        "p (h d) -> p h d", h=8), axis=AX.X, op=ALU.add),
                    r=[sq_scr], w=[ssq], tag=tg + "rd")
                p.op("ACT", lambda eng: eng.activation(
                    out=ssq[:], in_=ssq[:], func=AF.Sqrt),
                    r=[ssq], w=[ssq], tag=tg + "sd")
                p.op("DVE", lambda eng: ts_op(eng, ssq[:], ssq[:], 1e-12, None,
                                              ALU.max), r=[ssq], w=[ssq],
                     tag=tg + "mx")
                p.op("DVE", lambda eng: eng.reciprocal(out=rln[:], in_=ssq[:]),
                     r=[ssq], w=[rln], tag=tg + "rc")
                p.op("DVE", (lambda src: lambda eng: eng.tensor_tensor(
                    out=q1.ap().rearrange("p (h d) -> p h d", h=8),
                    in0=src.ap().rearrange("p (h d) -> p h d", h=8),
                    in1=rln.ap().unsqueeze(2).to_broadcast([128, 8, 64]),
                    op=ALU.mult))(src), r=[src, rln], w=[q1], tag=tg + "l2")
                p.op("DVE", (lambda sc_b: lambda eng: eng.tensor_tensor(
                    out=q1.ap().rearrange("p (h d) -> p h d", h=8),
                    in0=q1.ap().rearrange("p (h d) -> p h d", h=8),
                    in1=sc_b.ap().unsqueeze(1).to_broadcast([128, 8, 64]),
                    op=ALU.mult))(sc_b), r=[q1, sc_b], w=[q1], tag=tg + "qs")
                def rotf(eng):
                    eng.tensor_scalar(
                        out=rotb.ap().rearrange("p (h d) -> p h d", h=8)[:, :, 0:32],
                        in0=q1.ap().rearrange("p (h d) -> p h d", h=8)[:, :, 32:64],
                        scalar1=-1.0, scalar2=None, op0=ALU.mult)
                    return eng.tensor_copy(
                        rotb.ap().rearrange("p (h d) -> p h d", h=8)[:, :, 32:64],
                        q1.ap().rearrange("p (h d) -> p h d", h=8)[:, :, 0:32])
                p.op("DVE", rotf, r=[q1], w=[rotb], tag=tg + "rot")
                cap = ctab[:, seqpos * 64:(seqpos + 1) * 64].unsqueeze(1)\
                    .to_broadcast([128, 8, 64])
                sap = stab[:, seqpos * 64:(seqpos + 1) * 64].unsqueeze(1)\
                    .to_broadcast([128, 8, 64])
                p.op("DVE", (lambda cap: lambda eng: eng.tensor_tensor(
                    out=tmp1.ap().rearrange("p (h d) -> p h d", h=8),
                    in0=q1.ap().rearrange("p (h d) -> p h d", h=8),
                    in1=cap, op=ALU.mult))(cap), r=[q1], w=[tmp1],
                    tag=tg + "c")
                p.op("DVE", (lambda sap: lambda eng: eng.tensor_tensor(
                    out=rotb.ap().rearrange("p (h d) -> p h d", h=8),
                    in0=rotb.ap().rearrange("p (h d) -> p h d", h=8),
                    in1=sap, op=ALU.mult))(sap), r=[rotb], w=[rotb],
                    tag=tg + "s")
                p.op("DVE", (lambda dst_bf: lambda eng: eng.tensor_tensor(
                    out=dst_bf[:], in0=tmp1[:], in1=rotb[:], op=ALU.add))(dst_bf),
                    r=[tmp1, rotb], w=[dst_bf], tag=tg + "a")

            qk_norm(q_t, qs_b, xp["cq8"].h, xp["sq8"].h, qk_bf, f"qn{l}_{s}")
            tr8(qk_bf, 0, q_fm, 0, 4, f"qtr{l}_{s}")
            tap2("d_qfm", q_fm, l == 0 and s == 0)
            tap2("d_hsh", hsh_fm, l == 0 and s == 0)
            qk_norm(k_t, ks_b, xp["cki"].h, xp["ski"].h, qk_bf, f"kn{l}_{s}")
            def trfk(eng):
                instr = None
                for ci in range(4):
                    instr = eng.transpose(
                        pT[:, ci * 128:(ci + 1) * 128],
                        qk_bf[:, ci * 128:(ci + 1) * 128], c_idbf[:])
                return instr
            p.op("TE", trfk, r=[qk_bf, c_idbf], w=[pT], tag=f"ktr{l}_{s}")
            p.op("ACT", (lambda su: lambda eng: eng.activation(
                out=kcache[:, 2048 + su * 512: 2048 + (su + 1) * 512],
                in_=pT[:], func=AF.Copy))(su),
                r=[pT], w=[kcache], tag=f"kce{l}_{s}")

            mi = (0 if w_ == 0 else 1) * 4 + ppos
            tap2("d_kc", kcache, l == 0 and s == 0)
            for hh in range(8):
                prob = prob_L[hh % 2]
                probT = probT_L[hh % 2]
                nmx = nmx_L[hh % 2]
                den = den_L[hh % 2]
                base = (hh % 2) * 64
                ci = hh // 2
                kap = kcache.ap().rearrange(
                    "p (b ci x) -> p b ci x", b=8, ci=4)[
                    base:base + 64, :, ci, :]
                scb = sc_L[hh % 2]
                for nh in range(2):
                    p.mm(pS[:, nh * 512:(nh + 1) * 512],
                         [(q_fm[base:base + 64, ci * 128:(ci + 1) * 128],
                           kap[:, nh * 4:(nh + 1) * 4, :])],
                         r=[q_fm, kcache], w=[pS], tag=f"sc{l}_{s}_{hh}_{nh}")
                p.op("ACT", (lambda scb: lambda eng: eng.activation(
                    out=scb[:], in_=pS[:], func=AF.Copy))(scb),
                    r=[pS], w=[scb], tag=f"sce{l}_{s}_{hh}")
                p.op("DVE", (lambda mi, scb: lambda eng: eng.tensor_tensor(
                    out=scb[:], in0=scb[:],
                    in1=masks[:, mi * 1024:(mi + 1) * 1024], op=ALU.add))(mi, scb),
                    r=[scb, masks], w=[scb], tag=f"msk{l}_{s}_{hh}")
                p.op("DVE", (lambda nmx, scb: lambda eng: eng.tensor_reduce(
                    out=nmx[:], in_=scb[:], axis=AX.X, op=ALU.max,
                    negate=True))(nmx, scb),
                    r=[scb], w=[nmx], tag=f"mx{l}_{s}_{hh}")
                p.op("ACT", (lambda prob, nmx, den, scb: lambda eng: eng.activation(
                    out=prob[:], in_=scb[:], func=AF.Exp, bias=nmx[:],
                    accum_out=den[:]))(prob, nmx, den, scb),
                    r=[scb, nmx], w=[prob, den],
                    tag=f"ex{l}_{s}_{hh}")
                p.op("DVE", (lambda hh, den: lambda eng: eng.reciprocal(
                    out=rden8[:, hh:hh + 1], in_=den[:]))(hh, den),
                    r=[den], w=[rden8], tag=f"rd{l}_{s}_{hh}")
                tap2("d_prob", prob, l == 0 and s == 0 and hh == 0)
                tr8(prob, 0, probT, 0, 8, f"ptr{l}_{s}_{hh}")
                pairs = [(probT[:, su2 * 128:(su2 + 1) * 128],
                          vcache[:, su2 * 512 + hh * 64: su2 * 512 + hh * 64 + 64])
                         for su2 in range(8)]
                p.mm(pO[:, hh * 64:(hh + 1) * 64], pairs, r=[probT, vcache],
                     w=[pO], tag=f"av{l}_{s}_{hh}")
            def oev(eng):
                instr = None
                for hh in range(8):
                    instr = eng.tensor_scalar(
                        out=o_t[:, hh * 64:(hh + 1) * 64],
                        in0=pO[:, hh * 64:(hh + 1) * 64],
                        scalar1=rden8[:, hh:hh + 1], scalar2=None, op0=ALU.mult)
                return instr
            p.op("DVE", oev, r=[pO, rden8], w=[o_t], tag=f"oev{l}_{s}")
            tap2("d_ot", o_t, l == 0 and s == 0)
            tr8(o_t, 0, o_fm, 0, 4, f"otr{l}_{s}")
            for nh in range(2):
                pairs = [(o_fm[:, ci2 * 128:(ci2 + 1) * 128],
                          arena[:, 12288 + ci2 * 1024 + nh * 512:
                                12288 + ci2 * 1024 + nh * 512 + 512])
                         for ci2 in range(4)]
                p.mm(zP[:, nh * 512:(nh + 1) * 512], pairs, r=[o_fm, arena],
                     w=[zP], tag=f"wo{l}_{s}_{nh}")
            p.op("DVE", (lambda z_t: lambda eng: eng.tensor_tensor(
                out=zo[:], in0=zP[:], in1=z_t[:], op=ALU.add))(z_t),
                r=[zP, z_t], w=[zo], tag=f"zr{l}_{s}")
            tap2("d_zo", zo, l == 0 and s == 0)
            p.dma(zov[s], zo[:], r=[zo], w=[zout], engine="POOL",
                  tag=f"p1st{l}_{s}")

        zi2 = zout.ap().rearrange("(n p) d -> n p d", p=128)
        z2flat = zout.ap()
        for s in range(NSUB):
            seqpos = s % NR
            z_t = zt[s % 2]
            p.dma(z_t[:], zi2[s], r=[zout], w=[z_t], tag=f"h2ld{l}_{s}")
            if seqpos == 0:
                p.op("DVE", lambda eng: eng.memset(zsh[:], 0.0), w=[zsh],
                     tag=f"h2z0{l}_{s}")
                p.dma(zsh[1:128, :], z2flat[s * 128: s * 128 + 127, :],
                      r=[zout], w=[zsh], tag=f"h2lds{l}_{s}")
            else:
                p.dma(zsh[:], z2flat[s * 128 - 1: s * 128 + 127, :],
                      r=[zout], w=[zsh], tag=f"h2lds{l}_{s}")
            p.op("DVE", (lambda z_t: lambda eng: eng.tensor_copy(
                h2[:, 0:512], z_t[:, 0:512]))(z_t), r=[z_t], w=[h2],
                tag=f"h2a{l}_{s}")
            p.op("DVE", lambda eng: eng.tensor_copy(h2[:, 512:1024],
                                                    zsh[:, 512:1024]),
                 r=[zsh], w=[h2], tag=f"h2b{l}_{s}")
            ln_stats(h2, h2[:], rstd, negm, 1e-5, f"h2ln{l}_{s}")
            p.op("ACT", lambda eng: eng.activation(
                out=h2n[:], in_=h2[:], func=AF.Identity, scale=rstd[:],
                bias=negm[:]), r=[h2, rstd, negm], w=[h2n], tag=f"h2n{l}_{s}")
            p.op("DVE", lambda eng: eng.tensor_tensor(
                out=h2n[:], in0=h2n[:], in1=lg_b[:], op=ALU.mult),
                r=[h2n, lg_b], w=[h2n], tag=f"h2g{l}_{s}")
            p.op("DVE", lambda eng: eng.tensor_tensor(
                out=h2n_bf[:], in0=h2n[:], in1=lb_b[:], op=ALU.add),
                r=[h2n, lb_b], w=[h2n_bf], tag=f"h2o{l}_{s}")
            tr8(h2n_bf, 0, zt_bf, 0, 8, f"h2t{l}_{s}")
            p.dma(h2n_s.ap()[:, :, s * 128:(s + 1) * 128].rearrange(
                "c p x -> p c x"),
                zt_bf.ap().rearrange("p (c x) -> p c x", c=8),
                r=[zt_bf], w=[h2n_s], engine="POOL", tag=f"h2st{l}_{s}")

        iv = ium_s.ap()
        for half in range(2):
            fos = list(range(half * 11, (half + 1) * 11))
            for j, fo in enumerate(fos):
                st = stg[j % 2]
                w1av = w1a_h.ap()[l, :, fo * 128:(fo + 1) * 128]\
                    .rearrange("(nc p) f -> p nc f", p=128)
                p.dma(st.ap()[:, 0:1024].rearrange("p (nc f) -> p nc f", nc=8),
                      w1av, r=[w1a_h], w=[st], tag=f"w1al{l}_{fo}")
                p.op("DVE", (lambda st, j: lambda eng: eng.tensor_copy(
                    arena[:, j * 2048: j * 2048 + 1024], st[:, 0:1024]))(st, j),
                    r=[st], w=[arena], tag=f"w1ac{l}_{fo}")
                w1gv = w1g_h.ap()[l, :, fo * 128:(fo + 1) * 128]\
                    .rearrange("(nc p) f -> p nc f", p=128)
                p.dma(st.ap()[:, 0:1024].rearrange("p (nc f) -> p nc f", nc=8),
                      w1gv, r=[w1g_h], w=[st], tag=f"w1gl{l}_{fo}")
                p.op("DVE", (lambda st, j: lambda eng: eng.tensor_copy(
                    arena[:, j * 2048 + 1024: j * 2048 + 2048],
                    st[:, 0:1024]))(st, j),
                    r=[st], w=[arena], tag=f"w1gc{l}_{fo}")
            for ch in range(8):
                p.dma(h2n_ch.ap().rearrange("p (c x) -> p c x", c=8),
                      h2n_s.ap()[:, :, ch * 512:(ch + 1) * 512].rearrange(
                          "c p x -> p c x"),
                      r=[h2n_s], w=[h2n_ch], tag=f"h2cl{l}_{half}_{ch}")
                for j, fo in enumerate(fos):
                    pa = [(arena[:, j * 2048 + ci * 128: j * 2048 + (ci + 1) * 128],
                           h2n_ch[:, ci * 512:(ci + 1) * 512])
                          for ci in range(8)]
                    pg = [(arena[:, j * 2048 + 1024 + ci * 128:
                                 j * 2048 + 1024 + (ci + 1) * 128],
                           h2n_ch[:, ci * 512:(ci + 1) * 512])
                          for ci in range(8)]
                    p.mm(pS[:, 0:512], pa, r=[arena, h2n_ch], w=[pS],
                         tag=f"w1am{l}_{fo}_{ch}")
                    p.mm(pS[:, 512:1024], pg, r=[arena, h2n_ch], w=[pS],
                         tag=f"w1gm{l}_{fo}_{ch}")
                    p.op("ACT", (lambda fo: lambda eng: eng.activation(
                        out=aa[:], in_=pS[:, 0:512], func=AF.Identity,
                        bias=b1a_c[:, fo:fo + 1]))(fo),
                        r=[pS, b1a_c], w=[aa], tag=f"aae{l}_{fo}_{ch}")
                    p.op("ACT", (lambda fo: lambda eng: eng.activation(
                        out=gg[:], in_=pS[:, 512:1024], func=AF.Gelu,
                        bias=b1g_c[:, fo:fo + 1]))(fo),
                        r=[pS, b1g_c], w=[gg], tag=f"gge{l}_{fo}_{ch}")
                    p.op("DVE", lambda eng: eng.tensor_tensor(
                        out=aa[:], in0=aa[:], in1=gg[:], op=ALU.mult),
                        r=[aa, gg], w=[aa], tag=f"ium{l}_{fo}_{ch}")
                    p.dma(iv[fo, :, ch * 512:(ch + 1) * 512], aa[:],
                          r=[aa], w=[ium_s], engine="POOL",
                          tag=f"iust{l}_{fo}_{ch}")

        for j in range(FO):
            st = stg[j % 2]
            w2v = w2_h.ap()[l, j * 128:(j + 1) * 128, :]
            p.dma(st[:, 0:1024], w2v, r=[w2_h], w=[st], tag=f"w2l{l}_{j}")
            p.op("DVE", (lambda st, j: lambda eng: eng.tensor_copy(
                arena[:, j * 1024:(j + 1) * 1024], st[:, 0:1024]))(st, j),
                r=[st], w=[arena], tag=f"w2c{l}_{j}")
        for ch in range(8):
            for tsub in range(4):
                s = ch * 4 + tsub
                for fo in range(FO):
                    p.dma(ium_ch[:, fo * 128:(fo + 1) * 128],
                          iv[fo, :, s * 128:(s + 1) * 128],
                          r=[ium_s], w=[ium_ch], tag=f"iuld{l}_{s}_{fo}")
                for nh in range(2):
                    pairs = [(ium_ch[:, fo * 128:(fo + 1) * 128],
                              arena[:, fo * 1024 + nh * 512:
                                    fo * 1024 + nh * 512 + 512])
                             for fo in range(FO)]
                    p.mm(zP[:, nh * 512:(nh + 1) * 512], pairs,
                         r=[ium_ch, arena], w=[zP], tag=f"w2m{l}_{s}_{nh}")
                z_t = zt[s % 2]
                p.dma(z_t[:], zi2[s], r=[zout], w=[z_t], tag=f"p2bl{l}_{s}")
                p.op("DVE", lambda eng: eng.tensor_tensor(
                    out=zo[:], in0=zP[:], in1=b2_b[:], op=ALU.add),
                    r=[zP, b2_b], w=[zo], tag=f"zb2{l}_{s}")
                p.op("DVE", (lambda z_t: lambda eng: eng.tensor_tensor(
                    out=zo[:], in0=zo[:], in1=z_t[:], op=ALU.add))(z_t),
                    r=[zo, z_t], w=[zo], tag=f"zr2{l}_{s}")
                p.dma(ziv[s] if False else zin.ap().rearrange(
                    "(n p) d -> n p d", p=128)[s], zo[:],
                    r=[zo], w=[zin], engine="POOL", tag=f"p2st{l}_{s}")

    zfv = zin.ap().rearrange("(n p) d -> n p d", p=128)
    for s in range(NSUB):
        z_t = zt[s % 2]
        p.dma(z_t[:], zfv[s], r=[zin], w=[z_t], tag=f"p3ld{s}")
        p.dma(zsh[:], xv[s], r=[x_in], w=[zsh], tag=f"p3lx{s}")
        ln_stats(z_t, z_t[:], rstd, negm, 1e-5, f"p3ln{s}")
        p.op("ACT", (lambda z_t: lambda eng: eng.activation(
            out=zo[:], in_=z_t[:], func=AF.Identity, scale=rstd[:],
            bias=negm[:]))(z_t), r=[z_t, rstd, negm], w=[zo], tag=f"p3a{s}")
        p.op("DVE", lambda eng: eng.tensor_tensor(
            out=zo[:], in0=zo[:], in1=zsh[:], op=ALU.add),
            r=[zo, zsh], w=[zo], tag=f"p3r{s}")
        od = p.dma(ov[s], zo[:], r=[zo], w=[x_out], engine="POOL",
                   tag=f"p3st{s}")
        out_dmas.append(od)

    return out_dmas


def _prep_inputs_l2(inputs, core):
    c = host_constants2()
    wqkv = inputs["L0_ffn_wqkv"].astype(np.float32)
    w1 = inputs["L0_ffn_w1"]
    w1a = np.zeros((2, 1024, 2816), np.float32)
    w1g = np.zeros((2, 1024, 2816), np.float32)
    w1a[:, :, 0:2730] = w1[:, :, 0:2730]
    w1g[:, :, 0:2730] = w1[:, :, 2730:5460]
    w2 = np.zeros((2, 2816, 1024), np.float32)
    w2[:, 0:2730, :] = inputs["L0_ffn_w2"]
    b1 = inputs["L0_ffn_b1"]
    b1a = np.zeros((2, 2816), np.float32)
    b1g = np.zeros((2, 2816), np.float32)
    b1a[:, 0:2730] = b1[:, 0:2730]
    b1g[:, 0:2730] = b1[:, 2730:5460]
    b1a = np.ascontiguousarray(b1a.reshape(2, 22, 128).transpose(0, 2, 1))
    b1g = np.ascontiguousarray(b1g.reshape(2, 22, 128).transpose(0, 2, 1))
    rows = np.stack([np.concatenate([
        inputs["L0_ffn_ln_g"][l], inputs["L0_ffn_ln_b"][l],
        inputs["L0_ffn_b2"][l], np.zeros(1024 * 3, np.float32)])
        for l in range(2)]).reshape(2, 6, 1024)
    m = {
        "x_in": np.ascontiguousarray(inputs["x_prev"][2 * core:2 * core + 2]),
        "ffn_wqkv": wqkv,
        "ffn_wo": inputs["L0_ffn_wo"].astype(np.float32),
        "ffn_w1a": w1a, "ffn_w1g": w1g, "ffn_w2": w2,
        "ffn_b1a": b1a, "ffn_b1g": b1g,
        "ffn_rows": rows,
        "ffn_qs": inputs["L0_ffn_q_scale"].reshape(2, 1, 64).astype(np.float32),
        "ffn_ks": inputs["L0_ffn_k_scale"].reshape(2, 1, 64).astype(np.float32),
    }
    m.update(c)
    return m


def host_constants2():
    c = {}
    ident = np.eye(128, dtype=np.float32)
    c["c_ident_f32"] = ident
    c["c_ones_row"] = np.ones((1, 128), dtype=np.float32)
    Dh, W = 64, 512
    half = np.arange(0, Dh, 2, dtype=np.float32)
    inv = 1.0 / 10000.0 ** (half / Dh)
    t = np.arange(T, dtype=np.float32)
    fr = t[:, None] * inv[None, :]
    fr2 = np.concatenate([fr, fr], -1)
    base = (half + 0.4 * Dh) / (1.4 * Dh)
    power = (t - T // 2) / float(W // 2)
    sc = base[None, :] ** power[:, None]
    sc = np.concatenate([sc, sc], -1)
    cos, sin = np.cos(fr2), np.sin(fr2)
    c["c_xp_cq8"] = (8.0 * cos * sc).astype(ml_dtypes.bfloat16)
    c["c_xp_sq8"] = (8.0 * sin * sc).astype(ml_dtypes.bfloat16)
    c["c_xp_cki"] = (cos / sc).astype(ml_dtypes.bfloat16)
    c["c_xp_ski"] = (sin / sc).astype(ml_dtypes.bfloat16)
    masks = np.full((8, 128, 1024), -1e30, np.float32)
    sr = np.arange(1024)
    for pzn in range(4):
        for tl in range(128):
            tpos = pzn * 128 + tl
            ok = (sr >= tpos) & (sr <= 512 + tpos)
            masks[4 + pzn, tl, ok] = 0.0
            ok0 = (sr >= 512) & (sr - 512 <= tpos)
            masks[pzn, tl, ok0] = 0.0
    c["c_ffn_mask"] = masks
    return c



def emit_launch3(p, x_in, x_out):
    NR = T // 128
    NCH = NT // 512
    CT = 16
    S = 16
    xv = x_in.ap().rearrange("b (n p) d -> (b n) p d", p=128)
    ov = x_out.ap().rearrange("b (n p) d -> (b n) p d", p=128)

    zm = [p.dram_scratch(f"zm{i}", (NT, D), F32) for i in range(2)]
    wouts = p.dram_scratch("wouts", (16, 128, 1024), BF16)
    wres_s = p.dram_scratch("wres_s", (8, 128, 2048), BF16)

    ones_row_h = p.dram_in("c_ones_row", (1, 128), F32)
    ident32_h = p.dram_in("c_ident_f32", (128, 128), F32)
    esel_h = p.dram_in("c_esel", (16, 2048), F32)
    c_id32 = p.sbuf("id32", [128, 128], F32)
    c_idbf = p.sbuf("idbf", [128, 128], BF16)
    c_ones_row = p.sbuf("ones_row", [1, 128], F32)
    c_esel = p.sbuf("esel", [16, 2048], F32)
    p.dma(c_id32[:], ident32_h.ap(), r=[ident32_h], w=[c_id32])
    p.op("DVE", lambda eng: eng.tensor_copy(c_idbf[:], c_id32[:]),
         r=[c_id32], w=[c_idbf], tag="idbf")
    p.dma(c_ones_row[:], ones_row_h.ap(), r=[ones_row_h], w=[c_ones_row])
    p.dma(c_esel[:], esel_h.ap(), r=[esel_h], w=[c_esel])

    win_h = p.dram_in("mb_W_in", (1024, 4096), F32)
    wout_h = p.dram_in("mb_W_out", (2048, 1024), F32)
    wxd_h = p.dram_in("mb_Wx_dt", (2048, 64), F32)
    wxb_h = p.dram_in("mb_Wx_B", (2048, 16), F32)
    wxc_h = p.dram_in("mb_Wx_C", (2048, 16), F32)
    wdt_h = p.dram_in("mb_W_dt", (64, 2048), F32)
    convw_h = p.dram_in("mb_conv_w", (128, 16 * 4), F32)
    cols_h = p.dram_in("mb_cols", (128, 16 * 4), F32)
    A_h = p.dram_in("mb_A", (128, 16 * 16), F32)

    w_in = p.sbuf("w_in", [128, 8 * 2048], BF16)
    wres_st = [p.sbuf(f"wres{i}", [128, 1024], BF16) for i in range(2)]
    wx_dt = p.sbuf("wx_dt", [128, CT * 64], BF16)
    wx_B = p.sbuf("wx_B", [128, CT * 16], BF16)
    wx_C = p.sbuf("wx_C", [128, CT * 16], BF16)
    w_dt = p.sbuf("w_dt", [64, 2048], BF16)
    convw = p.sbuf("convw", [128, 64], F32)
    colsb = p.sbuf("colsb", [128, 64], F32)
    A_sb = p.sbuf("A_sb", [128, 256], F32)
    wo_st = [p.sbuf(f"wo_st{i}", [128, 512], BF16) for i in range(2)]

    stg = [p.sbuf("stg0", [128, 2048], F32)]
    stg = [stg[0], stg[0]]

    zc = p.sbuf("zc", [128, 2 * 1024], F32)
    zn_fm = p.sbuf("zn_fm", [128, 8 * 512], BF16)
    u_bf = p.sbuf("u_bf", [128, CT * 512], BF16)
    d_bf = p.sbuf("d_bf", [128, CT * 512], BF16)
    du_bf = p.sbuf("du_bf", [128, CT * 512], BF16)
    res_bf = p.sbuf("res_bf", [128, CT * 512], BF16)
    y_bf = p.sbuf("y_bf", [128, CT * 512], BF16)
    x_halo = p.sbuf("x_halo", [128, 515], BF16)
    tails = p.sbuf("tails", [128, CT * 3], BF16)
    conv_f = p.sbuf("conv_f", [128, 512], F32)
    conv_g = p.sbuf("conv_g", [128, 512], F32)
    dt_bf = p.sbuf("dt_bf", [64, 512], BF16)
    B_sb = p.sbuf("B_sb", [16, 512], F32)
    C_sb = p.sbuf("C_sb", [16, 512], F32)
    B_bc = p.sbuf("B_bc", [128, 512], BF16)
    C_bc = p.sbuf("C_bc", [128, 512], BF16)
    d_t = [[p.sbuf(f"d_t{i}_{k}", [128, 512], BF16) for k in range(4)]
           for i in range(2)]
    b_t = [p.sbuf(f"b_t{k}", [128, 512], BF16) for k in range(4)]
    h4 = [p.sbuf(f"h4_{i}", [128, 2048], BF16) for i in range(2)]
    m4 = p.sbuf("m4", [128, 2048], BF16)
    yd4L = h4
    hstate = p.sbuf("hstate", [128, 256], F32)
    sp_t0 = conv_f
    sp_a = conv_g
    zout_t = p.sbuf("zout_t", [128, 1024], F32)
    scr_bf = p.sbuf("scr_bf", [128, 1024], BF16)
    zn_t = scr_bf
    colA = p.sbuf("colA", [128, 8], F32)
    rstd = p.sbuf("rstd", [128, 1], F32)
    negm = p.sbuf("negm", [128, 1], F32)

    pX = p.psum("pX", [128, 512], F32)
    pD = p.psum("pD", [64, 512], F32)
    pBB = p.psum("pBB", [128, 512], F32)
    pT = p.psum("pT", [128, 512], BF16)
    pZ = p.psum("pZ", [128, 2048], F32)

    out_dmas = []
    import os as _os3
    dbg = {}
    if _os3.environ.get("K_DEBUG3"):
        for nm, shape, dt in [("d_u", (128, CT * 512), BF16),
                              ("d_dl", (128, CT * 512), BF16),
                              ("d_res", (128, CT * 512), BF16),
                              ("d_y", (128, CT * 512), BF16),
                              ("d_B", (16, 512), F32),
                              ("d_C", (16, 512), F32),
                              ("d_znfm", (128, 8 * 512), BF16),
                              ("d_yfin", (128, CT * 512), BF16)]:
            dbg[nm] = p.dram_out(nm, shape, dt)

    def tap3(nm, buf, cond):
        if dbg and cond and nm in dbg:
            od = p.dma(dbg[nm].ap(), buf[:], r=[buf], w=[dbg[nm]],
                       engine="POOL", tag="tap" + nm)
            out_dmas.append(od)

    def ln_stats(src_buf, src_ap, eps, tag):
        p.op("DVE", (lambda src_ap: lambda eng: eng.tensor_reduce(
            out=colA[:, 0:1], in_=src_ap, axis=AX.X, op=ALU.add,
            negate=True))(src_ap), r=[src_buf], w=[colA], tag=tag + "_s")
        p.op("ACT", (lambda src_ap: lambda eng: eng.activation(
            out=scr_bf[:], in_=src_ap, func=AF.Square,
            accum_out=colA[:, 1:2]))(src_ap),
            r=[src_buf], w=[scr_bf, colA], tag=tag + "_q")
        p.op("DVE", lambda eng: ts_op(eng, negm[:], colA[:, 0:1], 1.0 / D, None,
                                      ALU.mult), r=[colA], w=[negm], tag=tag)
        def var1(eng):
            eng.tensor_scalar(out=colA[:, 2:3], in0=negm[:], scalar1=negm[:],
                              scalar2=None, op0=ALU.mult)
            return eng.tensor_scalar(out=colA[:, 3:4], in0=colA[:, 1:2],
                                     scalar1=1.0 / D, scalar2=eps,
                                     op0=ALU.mult, op1=ALU.add)
        p.op("DVE", var1, r=[colA, negm], w=[colA], tag=tag + "_v1")
        p.op("DVE", lambda eng: eng.tensor_tensor(
            out=colA[:, 4:5], in0=colA[:, 3:4], in1=colA[:, 2:3],
            op=ALU.subtract), r=[colA], w=[colA], tag=tag + "_v2")
        p.op("ACT", lambda eng: eng.activation(out=colA[:, 5:6],
                                               in_=colA[:, 4:5], func=AF.Sqrt),
             r=[colA], w=[colA], tag=tag + "_sd")
        p.op("DVE", lambda eng: eng.reciprocal(out=rstd[:], in_=colA[:, 5:6]),
             r=[colA], w=[rstd], tag=tag + "_r")
        p.op("DVE", lambda eng: eng.tensor_tensor(out=negm[:], in0=negm[:],
                                                  in1=rstd[:], op=ALU.mult),
             r=[rstd, negm, colA], w=[negm], tag=tag + "_n")

    wiv = win_h.ap().rearrange("(n p) c -> n p c", p=128)
    for ci in range(8):
        st = stg[0]
        p.dma(st[:], wiv[ci][:, 0:2048], r=[win_h], w=[st], tag=f"wil{ci}")
        p.op("DVE", (lambda st, ci: lambda eng: eng.tensor_copy(
            w_in[:, ci * 2048:(ci + 1) * 2048], st[:]))(st, ci),
            r=[st], w=[w_in], tag=f"wic{ci}")
        p.dma(st[:], wiv[ci][:, 2048:4096], r=[win_h], w=[st], tag=f"wrl{ci}")
        p.op("DVE", (lambda st: lambda eng: eng.tensor_copy(
            u_bf[:, 0:2048], st[:]))(st), r=[st], w=[u_bf], tag=f"wrc{ci}")
        p.dma(wres_s.ap()[ci], u_bf[:, 0:2048], r=[u_bf], w=[wres_s],
              engine="POOL", tag=f"wrs{ci}")
    wov = wout_h.ap().rearrange("(n p) c -> n p c", p=128)
    for ci in range(16):
        st = stg[ci % 2]
        p.dma(st[:, 0:1024], wov[ci], r=[wout_h], w=[st], tag=f"wol{ci}")
        p.op("DVE", (lambda st: lambda eng: eng.tensor_copy(
            scr_bf[:, 0:1024], st[:, 0:1024]))(st), r=[st], w=[scr_bf],
            tag=f"woc{ci}")
        p.dma(wouts.ap()[ci], scr_bf[:, 0:1024], r=[scr_bf], w=[wouts],
              engine="POOL", tag=f"wos{ci}")
    for (hh, dst, cols) in ((wxd_h, wx_dt, 64), (wxb_h, wx_B, 16),
                            (wxc_h, wx_C, 16)):
        vv = hh.ap().rearrange("(n p) c -> n p c", p=128)
        for ci in range(16):
            st = stg[ci % 2]
            p.dma(st[:, 0:cols], vv[ci], r=[hh], w=[st], tag=f"wx{cols}_{ci}")
            p.op("DVE", (lambda st, ci, dst, cols: lambda eng: eng.tensor_copy(
                dst[:, ci * cols:(ci + 1) * cols], st[:, 0:cols]))(st, ci, dst, cols),
                r=[st], w=[dst], tag=f"wxc{cols}_{ci}")
    st = stg[0]
    p.dma(st[:, 0:2048].rearrange("p c -> p c")[0:64, :], wdt_h.ap(),
          r=[wdt_h], w=[st], tag="wdtl")
    p.op("DVE", (lambda st: lambda eng: eng.tensor_copy(
        w_dt[:], st[0:64, 0:2048]))(st), r=[st], w=[w_dt], tag="wdtc")
    p.dma(convw[:], convw_h.ap(), r=[convw_h], w=[convw], tag="convwl")
    p.dma(colsb[:], cols_h.ap(), r=[cols_h], w=[colsb], tag="colsl")
    p.dma(A_sb[:], A_h.ap(), r=[A_h], w=[A_sb], tag="al")

    zv0 = zm[0].ap().rearrange("(n p) d -> n p d", p=128)
    for s in range(NSUB):
        x_t = zc
        p.dma(zc[:, 0:1024], xv[s], r=[x_in], w=[zc], tag=f"p0ld{s}")
        ln_stats(zc, zc[:, 0:1024], 1e-5, f"p0ln{s}")
        p.op("ACT", lambda eng: eng.activation(
            out=zout_t[:], in_=zc[:, 0:1024], func=AF.Identity, scale=rstd[:],
            bias=negm[:]), r=[zc, rstd, negm], w=[zout_t], tag=f"p0a{s}")
        p.dma(zv0[s], zout_t[:], r=[zout_t], w=[zm[0]], engine="POOL",
              tag=f"p0st{s}")

    SQD = float(np.sqrt(D))
    zin, zout = zm[0], zm[1]
    for ps in range(4):
        ziv = zin.ap().rearrange("(n p) d -> n p d", p=128)
        zov = zout.ap().rearrange("(n p) d -> n p d", p=128)
        for seq in range(BPC):
            p.op("DVE", lambda eng: eng.memset(tails[:], 0.0), w=[tails],
                 tag=f"t0_{ps}_{seq}")
            p.op("DVE", lambda eng: eng.memset(hstate[:], 0.0), w=[hstate],
                 tag=f"h0_{ps}_{seq}")
            for ch in range(4):
                gch = seq * 4 + ch
                for su in range(4):
                    s = gch * 4 + su
                    sl = (su % 2) * 1024
                    p.dma(zc[:, sl:sl + 1024], ziv[s], r=[zin],
                          w=[zc], tag=f"zl{ps}_{gch}_{su}")
                    ln_src = zc[:, sl:sl + 1024]
                    p.op("ACT", (lambda ln_src: lambda eng: eng.activation(
                        out=scr_bf[:], in_=ln_src, func=AF.Square,
                        accum_out=colA[:, 1:2]))(ln_src),
                        r=[zc], w=[scr_bf, colA], tag=f"rq{ps}_{gch}_{su}")
                    p.op("ACT", lambda eng: eng.activation(
                        out=colA[:, 2:3], in_=colA[:, 1:2], func=AF.Sqrt),
                        r=[colA], w=[colA], tag=f"rsd{ps}_{gch}_{su}")
                    p.op("DVE", lambda eng: ts_op(
                        eng, colA[:, 3:4], colA[:, 2:3], 1e-12, None, ALU.max),
                        r=[colA], w=[colA], tag=f"rmx{ps}_{gch}_{su}")
                    p.op("DVE", lambda eng: eng.reciprocal(
                        out=colA[:, 4:5], in_=colA[:, 3:4]),
                        r=[colA], w=[colA], tag=f"rrc{ps}_{gch}_{su}")
                    p.op("DVE", lambda eng: ts_op(
                        eng, rstd[:], colA[:, 4:5], SQD, None, ALU.mult),
                        r=[colA], w=[rstd], tag=f"rsc{ps}_{gch}_{su}")
                    p.op("ACT", (lambda ln_src: lambda eng: eng.activation(
                        out=zn_t[:], in_=ln_src, func=AF.Identity,
                        scale=rstd[:]))(ln_src),
                        r=[zc, rstd], w=[zn_t], tag=f"ra{ps}_{gch}_{su}")
                    for g in range(2):
                        def trf(eng, g=g):
                            instr = None
                            for i in range(4):
                                ci = g * 4 + i
                                instr = eng.transpose(
                                    pT[:, i * 128:(i + 1) * 128],
                                    zn_t[:, ci * 128:(ci + 1) * 128], c_idbf[:])
                            return instr
                        p.op("TE", trf, r=[zn_t, c_idbf], w=[pT],
                             tag=f"zt{ps}_{gch}_{su}{g}")
                        def ev(eng, g=g, su=su):
                            instr = None
                            for i in range(4):
                                ci = g * 4 + i
                                instr = eng.activation(
                                    out=zn_fm[:, ci * 512 + su * 128:
                                              ci * 512 + su * 128 + 128],
                                    in_=pT[:, i * 128:(i + 1) * 128],
                                    func=AF.Copy)
                            return instr
                        p.op("ACT", ev, r=[pT], w=[zn_fm],
                             tag=f"ze{ps}_{gch}_{su}{g}")

                for ct in range(CT):
                    wrb = wres_st[ct % 2]
                    p.dma(wrb.ap().rearrange("p (c x) -> p c x", c=8),
                          wres_s.ap()[:, :, ct * 128:(ct + 1) * 128].rearrange(
                              "c p x -> p c x"),
                          r=[wres_s], w=[wrb], tag=f"wrld{ps}_{gch}_{ct}")
                    pairs = [(w_in[:, ci * 2048 + ct * 128: ci * 2048 + (ct + 1) * 128],
                              zn_fm[:, ci * 512:(ci + 1) * 512])
                             for ci in range(8)]
                    p.mm(pX[:], pairs, r=[w_in, zn_fm], w=[pX],
                         tag=f"xin{ps}_{gch}_{ct}")
                    p.op("DVE", (lambda ct: lambda eng: eng.tensor_copy(
                        x_halo[:, 0:3], tails[:, ct * 3:(ct + 1) * 3]))(ct),
                        r=[tails], w=[x_halo], tag=f"halo{ps}_{gch}_{ct}")
                    p.op("ACT", lambda eng: eng.activation(
                        out=x_halo[:, 3:515], in_=pX[:], func=AF.Copy),
                        r=[pX], w=[x_halo], tag=f"xe{ps}_{gch}_{ct}")
                    p.op("DVE", (lambda ct: lambda eng: eng.tensor_copy(
                        tails[:, ct * 3:(ct + 1) * 3], x_halo[:, 512:515]))(ct),
                        r=[x_halo], w=[tails], tag=f"tsv{ps}_{gch}_{ct}")
                    p.op("DVE", (lambda ct: lambda eng: eng.tensor_scalar(
                        out=conv_f[:], in0=x_halo[:, 0:512],
                        scalar1=convw[:, ct * 4 + 0: ct * 4 + 1], scalar2=None,
                        op0=ALU.mult))(ct),
                        r=[x_halo, convw], w=[conv_f], tag=f"cv0{ps}_{gch}_{ct}")
                    for k in range(1, 4):
                        src_b, dst_b = (conv_f, conv_g) if k % 2 == 1 else (conv_g, conv_f)
                        p.op("DVE", (lambda ct, k, src_b, dst_b: lambda eng:
                             eng.scalar_tensor_tensor(
                                 out=dst_b[:], in0=x_halo[:, k:k + 512],
                                 scalar=convw[:, ct * 4 + k: ct * 4 + k + 1],
                                 in1=src_b[:], op0=ALU.mult, op1=ALU.add))(
                                     ct, k, src_b, dst_b),
                            r=[x_halo, convw, src_b], w=[dst_b],
                            tag=f"cv{k}{ps}_{gch}_{ct}")
                    p.op("ACT", (lambda ct: lambda eng: eng.activation(
                        out=u_bf[:, ct * 512:(ct + 1) * 512], in_=conv_g[:],
                        func=AF.Silu, bias=colsb[:, ct * 4: ct * 4 + 1]))(ct),
                        r=[conv_g, colsb], w=[u_bf], tag=f"silu{ps}_{gch}_{ct}")
                    pairs = [(wrb[:, ci * 128:(ci + 1) * 128],
                              zn_fm[:, ci * 512:(ci + 1) * 512])
                             for ci in range(8)]
                    p.mm(pX[:], pairs, r=[wrb, zn_fm], w=[pX],
                         tag=f"res{ps}_{gch}_{ct}")
                    p.op("ACT", (lambda ct: lambda eng: eng.activation(
                        out=res_bf[:, ct * 512:(ct + 1) * 512], in_=pX[:],
                        func=AF.Silu))(ct),
                        r=[pX], w=[res_bf], tag=f"rese{ps}_{gch}_{ct}")

                tap3("d_znfm", zn_fm, ps == 0 and gch == 0)
                tap3("d_u", u_bf, ps == 0 and gch == 0)
                tap3("d_res", res_bf, ps == 0 and gch == 0)
                p.mm(pD[:], [(wx_dt[:, ct2 * 64:(ct2 + 1) * 64],
                              u_bf[:, ct2 * 512:(ct2 + 1) * 512])
                             for ct2 in range(CT)],
                     r=[wx_dt, u_bf], w=[pD], tag=f"dt{ps}_{gch}")
                p.op("ACT", lambda eng: eng.activation(
                    out=dt_bf[:], in_=pD[:], func=AF.Copy),
                    r=[pD], w=[dt_bf], tag=f"dte{ps}_{gch}")
                p.mm(pBB[0:16, :], [(wx_B[:, ct2 * 16:(ct2 + 1) * 16],
                                     u_bf[:, ct2 * 512:(ct2 + 1) * 512])
                                    for ct2 in range(CT)],
                     r=[wx_B, u_bf], w=[pBB], tag=f"bm{ps}_{gch}")
                p.op("DVE", lambda eng: eng.tensor_copy(B_sb[:], pBB[0:16, :]),
                     r=[pBB], w=[B_sb], tag=f"be{ps}_{gch}")
                p.mm(pBB[0:16, :], [(wx_C[:, ct2 * 16:(ct2 + 1) * 16],
                                     u_bf[:, ct2 * 512:(ct2 + 1) * 512])
                                    for ct2 in range(CT)],
                     r=[wx_C, u_bf], w=[pBB], tag=f"cm{ps}_{gch}")
                p.op("DVE", lambda eng: eng.tensor_copy(C_sb[:], pBB[0:16, :]),
                     r=[pBB], w=[C_sb], tag=f"ce{ps}_{gch}")

                for ct in range(CT):
                    p.mm(pX[:], [(w_dt[:, ct * 128:(ct + 1) * 128], dt_bf[:])],
                         r=[w_dt, dt_bf], w=[pX], tag=f"dl{ps}_{gch}_{ct}")
                    p.op("DVE", (lambda ct: lambda eng: eng.tensor_scalar(
                        out=sp_t0[:], in0=pX[:],
                        scalar1=colsb[:, ct * 4 + 1: ct * 4 + 2], scalar2=None,
                        op0=ALU.add))(ct),
                        r=[pX, colsb], w=[sp_t0], tag=f"sp0{ps}_{gch}_{ct}")
                    p.op("ACT", lambda eng: eng.activation(
                        out=sp_a[:], in_=sp_t0[:], func=AF.Abs),
                        r=[sp_t0], w=[sp_a], tag=f"sp1{ps}_{gch}_{ct}")
                    p.op("ACT", lambda eng: eng.activation(
                        out=sp_a[:], in_=sp_a[:], func=AF.Exp, scale=-1.0),
                        r=[sp_a], w=[sp_a], tag=f"sp2{ps}_{gch}_{ct}")
                    p.op("ACT", lambda eng: eng.activation(
                        out=sp_a[:], in_=sp_a[:], func=AF.Ln, bias=1.0),
                        r=[sp_a], w=[sp_a], tag=f"sp3{ps}_{gch}_{ct}")
                    p.op("DVE", lambda eng: eng.tensor_scalar(
                        out=sp_t0[:], in0=sp_t0[:], scalar1=0.0, scalar2=None,
                        op0=ALU.max), r=[sp_t0], w=[sp_t0],
                        tag=f"sp4{ps}_{gch}_{ct}")
                    p.op("DVE", (lambda ct: lambda eng: eng.tensor_tensor(
                        out=d_bf[:, ct * 512:(ct + 1) * 512], in0=sp_t0[:],
                        in1=sp_a[:], op=ALU.add))(ct),
                        r=[sp_t0, sp_a], w=[d_bf], tag=f"sp5{ps}_{gch}_{ct}")
                    p.op("DVE", (lambda ct: lambda eng: eng.tensor_tensor(
                        out=du_bf[:, ct * 512:(ct + 1) * 512],
                        in0=d_bf[:, ct * 512:(ct + 1) * 512],
                        in1=u_bf[:, ct * 512:(ct + 1) * 512], op=ALU.mult))(ct),
                        r=[d_bf, u_bf], w=[du_bf], tag=f"du{ps}_{gch}_{ct}")

                tap3("d_dl", d_bf, ps == 0 and gch == 0)
                tap3("d_B", B_sb, ps == 0 and gch == 0)
                tap3("d_C", C_sb, ps == 0 and gch == 0)
                hs_v = hstate.ap().rearrange("p (ct s) -> p ct s", s=16)
                du_v = du_bf.ap().rearrange("p (c t) -> p c t", t=512)
                y_v = y_bf.ap().rearrange("p (c t) -> p c t", t=512)
                for j in range(S):
                    p.mm(pBB[:], [(c_esel[:, j * 128:(j + 1) * 128], B_sb[:])],
                         r=[c_esel, B_sb], w=[pBB], tag=f"bb{ps}_{gch}_{j}")
                    p.op("ACT", lambda eng: eng.activation(
                        out=B_bc[:], in_=pBB[:], func=AF.Copy),
                        r=[pBB], w=[B_bc], tag=f"bbe{ps}_{gch}_{j}")
                    p.mm(pBB[:], [(c_esel[:, j * 128:(j + 1) * 128], C_sb[:])],
                         r=[c_esel, C_sb], w=[pBB], tag=f"cb{ps}_{gch}_{j}")
                    p.op("ACT", lambda eng: eng.activation(
                        out=C_bc[:], in_=pBB[:], func=AF.Copy),
                        r=[pBB], w=[C_bc], tag=f"cbe{ps}_{gch}_{j}")
                    Cbc = C_bc.ap().unsqueeze(1).to_broadcast([128, 4, 512])
                    for g in range(4):
                        par = (j * 4 + g) % 2
                        dT = d_t[par]
                        hh4 = h4[par]
                        def decf(eng, g=g, j=j, dT=dT):
                            instr = None
                            for k in range(4):
                                ct = 4 * g + k
                                instr = eng.activation(
                                    out=dT[k][:],
                                    in_=d_bf[:, ct * 512:(ct + 1) * 512],
                                    func=AF.Exp,
                                    scale=A_sb[:, ct * 16 + j: ct * 16 + j + 1])
                            return instr
                        p.op("ACT", decf, r=[d_bf, A_sb],
                             w=[dT[0], dT[1], dT[2], dT[3]],
                             tag=f"dc{ps}_{gch}_{j}_{g}")
                        def btf(eng, g=g):
                            instr = None
                            for k in range(4):
                                ct = 4 * g + k
                                instr = eng.tensor_tensor(
                                    out=b_t[k][:],
                                    in0=du_bf[:, ct * 512:(ct + 1) * 512],
                                    in1=B_bc[:], op=ALU.mult)
                            return instr
                        p.op("DVE", btf, r=[du_bf, B_bc],
                             w=[b_t[0], b_t[1], b_t[2], b_t[3]],
                             tag=f"bt{ps}_{gch}_{j}_{g}")
                        def scf(eng, g=g, j=j, dT=dT, hh4=hh4):
                            instr = None
                            for k in range(4):
                                ct = 4 * g + k
                                instr = eng.tensor_tensor_scan(
                                    out=hh4[:, k * 512:(k + 1) * 512],
                                    data0=dT[k][:],
                                    data1=b_t[k][:],
                                    initial=hstate[:, ct * 16 + j: ct * 16 + j + 1],
                                    op0=ALU.mult, op1=ALU.add)
                            return instr
                        p.op("DVE", scf,
                             r=[dT[0], dT[1], dT[2], dT[3],
                                b_t[0], b_t[1], b_t[2], b_t[3], hstate],
                             w=[hh4], tag=f"sc{ps}_{gch}_{j}_{g}")
                        p.op("DVE", (lambda g, j, hh4: lambda eng: eng.tensor_copy(
                            hs_v[:, 4 * g:4 * g + 4, j:j + 1],
                            hh4.ap().rearrange("p (k t) -> p k t", t=512)
                            [:, :, 511:512]))(g, j, hh4),
                            r=[hh4], w=[hstate], tag=f"hs{ps}_{gch}_{j}_{g}")
                        if j == 0:
                            p.op("DVE", (lambda g, hh4, Cbc: lambda eng:
                                 eng.tensor_tensor(
                                     out=y_v[:, 4 * g:4 * g + 4, :],
                                     in0=hh4.ap().rearrange(
                                         "p (k t) -> p k t", k=4),
                                     in1=Cbc, op=ALU.mult))(g, hh4, Cbc),
                                r=[hh4, C_bc], w=[y_bf],
                                tag=f"y0{ps}_{gch}_{j}_{g}")
                        else:
                            p.op("DVE", (lambda g, hh4, Cbc: lambda eng:
                                 eng.tensor_tensor(
                                     out=m4.ap().rearrange(
                                         "p (k t) -> p k t", k=4),
                                     in0=hh4.ap().rearrange(
                                         "p (k t) -> p k t", k=4),
                                     in1=Cbc, op=ALU.mult))(g, hh4, Cbc),
                                r=[hh4, C_bc], w=[m4],
                                tag=f"mt{ps}_{gch}_{j}_{g}")
                            p.op("DVE", (lambda g: lambda eng: eng.tensor_tensor(
                                out=y_bf[:, g * 2048:(g + 1) * 2048],
                                in0=y_bf[:, g * 2048:(g + 1) * 2048],
                                in1=m4[:], op=ALU.add))(g),
                                r=[y_bf, m4], w=[y_bf],
                                tag=f"ya{ps}_{gch}_{j}_{g}")

                tap3("d_y", y_bf, ps == 0 and gch == 0)
                for g in range(4):
                    yd4 = yd4L[g % 2]
                    def ydf(eng, g=g, yd4=yd4):
                        instr = None
                        for k in range(4):
                            ct = 4 * g + k
                            instr = eng.tensor_scalar(
                                out=yd4[:, k * 512:(k + 1) * 512],
                                in0=u_bf[:, ct * 512:(ct + 1) * 512],
                                scalar1=colsb[:, ct * 4 + 2: ct * 4 + 3],
                                scalar2=None, op0=ALU.mult)
                        return instr
                    p.op("DVE", ydf, r=[u_bf, colsb], w=[yd4],
                         tag=f"yd{ps}_{gch}_{g}")
                    p.op("DVE", (lambda g, yd4: lambda eng: eng.tensor_tensor(
                        out=yd4[:], in0=y_bf[:, g * 2048:(g + 1) * 2048],
                        in1=yd4[:], op=ALU.add))(g, yd4),
                        r=[y_bf, yd4], w=[yd4], tag=f"yu{ps}_{gch}_{g}")
                    p.op("DVE", (lambda g, yd4: lambda eng: eng.tensor_tensor(
                        out=du_bf[:, g * 2048:(g + 1) * 2048], in0=yd4[:],
                        in1=res_bf[:, g * 2048:(g + 1) * 2048],
                        op=ALU.mult))(g, yd4),
                        r=[yd4, res_bf], w=[du_bf], tag=f"yf{ps}_{gch}_{g}")

                tap3("d_yfin", du_bf, ps == 0 and gch == 0)
                for half in range(2):
                    for ct in range(CT):
                        wst = wo_st[ct % 2]
                        p.dma(wst[:], wouts.ap()[ct, :, half * 512:(half + 1) * 512],
                              r=[wouts], w=[wst], tag=f"wosl{ps}_{gch}_{half}_{ct}")
                        def wmf(eng, ct=ct, wst=wst):
                            instr = None
                            for su in range(4):
                                instr = eng.matmul(
                                    pZ[:, su * 512:(su + 1) * 512],
                                    lhsT=du_bf[:, ct * 512 + su * 128:
                                               ct * 512 + su * 128 + 128],
                                    rhs=wst[:], start=(ct == 0), stop=(ct == CT - 1))
                            return instr
                        p.op("TE", wmf, r=[du_bf, wo_st[ct % 2]], w=[pZ],
                             tag=f"wom{ps}_{gch}_{half}_{ct}")
                    for su in range(4):
                        s = gch * 4 + su
                        p.op("DVE", (lambda su, half: lambda eng: eng.tensor_copy(
                            zout_t[:, half * 512:(half + 1) * 512],
                            pZ[:, su * 512:(su + 1) * 512]))(su, half),
                            r=[pZ], w=[zout_t], tag=f"zo{ps}_{gch}_{half}_{su}")
                        p.dma(zov[s][:, half * 512:(half + 1) * 512],
                              zout_t[:, half * 512:(half + 1) * 512],
                              r=[zout_t], w=[zout],
                              engine="POOL", tag=f"zst{ps}_{gch}_{half}_{su}")
        zin, zout = zout, zin

    zfv = zin.ap().rearrange("(n p) d -> n p d", p=128)
    for s in range(NSUB):
        p.dma(zc[:, 0:1024], zfv[s], r=[zin], w=[zc], tag=f"p9ld{s}")
        p.dma(zc[:, 1024:2048], xv[s], r=[x_in], w=[zc], tag=f"p9lx{s}")
        ln_stats(zc, zc[:, 0:1024], 1e-5, f"p9ln{s}")
        p.op("ACT", lambda eng: eng.activation(
            out=zout_t[:], in_=zc[:, 0:1024], func=AF.Identity, scale=rstd[:],
            bias=negm[:]), r=[zc, rstd, negm], w=[zout_t], tag=f"p9a{s}")
        p.op("DVE", lambda eng: eng.tensor_tensor(
            out=zout_t[:], in0=zout_t[:], in1=zc[:, 1024:2048], op=ALU.add),
            r=[zout_t, zc], w=[zout_t], tag=f"p9r{s}")
        od = p.dma(ov[s], zout_t[:], r=[zout_t], w=[x_out], engine="POOL",
                   tag=f"p9st{s}")
        out_dmas.append(od)

    return out_dmas


def _prep_inputs_l3(inputs, core):
    c = {}
    c["c_ident_f32"] = np.eye(128, dtype=np.float32)
    c["c_ones_row"] = np.ones((1, 128), dtype=np.float32)
    esel = np.zeros((16, 2048), np.float32)
    for j in range(16):
        esel[j, j * 128:(j + 1) * 128] = 1.0
    c["c_esel"] = esel
    W_x = inputs["L1_attn_W_x"].astype(np.float32)
    conv_w = inputs["L1_attn_conv_w"].astype(np.float32)[:, 0, :]
    def colmat(v, k):
        return np.ascontiguousarray(
            v.reshape(16, 128, k).transpose(1, 0, 2).reshape(128, 16 * k))
    cols = np.zeros((2048, 4), np.float32)
    cols[:, 0] = inputs["L1_attn_conv_b"]
    cols[:, 1] = inputs["L1_attn_b_dt"]
    cols[:, 2] = inputs["L1_attn_Dp"]
    A = -np.exp(inputs["L1_attn_A_log"].astype(np.float32))
    m = {
        "x_in": np.ascontiguousarray(inputs["x_prev"][2 * core:2 * core + 2]),
        "mb_W_in": inputs["L1_attn_W_in"].astype(np.float32),
        "mb_W_out": inputs["L1_attn_W_out"].astype(np.float32),
        "mb_Wx_dt": np.ascontiguousarray(W_x[:, 0:64]),
        "mb_Wx_B": np.ascontiguousarray(W_x[:, 64:80]),
        "mb_Wx_C": np.ascontiguousarray(W_x[:, 80:96]),
        "mb_W_dt": inputs["L1_attn_W_dt"].astype(np.float32),
        "mb_conv_w": colmat(conv_w, 4),
        "mb_cols": colmat(cols, 4),
        "mb_A": colmat(A, 16),
    }
    m.update(c)
    return m



def emit_launch4(p, x_in, x_out):
    NR = T // 128
    E, SL, HH = 16, 16, 256
    xv = x_in.ap().rearrange("b (n p) d -> (b n) p d", p=128)
    ov = x_out.ap().rearrange("b (n p) d -> (b n) p d", p=128)

    ident32_h = p.dram_in("c_ident_f32", (128, 128), F32)
    ones_row_h = p.dram_in("c_ones_row", (1, 128), F32)
    ones_col_h = p.dram_in("c_ones_col", (128, 1), F32)
    c_id32 = p.sbuf("id32", [128, 128], F32)
    c_idbf = p.sbuf("idbf", [128, 128], BF16)
    c_ones_row = p.sbuf("ones_row", [1, 128], F32)
    c_ones_col = p.sbuf("ones_col", [128, 1], F32)
    p.dma(c_id32[:], ident32_h.ap(), r=[ident32_h], w=[c_id32])
    p.op("DVE", lambda eng: eng.tensor_copy(c_idbf[:], c_id32[:]),
         r=[c_id32], w=[c_idbf], tag="idbf")
    p.dma(c_ones_row[:], ones_row_h.ap(), r=[ones_row_h], w=[c_ones_row])
    p.dma(c_ones_col[:], ones_col_h.ap(), r=[ones_col_h], w=[c_ones_col])

    mu_h = p.dram_in("moe_mu", (1024, 256), F32)
    gb_h = p.dram_in("moe_gb", (2, 1024), F32)
    sc_h = p.dram_in("moe_scale", (1, 1), F32)
    w1_h = p.dram_in("moe_w1", (E, 1024, 256), F32)
    w2_h = p.dram_in("moe_w2", (E, 256, 1024), F32)
    b1_h = p.dram_in("moe_b1cols", (128, 32), F32)
    b2_h = p.dram_in("moe_b2", (16, 1024), F32)

    mu_bf = p.sbuf("mu_bf", [128, 8 * 256], BF16)
    mu_f = p.sbuf("mu_f", [128, 8 * 256], F32)
    g_b = p.sbuf("g_b", [128, 1024], F32)
    b_b = p.sbuf("b_b", [128, 1024], F32)
    sc_col = p.sbuf("sc_col", [128, 1], F32)
    b1c = p.sbuf("b1c", [128, 32], F32)
    b2_bf = p.sbuf("b2_bf", [16, 1024], BF16)
    w1s = [p.sbuf(f"w1s{i}", [128, 8 * 256], BF16) for i in range(2)]
    w2s = [p.sbuf(f"w2s{i}", [128, 2 * 1024], BF16) for i in range(2)]
    stg = p.sbuf("stg", [128, 2304], F32)

    xn_seq = p.sbuf("xn_seq", [128, NR * 1024], BF16)
    ltm = p.sbuf("ltm", [128, NR * 256], F32)
    lesm = p.sbuf("lesm", [128, 2 * T], F32)
    disp = p.sbuf("disp", [128, 2 * T], BF16)
    disp_tm = p.sbuf("disp_tm", [128, NR * 256], BF16)
    comb_esm = p.sbuf("comb_esm", [128, 2 * T], BF16)
    comb_tm = p.sbuf("comb_tm", [128, NR * 256], BF16)
    si_dm = p.sbuf("si_dm", [128, 8 * 256], BF16)
    so_esm = p.sbuf("so_esm", [128, 2 * 1024], BF16)
    so_e = p.sbuf("so_e", [16, 1024], BF16)
    h_e = p.sbuf("h_e", [128, 2 * 16], BF16)
    cb_fm = p.sbuf("cb_fm", [16, 128], BF16)
    cbt = p.sbuf("cbt", [128, 16], F32)
    xt = p.sbuf("xt", [128, 1024], F32)
    xn_t = p.sbuf("xn_t", [128, 1024], F32)
    xq_bf = p.sbuf("xq_bf", [128, 1024], BF16)
    ffo = p.sbuf("ffo", [128, 1024], F32)
    zo = p.sbuf("zo", [128, 1024], F32)
    scr_bf = p.sbuf("scr_bf", [128, 1024], BF16)
    colA = p.sbuf("colA", [128, 8], F32)
    rstd = p.sbuf("rstd", [128, 1], F32)
    negm = p.sbuf("negm", [128, 1], F32)
    nmx = p.sbuf("nmx", [128, 1], F32)
    den = p.sbuf("den", [128, 1], F32)
    rden = p.sbuf("rden", [128, 1], F32)
    comb_t = p.sbuf("comb_t", [128, 256], BF16)

    pL = p.psum("pL", [128, 512], F32)
    pT = p.psum("pT", [128, 512], BF16)
    pF = p.psum("pF", [128, 512], F32)
    pH = p.psum("pH", [128, 2 * 16], F32)
    pSO = p.psum("pSO", [16, 1024], F32)
    pZ4 = p.psum("pZ4", [128, 1024], F32)

    out_dmas = []

    def ln_stats(src_buf, src_ap, eps, tag):
        p.op("DVE", (lambda src_ap: lambda eng: eng.tensor_reduce(
            out=colA[:, 0:1], in_=src_ap, axis=AX.X, op=ALU.add,
            negate=True))(src_ap), r=[src_buf], w=[colA], tag=tag + "_s")
        p.op("ACT", (lambda src_ap: lambda eng: eng.activation(
            out=scr_bf[:], in_=src_ap, func=AF.Square,
            accum_out=colA[:, 1:2]))(src_ap),
            r=[src_buf], w=[scr_bf, colA], tag=tag + "_q")
        p.op("DVE", lambda eng: ts_op(eng, negm[:], colA[:, 0:1], 1.0 / D, None,
                                      ALU.mult), r=[colA], w=[negm], tag=tag)
        def var1(eng):
            eng.tensor_scalar(out=colA[:, 2:3], in0=negm[:], scalar1=negm[:],
                              scalar2=None, op0=ALU.mult)
            return eng.tensor_scalar(out=colA[:, 3:4], in0=colA[:, 1:2],
                                     scalar1=1.0 / D, scalar2=eps,
                                     op0=ALU.mult, op1=ALU.add)
        p.op("DVE", var1, r=[colA, negm], w=[colA], tag=tag + "_v1")
        p.op("DVE", lambda eng: eng.tensor_tensor(
            out=colA[:, 4:5], in0=colA[:, 3:4], in1=colA[:, 2:3],
            op=ALU.subtract), r=[colA], w=[colA], tag=tag + "_v2")
        p.op("ACT", lambda eng: eng.activation(out=colA[:, 5:6],
                                               in_=colA[:, 4:5], func=AF.Sqrt),
             r=[colA], w=[colA], tag=tag + "_sd")
        p.op("DVE", lambda eng: eng.reciprocal(out=rstd[:], in_=colA[:, 5:6]),
             r=[colA], w=[rstd], tag=tag + "_r")
        p.op("DVE", lambda eng: eng.tensor_tensor(out=negm[:], in0=negm[:],
                                                  in1=rstd[:], op=ALU.mult),
             r=[rstd, negm, colA], w=[negm], tag=tag + "_n")

    muv = mu_h.ap().rearrange("(n p) c -> n p c", p=128)
    for ci in range(8):
        p.dma(stg[:, 0:256], muv[ci], r=[mu_h], w=[stg], tag=f"mul{ci}")
        p.op("DVE", (lambda ci: lambda eng: eng.tensor_copy(
            mu_f[:, ci * 256:(ci + 1) * 256], stg[:, 0:256]))(ci),
            r=[stg], w=[mu_f], tag=f"muc{ci}")
    for ci in range(8):
        p.op("ACT", (lambda ci: lambda eng: eng.activation(
            out=scr_bf[:, 0:256], in_=mu_f[:, ci * 256:(ci + 1) * 256],
            func=AF.Square))(ci), r=[mu_f], w=[scr_bf], tag=f"musq{ci}")
        p.op("DVE", (lambda ci: lambda eng: eng.tensor_copy(
            stg[:, 256 + ci * 256: 256 + (ci + 1) * 256],
            scr_bf[:, 0:256]))(ci), r=[scr_bf], w=[stg], tag=f"musqc{ci}")
    p.mm(pL[0:1, 0:256], [(c_ones_col[:], stg[:, 256 + ci * 256:
                                              256 + (ci + 1) * 256])
                          for ci in range(8)],
         r=[c_ones_col, stg], w=[pL], tag="muss")
    p.op("ACT", lambda eng: eng.activation(out=stg[0:1, 0:256],
                                           in_=pL[0:1, 0:256], func=AF.Sqrt),
         r=[pL], w=[stg], tag="musd")
    p.op("DVE", lambda eng: ts_op(eng, stg[0:1, 512:768], stg[0:1, 0:256],
                                  1e-12, None, ALU.max),
         r=[stg], w=[stg], tag="mumx")
    p.op("DVE", lambda eng: eng.reciprocal(out=stg[0:1, 0:256],
                                           in_=stg[0:1, 512:768]),
         r=[stg], w=[stg], tag="murc")
    p.mm(pL[:, 0:256], [(c_ones_row[0:1, :], stg[0:1, 0:256])],
         r=[c_ones_row, stg], w=[pL], tag="murb")
    p.op("DVE", lambda eng: eng.tensor_copy(stg[:, 1024:1280], pL[:, 0:256]),
         r=[pL], w=[stg], tag="murbe")
    for ci in range(8):
        p.op("DVE", (lambda ci: lambda eng: eng.tensor_tensor(
            out=mu_bf[:, ci * 256:(ci + 1) * 256],
            in0=mu_f[:, ci * 256:(ci + 1) * 256],
            in1=stg[:, 1024:1280], op=ALU.mult))(ci),
            r=[mu_f, stg], w=[mu_bf], tag=f"muu{ci}")
    p.dma(stg[0:1, 0:2048], gb_h.ap().rearrange("a b -> (a b)").unsqueeze(0),
          r=[gb_h], w=[stg], tag="gbl")
    for (dst, off, tg) in ((g_b, 0, "gb"), (b_b, 1024, "bb")):
        for si in range(2):
            p.mm(pL[:], [(c_ones_row[0:1, :],
                          stg[0:1, off + si * 512: off + (si + 1) * 512])],
                 r=[c_ones_row, stg], w=[pL], tag=tg + f"m{si}")
            p.op("DVE", (lambda dst, si: lambda eng: eng.tensor_copy(
                dst[:, si * 512:(si + 1) * 512], pL[:]))(dst, si),
                r=[pL], w=[dst], tag=tg + f"e{si}")
    p.dma(stg[0:1, 0:1], sc_h.ap(), r=[sc_h], w=[stg], tag="scl")
    p.mm(pL[:, 0:1], [(c_ones_row[0:1, :], stg[0:1, 0:1])],
         r=[c_ones_row, stg], w=[pL], tag="scm")
    p.op("DVE", lambda eng: eng.tensor_copy(sc_col[:], pL[:, 0:1]),
         r=[pL], w=[sc_col], tag="sce")
    p.dma(b1c[:], b1_h.ap(), r=[b1_h], w=[b1c], tag="b1l")
    p.dma(stg[0:16, 0:1024], b2_h.ap(), r=[b2_h], w=[stg], tag="b2l")
    p.op("DVE", lambda eng: eng.tensor_copy(b2_bf[:], stg[0:16, 0:1024]),
         r=[stg], w=[b2_bf], tag="b2c")

    for seq in range(BPC):
        for su in range(NR):
            s = seq * NR + su
            p.dma(xt[:], xv[s], r=[x_in], w=[xt], tag=f"xl{seq}_{su}")
            ln_stats(xt, xt[:], 1e-5, f"ln{seq}_{su}")
            p.op("ACT", lambda eng: eng.activation(
                out=xn_t[:], in_=xt[:], func=AF.Identity, scale=rstd[:],
                bias=negm[:]), r=[xt, rstd, negm], w=[xn_t], tag=f"xa{seq}_{su}")
            p.op("DVE", lambda eng: eng.tensor_tensor(
                out=xn_t[:], in0=xn_t[:], in1=g_b[:], op=ALU.mult),
                r=[xn_t, g_b], w=[xn_t], tag=f"xg{seq}_{su}")
            p.op("DVE", lambda eng: eng.tensor_tensor(
                out=xn_t[:], in0=xn_t[:], in1=b_b[:], op=ALU.add),
                r=[xn_t, b_b], w=[xn_t], tag=f"xb{seq}_{su}")
            p.op("DVE", (lambda su: lambda eng: eng.tensor_copy(
                xn_seq[:, su * 1024:(su + 1) * 1024], xn_t[:]))(su),
                r=[xn_t], w=[xn_seq], tag=f"xc{seq}_{su}")
            p.op("ACT", lambda eng: eng.activation(
                out=scr_bf[:], in_=xn_t[:], func=AF.Square,
                accum_out=colA[:, 6:7]), r=[xn_t], w=[scr_bf, colA],
                tag=f"q2{seq}_{su}")
            p.op("ACT", lambda eng: eng.activation(
                out=colA[:, 7:8], in_=colA[:, 6:7], func=AF.Sqrt),
                r=[colA], w=[colA], tag=f"q2s{seq}_{su}")
            p.op("DVE", lambda eng: ts_op(eng, colA[:, 6:7], colA[:, 7:8],
                                          1e-12, None, ALU.max),
                 r=[colA], w=[colA], tag=f"q2m{seq}_{su}")
            p.op("DVE", lambda eng: eng.reciprocal(out=colA[:, 7:8],
                                                   in_=colA[:, 6:7]),
                 r=[colA], w=[colA], tag=f"q2r{seq}_{su}")
            p.op("DVE", lambda eng: eng.tensor_scalar(
                out=colA[:, 6:7], in0=colA[:, 7:8], scalar1=sc_col[:],
                scalar2=None, op0=ALU.mult), r=[colA, sc_col], w=[colA],
                tag=f"q2c{seq}_{su}")
            p.op("DVE", lambda eng: eng.tensor_scalar(
                out=xq_bf[:], in0=xn_t[:], scalar1=colA[:, 6:7], scalar2=None,
                op0=ALU.mult), r=[xn_t, colA], w=[xq_bf], tag=f"q2a{seq}_{su}")
            for g in range(2):
                def trf(eng, g=g):
                    instr = None
                    for i in range(4):
                        ci = g * 4 + i
                        instr = eng.transpose(
                            pT[:, i * 128:(i + 1) * 128],
                            xq_bf[:, ci * 128:(ci + 1) * 128], c_idbf[:])
                    return instr
                p.op("TE", trf, r=[xq_bf, c_idbf], w=[pT], tag=f"xtr{seq}_{su}{g}")
                p.op("ACT", (lambda g: lambda eng: eng.activation(
                    out=scr_bf[:, g * 512:(g + 1) * 512], in_=pT[:],
                    func=AF.Copy))(g), r=[pT], w=[scr_bf], tag=f"xte{seq}_{su}{g}")
            p.mm(pL[:, 0:256], [(scr_bf[:, ci * 128:(ci + 1) * 128],
                                 mu_bf[:, ci * 256:(ci + 1) * 256])
                                for ci in range(8)],
                 r=[scr_bf, mu_bf], w=[pL], tag=f"lg{seq}_{su}")
            p.op("DVE", (lambda su: lambda eng: eng.tensor_copy(
                ltm[:, su * 256:(su + 1) * 256], pL[:, 0:256]))(su),
                r=[pL], w=[ltm], tag=f"lge{seq}_{su}")
            p.op("DVE", lambda eng: eng.tensor_reduce(
                out=nmx[:], in_=pL[:, 0:256], axis=AX.X, op=ALU.max,
                negate=True), r=[pL], w=[nmx], tag=f"cmx{seq}_{su}")
            p.op("ACT", lambda eng: eng.activation(
                out=comb_t[:], in_=pL[:, 0:256], func=AF.Exp, bias=nmx[:],
                accum_out=den[:]), r=[pL, nmx], w=[comb_t, den],
                tag=f"cex{seq}_{su}")
            p.op("DVE", lambda eng: eng.reciprocal(out=rden[:], in_=den[:]),
                 r=[den], w=[rden], tag=f"crd{seq}_{su}")
            p.op("DVE", lambda eng: eng.tensor_scalar(
                out=comb_t[:], in0=comb_t[:], scalar1=rden[:], scalar2=None,
                op0=ALU.mult), r=[comb_t, rden], w=[comb_t], tag=f"cn{seq}_{su}")
            p.op("DVE", (lambda su: lambda eng: eng.tensor_copy(
                comb_tm[:, su * 256:(su + 1) * 256], comb_t[:]))(su),
                r=[comb_t], w=[comb_tm], tag=f"cc{seq}_{su}")
            def trc(eng):
                eng.transpose(pT[:, 0:128], comb_t[:, 0:128], c_idbf[:])
                return eng.transpose(pT[:, 128:256], comb_t[:, 128:256],
                                     c_idbf[:])
            p.op("TE", trc, r=[comb_t, c_idbf], w=[pT], tag=f"ctr{seq}_{su}")
            def cev(eng, su=su):
                eng.activation(out=comb_esm[:, su * 128: su * 128 + 128],
                               in_=pT[:, 0:128], func=AF.Copy)
                return eng.activation(
                    out=comb_esm[:, T + su * 128: T + su * 128 + 128],
                    in_=pT[:, 128:256], func=AF.Copy)
            p.op("ACT", cev, r=[pT], w=[comb_esm], tag=f"cte{seq}_{su}")
            def trl(eng, su=su):
                eng.transpose(pF[:, 0:128], ltm[:, su * 256: su * 256 + 128],
                              c_id32[:])
                return eng.transpose(pF[:, 128:256],
                                     ltm[:, su * 256 + 128: su * 256 + 256],
                                     c_id32[:])
            p.op("TE", trl, r=[ltm, c_id32], w=[pF], tag=f"ltr{seq}_{su}")
            def lev(eng, su=su):
                eng.activation(out=lesm[:, su * 128: su * 128 + 128],
                               in_=pF[:, 0:128], func=AF.Copy)
                return eng.activation(
                    out=lesm[:, T + su * 128: T + su * 128 + 128],
                    in_=pF[:, 128:256], func=AF.Copy)
            p.op("ACT", lev, r=[pF], w=[lesm], tag=f"lte{seq}_{su}")

        for ec in range(2):
            sl_ = lambda b: b[:, ec * T:(ec + 1) * T]
            p.op("DVE", (lambda ec: lambda eng: eng.tensor_reduce(
                out=nmx[:], in_=lesm[:, ec * T:(ec + 1) * T], axis=AX.X,
                op=ALU.max, negate=True))(ec),
                r=[lesm], w=[nmx], tag=f"dmx{seq}_{ec}")
            p.op("ACT", (lambda ec: lambda eng: eng.activation(
                out=disp[:, ec * T:(ec + 1) * T],
                in_=lesm[:, ec * T:(ec + 1) * T], func=AF.Exp, bias=nmx[:],
                accum_out=den[:]))(ec),
                r=[lesm, nmx], w=[disp, den], tag=f"dex{seq}_{ec}")
            p.op("DVE", lambda eng: eng.reciprocal(out=rden[:], in_=den[:]),
                 r=[den], w=[rden], tag=f"drd{seq}_{ec}")
            p.op("DVE", (lambda ec: lambda eng: eng.tensor_scalar(
                out=disp[:, ec * T:(ec + 1) * T],
                in0=disp[:, ec * T:(ec + 1) * T], scalar1=rden[:],
                scalar2=None, op0=ALU.mult))(ec),
                r=[disp, rden], w=[disp], tag=f"dn{seq}_{ec}")
            for su in range(NR):
                def trd(eng, ec=ec, su=su):
                    return eng.transpose(
                        pT[:, 0:128],
                        disp[:, ec * T + su * 128: ec * T + su * 128 + 128],
                        c_idbf[:])
                p.op("TE", trd, r=[disp, c_idbf], w=[pT], tag=f"dtr{seq}_{ec}_{su}")
                p.op("ACT", (lambda ec, su: lambda eng: eng.activation(
                    out=disp_tm[:, su * 256 + ec * 128:
                                su * 256 + ec * 128 + 128],
                    in_=pT[:, 0:128], func=AF.Copy))(ec, su),
                    r=[pT], w=[disp_tm], tag=f"dte{seq}_{ec}_{su}")

        for dc in range(8):
            p.mm(pL[:, 0:256], [(xn_seq[:, su2 * 1024 + dc * 128:
                                        su2 * 1024 + dc * 128 + 128],
                                 disp_tm[:, su2 * 256:(su2 + 1) * 256])
                                for su2 in range(NR)],
                 r=[xn_seq, disp_tm], w=[pL], tag=f"si{seq}_{dc}")
            p.op("DVE", (lambda dc: lambda eng: eng.tensor_copy(
                si_dm[:, dc * 256:(dc + 1) * 256], pL[:, 0:256]))(dc),
                r=[pL], w=[si_dm], tag=f"sie{seq}_{dc}")

        for e in range(E):
            w1b = w1s[e % 2]
            w2b = w2s[e % 2]
            w1v = w1_h.ap()[e].rearrange("(n p) c -> n p c", p=128)
            for ci in range(8):
                p.dma(stg[:, 0:256], w1v[ci], r=[w1_h], w=[stg],
                      tag=f"w1l{seq}_{e}_{ci}")
                p.op("DVE", (lambda ci, w1b: lambda eng: eng.tensor_copy(
                    w1b[:, ci * 256:(ci + 1) * 256], stg[:, 0:256]))(ci, w1b),
                    r=[stg], w=[w1b], tag=f"w1c{seq}_{e}_{ci}")
            w2v = w2_h.ap()[e].rearrange("(n p) c -> n p c", p=128)
            for ci in range(2):
                p.dma(stg[:, 0:1024], w2v[ci], r=[w2_h], w=[stg],
                      tag=f"w2l{seq}_{e}_{ci}")
                p.op("DVE", (lambda ci, w2b: lambda eng: eng.tensor_copy(
                    w2b[:, ci * 1024:(ci + 1) * 1024], stg[:, 0:1024]))(ci, w2b),
                    r=[stg], w=[w2b], tag=f"w2c{seq}_{e}_{ci}")
            for hc in range(2):
                p.mm(pH[:, hc * 16:(hc + 1) * 16],
                     [(w1b[:, ci * 256 + hc * 128: ci * 256 + hc * 128 + 128],
                       si_dm[:, ci * 256 + e * 16: ci * 256 + e * 16 + 16])
                      for ci in range(8)],
                     r=[w1b, si_dm], w=[pH], tag=f"he{seq}_{e}_{hc}")
                p.op("ACT", (lambda e, hc: lambda eng: eng.activation(
                    out=h_e[:, hc * 16:(hc + 1) * 16],
                    in_=pH[:, hc * 16:(hc + 1) * 16], func=AF.Gelu,
                    bias=b1c[:, e * 2 + hc: e * 2 + hc + 1]))(e, hc),
                    r=[pH, b1c], w=[h_e], tag=f"hg{seq}_{e}_{hc}")
            for nh in range(2):
                p.mm(pSO[:, nh * 512:(nh + 1) * 512],
                     [(h_e[:, hc * 16:(hc + 1) * 16],
                       w2b[:, hc * 1024 + nh * 512: hc * 1024 + nh * 512 + 512])
                      for hc in range(2)],
                     r=[h_e, w2b], w=[pSO], tag=f"so{seq}_{e}_{nh}")
            p.op("DVE", lambda eng: eng.tensor_copy(so_e[:], pSO[:]),
                 r=[pSO], w=[so_e], tag=f"soe{seq}_{e}")
            p.dma(so_esm[(e % 8) * 16:(e % 8) * 16 + 16,
                         (e // 8) * 1024:(e // 8) * 1024 + 1024],
                  so_e[:], r=[so_e], w=[so_esm], engine="POOL",
                  tag=f"sod{seq}_{e}")

        for su in range(NR):
            s = seq * NR + su
            p.op("DVE", (lambda su: lambda eng: eng.tensor_reduce(
                out=cbt[:], in_=comb_tm.ap().rearrange(
                    "p (n e s) -> p n e s", n=NR, e=E)[:, su], axis=AX.X,
                op=ALU.add))(su), r=[comb_tm], w=[cbt], tag=f"cb{seq}_{su}")
            def trb(eng):
                return eng.transpose(pF[0:16, 0:128], cbt[:], c_id32[:])
            p.op("TE", trb, r=[cbt, c_id32], w=[pF], tag=f"cbt{seq}_{su}")
            p.op("DVE", lambda eng: eng.tensor_copy(cb_fm[:], pF[0:16, 0:128]),
                 r=[pF], w=[cb_fm], tag=f"cbe{seq}_{su}")
            def fmm(eng, su=su):
                instr = None
                for nh in range(2):
                    for ec in range(2):
                        instr = eng.matmul(
                            pZ4[:, nh * 512:(nh + 1) * 512],
                            lhsT=comb_esm[:, ec * T + su * 128:
                                          ec * T + su * 128 + 128],
                            rhs=so_esm[:, ec * 1024 + nh * 512:
                                       ec * 1024 + nh * 512 + 512],
                            start=(ec == 0), stop=False)
                    instr = eng.matmul(
                        pZ4[:, nh * 512:(nh + 1) * 512],
                        lhsT=cb_fm[:], rhs=b2_bf[:, nh * 512:(nh + 1) * 512],
                        start=False, stop=True)
                return instr
            p.op("TE", fmm, r=[comb_esm, so_esm, cb_fm, b2_bf], w=[pZ4],
                 tag=f"fm{seq}_{su}")
            p.op("DVE", lambda eng: eng.tensor_copy(ffo[:], pZ4[:]),
                 r=[pZ4], w=[ffo], tag=f"fe{seq}_{su}")
            p.dma(xt[:], xv[s], r=[x_in], w=[xt], tag=f"x2l{seq}_{su}")
            ln_stats(ffo, ffo[:], 1e-5, f"fln{seq}_{su}")
            p.op("ACT", lambda eng: eng.activation(
                out=zo[:], in_=ffo[:], func=AF.Identity, scale=rstd[:],
                bias=negm[:]), r=[ffo, rstd, negm], w=[zo], tag=f"fa{seq}_{su}")
            p.op("DVE", lambda eng: eng.tensor_tensor(
                out=zo[:], in0=zo[:], in1=xt[:], op=ALU.add),
                r=[zo, xt], w=[zo], tag=f"fr{seq}_{su}")
            od = p.dma(ov[s], zo[:], r=[zo], w=[x_out], engine="POOL",
                       tag=f"fst{seq}_{su}")
            out_dmas.append(od)

    return out_dmas


def _prep_inputs_l4(inputs, core):
    c = {}
    c["c_ident_f32"] = np.eye(128, dtype=np.float32)
    c["c_ones_row"] = np.ones((1, 128), dtype=np.float32)
    c["c_ones_col"] = np.ones((128, 1), dtype=np.float32)
    b1 = inputs["L1_ffn_b1"].astype(np.float32)
    b1c = np.zeros((128, 32), np.float32)
    for e in range(16):
        for hc in range(2):
            b1c[:, e * 2 + hc] = b1[e, hc * 128:(hc + 1) * 128]
    m = {
        "x_in": np.ascontiguousarray(inputs["x_prev"][2 * core:2 * core + 2]),
        "moe_mu": np.ascontiguousarray(
            inputs["L1_ffn_mu"].reshape(1024, 256).astype(np.float32)),
        "moe_gb": np.stack([inputs["L1_ffn_ln_gamma"],
                            inputs["L1_ffn_ln_beta"]]).astype(np.float32),
        "moe_scale": inputs["L1_ffn_scale"].reshape(1, 1).astype(np.float32),
        "moe_w1": inputs["L1_ffn_w1"].astype(np.float32),
        "moe_w2": inputs["L1_ffn_w2"].astype(np.float32),
        "moe_b1cols": b1c,
        "moe_b2": inputs["L1_ffn_b2"].astype(np.float32),
    }
    m.update(c)
    return m
